# version 09
# speedup vs baseline: 2.6486x; 1.0481x over previous
import math
import os
from contextlib import ExitStack

import numpy as np
import ml_dtypes
import concourse.bass as bass
import concourse.mybir as mybir
from concourse.bass_utils import run_bass_kernel_spmd

F32 = mybir.dt.float32
BF16 = mybir.dt.bfloat16
AF = mybir.ActivationFunctionType
ALU = mybir.AluOpType
AXX = mybir.AxisListType.X

D = 1024
S = 2048
NB_CORES = 8
SEQS_PER_CORE = 4
H = 16
HD = 64
NCH = D // 128
NTC = S // 128
TW = 1280
SHW = 1152
BLK = 256
NBLK = S // BLK
FB = NCH * BLK

QK_SCALE = 1.0 / math.sqrt(HD * 3.0)
PK_SCALE = math.sqrt(1.5)
PQ_SCALE = 1.0 / math.sqrt(HD * 2.0)
K2_SCALE = 0.02 / math.sqrt(D)
Q2_SCALE = 0.02 / math.sqrt(D)
LN_EPS = 1e-5


def ap(t, off, dims):
    return bass.AP(t, off, dims)



def _bucket_f():
    mid, max_pos = 128, 512
    d = np.arange(-640, 640)
    sign = np.sign(d).astype(np.float32)
    abs_pos = np.where((d < mid) & (d > -mid), mid - 1, np.abs(d)).astype(np.float32)
    log_pos = np.ceil(
        np.log(abs_pos / np.float32(mid)) / np.log((max_pos - 1) / mid)
        * np.float32(mid - 1)).astype(np.float32) + mid
    buck = np.where(abs_pos <= mid, d.astype(np.float32), log_pos * sign)
    return np.clip(buck.astype(np.int64) + 256, 0, 511)



class KB:
    def __init__(self, nc, stack):
        self.nc = nc
        self.stack = stack
        self._un = [0]
        _osb, _ops = nc.sbuf_tensor, nc.psum_tensor
        def _usb(name, shape, dtype):
            self._un[0] += 1
            return _osb(f"{name}_u{self._un[0]}", shape, dtype)
        def _ups(name, shape, dtype):
            self._un[0] += 1
            return _ops(f"{name}_u{self._un[0]}", shape, dtype)
        nc.sbuf_tensor = _usb
        nc.psum_tensor = _ups
        self.sems = {}
        self.cnt = {}
        self.seen = {}
        for name in ("te", "ve", "se", "gp", "ds", "dg", "dq", "dsl",
                     "dhi0", "dhi1", "dho", "dsp0", "dsp1", "dt0", "dt1",
                     "dsh", "dcx", "dct", "dtb"):
            self.sems[name] = stack.enter_context(nc.semaphore(name))
            self.cnt[name] = 0
        self.engs = {"te": nc.tensor, "ve": nc.vector, "se": nc.scalar,
                     "gp": nc.gpsimd, "sy": nc.sync}

    def inc(self, inst, s, k=1):
        self.cnt[s] += k
        inst.then_inc(self.sems[s], k)
        return (s, self.cnt[s])

    def tok(self, s):
        return (s, self.cnt[s])

    def wait(self, eng, *toks):
        e = self.engs[eng]
        for t in toks:
            if not t:
                continue
            s, v = t
            if v <= 0:
                continue
            if self.seen.get((eng, s), 0) < v:
                e.wait_ge(self.sems[s], v)
                self.seen[(eng, s)] = v

    @property
    def te(self): return self.nc.tensor
    @property
    def ve(self): return self.nc.vector
    @property
    def se(self): return self.nc.scalar
    @property
    def gp(self): return self.nc.gpsimd
    @property
    def sy(self): return self.nc.sync

    def sb(self, name, shape, dtype):
        return self.stack.enter_context(self.nc.sbuf_tensor(name, shape, dtype))

    def phase_sync(self):
        toks = tuple(self.tok(s) for s in self.sems)
        for e in ("te", "ve", "se", "gp", "sy"):
            self.wait(e, *toks)

    def psum(self, name, shape):
        return self.stack.enter_context(self.nc.psum_tensor(name, shape, F32))


def build_consts(kb):
    cm = {}
    cm["id_bf"] = kb.sb("id_bf", [128, 128], BF16)
    cm["id_f32"] = kb.sb("id_f32", [128, 128], F32)
    cm["ones_bf"] = kb.sb("ones_bf", [128, 512], BF16)
    cm["ones_f32"] = kb.sb("ones_f32", [128, 512], F32)
    g = kb.gp
    for tgt in (cm["id_bf"], cm["id_f32"]):
        tk = kb.inc(g.memset(tgt[:, :], 1.0), "gp")
        kb.wait("gp", tk)
        tk = kb.inc(g.affine_select(
            out=tgt[:, :], in_=tgt[:, :], compare_op=ALU.is_ge,
            fill=0.0, base=0, pattern=[[-1, 128]], channel_multiplier=1), "gp")
        kb.wait("gp", tk)
        tk = kb.inc(g.affine_select(
            out=tgt[:, :], in_=tgt[:, :], compare_op=ALU.is_ge,
            fill=0.0, base=0, pattern=[[1, 128]], channel_multiplier=-1), "gp")
        kb.wait("gp", tk)
    g.memset(cm["ones_bf"][:, :], 1.0)
    g.memset(cm["ones_f32"][:, :], 1.0)
    cm["negshift"] = kb.sb("negshift", [128, 1], F32)
    inst = g.memset(cm["negshift"][:, :], -ASHIFT)
    cm["tok"] = kb.inc(inst, "gp")
    cm["actscr"] = kb.sb("actscr", [1, 8], F32)
    tk = kb.inc(kb.gp.memset(cm["actscr"][0:1, :], 0.5), "gp")
    kb.wait("se", tk)
    seen = set()
    from concourse.hw_specs import get_activation_tables
    tabs = list(get_activation_tables(kb.nc.m.arch).values())
    for f in (AF.Square, AF.Sqrt, AF.Exp, AF.Ln, AF.Sigmoid, AF.Copy,
              AF.Identity, AF.Relu):
        for idx, funcs in enumerate(tabs):
            if f in funcs:
                if idx not in seen:
                    seen.add(idx)
                    kb.se.activation(cm["actscr"][0:1, 0:1],
                                     cm["actscr"][0:1, 1:2], f)
                break
    kb.inc(kb.se.nop(), "se")
    return cm



def load_w_bf16(kb, name, hbm, scale=None, stg=None):
    t = kb.sb(name, [128, NCH * D], BF16)
    tok = None
    if scale is None:
        for c in range(NCH):
            inst = kb.gp.dma_start(t[:, c * D:(c + 1) * D],
                                   hbm[c * 128:(c + 1) * 128, :])
            tok = kb.inc(inst, "dg", 16)
    else:
        for c in range(NCH):
            kb.wait("sy", kb.tok("dq"))
            inst = kb.sy.dma_start(stg[:, :D], hbm[c * 128:(c + 1) * 128, :])
            dt_ = kb.inc(inst, "dsl", 16)
            kb.wait("ve", dt_)
            inst = kb.ve.tensor_scalar_mul(t[:, c * D:(c + 1) * D], stg[:, :D], scale)
            tok = kb.inc(inst, "ve")
            kb.inc(kb.ve.engine_nop(), "dq")
    return t, tok


def load_bias_part(kb, name, hbm, scale=None):
    t = kb.sb(name, [128, NCH], F32)
    if scale is None:
        inst = kb.sy.dma_start(ap(t, 0, [[NCH, 128], [1, NCH]]),
                               ap(hbm, 0, [[1, 128], [128, NCH]]),
                               allow_slow_non_contiguous=True)
        return t, kb.inc(inst, "ds", 16)
    kb.wait("sy", kb.tok("dq"))
    inst = kb.sy.dma_start(ap(t, 0, [[NCH, 128], [1, NCH]]),
                           ap(hbm, 0, [[1, 128], [128, NCH]]),
                           allow_slow_non_contiguous=True)
    tok = kb.inc(inst, "dsl", 16)
    kb.wait("ve", tok)
    inst = kb.ve.tensor_scalar_mul(t[:, :], t[:, :], scale)
    tok = kb.inc(inst, "ve")
    kb.inc(kb.ve.engine_nop(), "dq")
    return t, tok


def load_bias_row(kb, name, hbm):
    t = kb.sb(name, [1, D], BF16)
    inst = kb.gp.dma_start(t[0:1, :], hbm[None, :])
    return t, kb.inc(inst, "dg", 16)



def ln_stats_pair(kb, h_aps, st, dep=()):
    n = len(h_aps)
    assert n <= 8
    se, ve = kb.se, kb.ve
    sq, t = st["sq"], st["t"]
    t_se = t_rd = None
    for c, h_ap in enumerate(h_aps):
        kb.wait("se", *dep)
        inst = se.activation(sq[:, :], h_ap, AF.Square,
                             accum_out=t[:, 8 + c:8 + c + 1])
        t_se = kb.inc(inst, "se")
        kb.wait("ve", *dep)
        inst = ve.reduce_sum(t[:, c:c + 1], h_ap, axis=AXX)
        t_rd = kb.inc(inst, "ve")
    kb.wait("ve", t_se, t_rd)
    s1 = t[:, 0:n]
    s2 = t[:, 8:8 + n]
    scr = t[:, 16:16 + n]
    inst = ve.tensor_tensor(scr, s1, s1, ALU.mult)
    t1 = kb.inc(inst, "ve")
    inst = ve.tensor_scalar_mul(s1, s1, -1.0 / D)
    t_m = kb.inc(inst, "ve")
    kb.wait("ve", t1)
    inst = ve.scalar_tensor_tensor(s2, scr, -1.0 / D, s2, ALU.mult, ALU.add)
    t2 = kb.inc(inst, "ve")
    kb.wait("ve", t2)
    inst = ve.tensor_scalar(s2, s2, 1.0 / D, LN_EPS, ALU.mult, ALU.add)
    t3 = kb.inc(inst, "ve")
    kb.wait("ve", t3)
    inst = ve.reciprocal(s2, s2)
    t_ve = kb.inc(inst, "ve")
    kb.wait("se", t_ve)
    inst = se.activation(s2, s2, AF.Sqrt)
    t_rs = kb.inc(inst, "se")
    return (t_rs, t_m)


def ln_apply_one(kb, h_ap, out_ap, st, c, n, dep=()):
    t = st["t"]
    kb.wait("ve", *dep)
    inst = kb.ve.tensor_scalar(out_ap, h_ap, t[:, c:c + 1], t[:, 8 + c:8 + c + 1],
                               ALU.add, ALU.mult)
    return kb.inc(inst, "ve")



def build_attn_b(kb, cm, wh, h_in, h_out, st, bufs=None):
    nc = kb.nc
    kb.phase_sync()
    with ExitStack() as stk:
        sbl = lambda n, shp, dt: stk.enter_context(nc.sbuf_tensor(n, shp, dt))
        kbsb, kb.sb = kb.sb, sbl
        w = {}
        w["wq1t"], _ = load_w_bf16(kb, "abw_q1", wh["wq1t"])
        w["wk1t"], _ = load_w_bf16(kb, "abw_k1", wh["wk1t"])
        w["wk2t"], _ = load_w_bf16(kb, "abw_k2", wh["wk2t"])
        w["wprojt"], _ = load_w_bf16(kb, "abw_pj", wh["wprojt"])
        w["bq1p"], _ = load_bias_part(kb, "abb_q1", wh["bq1"])
        w["bk1p"], _ = load_bias_part(kb, "abb_k1", wh["bk1"])
        w["bk2p"], _ = load_bias_part(kb, "abb_k2", wh["bk2"], scale=K2_SCALE)
        w["bproj_row"], _ = load_bias_row(kb, "abb_pj", wh["bproj"])
        kb.sb = kbsb
        alltoks = tuple(kb.tok(s) for s in ("ds", "dg", "dsl", "ve"))
        for e in ("te", "ve", "se", "gp"):
            kb.wait(e, *alltoks)
        w["tok"] = alltoks[0]
        bufs = dict(bufs or {})
        bufs.setdefault("dep_h", None)
        bufs["xh32"] = sbl("ab_xh32", [128, D], F32)
        bufs["hb"] = sbl("ab_hb", [128, 2 * (BLK // 128) * D], F32)
        for nm, dt in (("ab_xb", BF16), ("ab_v", F32), ("ab_k", F32),
                       ("ab_q", F32), ("ab_k2", F32), ("ab_kv", F32),
                       ("ab_kc", F32), ("ab_sc", F32), ("ab_ss", F32),
                       ("ab_obf", BF16)):
            bufs[nm] = sbl(nm, [128, FB], dt)
        bufs["ab_car"] = sbl("ab_car", [128, 6 * NCH], F32)
        bufs["ps_a"] = stk.enter_context(nc.psum_tensor("ab_psa", [128, 512], F32))
        bufs["ps_a2"] = stk.enter_context(nc.psum_tensor("ab_psa2", [128, 512], F32))
        bufs["ps_b2"] = stk.enter_context(nc.psum_tensor("ab_psb2", [128, 512], F32))
        bufs["ps_b"] = stk.enter_context(nc.psum_tensor("ab_psb", [128, 512], F32))
        bufs["ps_c"] = stk.enter_context(nc.psum_tensor("ab_psc", [128, 1024], F32))
        ret = _attn_b_body(kb, cm, w, h_in, h_out, st, bufs)
    kb.phase_sync()
    return ret


def _attn_b_body(kb, cm, w, h_in, h_out, st, bufs):
    nc = kb.nc
    CPB = BLK // 128
    XH = bufs["xh32"]
    HB = bufs["hb"]
    XB = bufs["ab_xb"]
    V = bufs["ab_v"]
    K = bufs["ab_k"]
    Q = bufs["ab_q"]
    K2 = bufs["ab_k2"]
    KV = bufs["ab_kv"]
    KC = bufs["ab_kc"]
    SC = bufs["ab_sc"]
    SS = bufs["ab_ss"]
    OA = KC
    E = SS
    K2E = KV
    S2 = K
    O32 = SC
    OBF = bufs["ab_obf"]
    CAR = bufs["ab_car"]
    ps_t = bufs["ps_a"]
    PTs = [bufs["ps_a"], bufs.get("ps_a2", bufs["ps_a"])]
    ps_p = bufs["ps_b"]
    PPs = [bufs["ps_b"], bufs.get("ps_b2", bufs["ps_b"])]
    war_pt = [(), ()]
    war_pp = [(), ()]
    ps_o = bufs["ps_c"]

    cKC = lambda c: CAR[:, 0 * NCH + c:0 * NCH + c + 1]
    cSC = lambda c: CAR[:, 1 * NCH + c:1 * NCH + c + 1]
    cS2 = lambda c: CAR[:, 2 * NCH + c:2 * NCH + c + 1]
    cOA = CAR[:, 3 * NCH:4 * NCH]
    cK2 = CAR[:, 4 * NCH:5 * NCH]
    cQ = CAR[:, 5 * NCH:6 * NCH]

    def chs(t3, lo, hi):
        return ap(t3, lo, [[FB, 128], [BLK, NCH], [1, hi - lo]])

    col0 = lambda t3: ap(t3, 0, [[FB, 128], [BLK, NCH], [1, 1]])

    def hb_chunk(b, i):
        base = (b % 2) * CPB * D
        return HB[:, base + i * D: base + (i + 1) * D]

    ld_tok = {}
    out_tok = {}

    def issue_load(b):
        if b >= NBLK:
            return
        kb.wait("sy", *(bufs.get("dep_h") or ()), kb.tok("dho"))
        s = "dhi%d" % (b % 2)
        for i in range(CPB):
            tc = b * CPB + i
            inst = kb.sy.dma_start(hb_chunk(b, i),
                                   h_in[tc * 128:(tc + 1) * 128, :])
            tk = kb.inc(inst, s, 16)
        ld_tok[b] = (tk,)

    no_prefetch = bool(os.environ.get("AB_NO_PREFETCH"))
    barrier = bool(os.environ.get("AB_BARRIER"))
    issue_load(0)

    war = {k: () for k in ("XH", "XB", "V", "Q", "K2", "P1", "P2", "P3", "P4",
                           "P5", "OBF", "ST")}

    for b in range(NBLK):
        tcs = [b * CPB + i for i in range(CPB)]

        t_st = ln_stats_pair(kb, [hb_chunk(b, i) for i in range(CPB)], st,
                             dep=(kb.tok("dhi%d" % (b % 2)),) + war["ST"])
        if not no_prefetch:
            issue_load(b + 1)
        t_ev = []
        pend = None

        def xb_evac(pcq, pti, ppar, pte):
            kb.wait("ve", pte, *war["V"], *war["XB"])
            inst = kb.ve.tensor_copy(
                ap(V, pcq * 4 * BLK + pti * 128,
                   [[FB, 128], [BLK, 4], [1, 128]]),
                ap(PTs[ppar], 0, [[512, 128], [128, 4], [1, 128]]))
            kb.inc(inst, "ve")
            inst = kb.ve.tensor_copy(
                ap(XB, pcq * 4 * BLK + pti * 128,
                   [[FB, 128], [BLK, 4], [1, 128]]),
                ap(PTs[ppar], 0, [[512, 128], [128, 4], [1, 128]]))
            t = kb.inc(inst, "ve")
            t_ev.append(t)
            war_pt[ppar] = (t,)

        for ti in range(CPB):
            t_x = ln_apply_one(kb, hb_chunk(b, ti), XH[:, :], st, ti, CPB,
                               dep=t_st + war["XH"])
            for cq in range(2):
                par = (ti * 2 + cq) % 2
                kb.wait("te", t_x, *war_pt[par])
                for k4 in range(4):
                    c = cq * 4 + k4
                    inst = kb.te.matmul(
                        PTs[par][:, k4 * 128:(k4 + 1) * 128],
                        XH[:, c * 128:(c + 1) * 128], cm["id_f32"][:, :],
                        start=(k4 == 0), stop=(k4 == 3), is_transpose=True)
                t_te = kb.inc(inst, "te")
                if pend is not None:
                    xb_evac(*pend)
                pend = (cq, ti, par, t_te)
            war["XH"] = (kb.tok("te"),)
        xb_evac(*pend)
        t_xb = t_ev[-1]

        def proj(dst, wt, biasp, func, scale, extra=()):
            tk = None
            for ec in range(NCH):
                par = ec % 2
                kb.wait("te", t_xb, w["tok"], *extra, *war_pp[par])
                for cc in range(NCH):
                    inst = kb.te.matmul(
                        PPs[par][:, 0:BLK],
                        wt[:, cc * D + ec * 128:cc * D + ec * 128 + 128],
                        XB[:, cc * BLK:(cc + 1) * BLK],
                        start=(cc == 0), stop=(cc == NCH - 1))
                t_mm = kb.inc(inst, "te")
                kb.wait("se", t_mm)
                inst = kb.se.activation(dst[:, ec * BLK:(ec + 1) * BLK],
                                        PPs[par][:, 0:BLK], func,
                                        bias=biasp[:, ec:ec + 1], scale=scale)
                tk = kb.inc(inst, "se")
                war_pp[par] = (tk,)
            return tk

        t_K = proj(K, w["wk1t"], w["bk1p"], AF.Exp, 1.0, extra=war["P4"])
        kb.wait("ve", t_K, *war["P5"])
        for c in range(NCH):
            init = 1e-6 if b == 0 else cKC(c)
            inst = kb.ve.tensor_tensor_scan(
                KC[:, c * BLK:(c + 1) * BLK], cm["ones_f32"][:, 0:BLK],
                K[:, c * BLK:(c + 1) * BLK], init, ALU.mult, ALU.add)
        t_kc = kb.inc(inst, "ve")
        kb.wait("ve", t_kc)
        inst = kb.ve.tensor_copy(CAR[:, 0:NCH], chs(KC, BLK - 1, BLK))
        kb.inc(inst, "ve")
        kb.wait("gp", t_K, t_ev[-1], *war["P1"])
        inst = kb.gp.tensor_mul(KV[:, :], K[:, :], V[:, :])
        t_kv = kb.inc(inst, "gp")
        kb.wait("se", t_kc, t_kv)
        inst = kb.se.activation(K[:, :], KC[:, :], AF.Ln)
        t_ln = kb.inc(inst, "se")
        kb.wait("se", t_ln)
        inst = kb.se.activation(K[:, :], K[:, :], AF.Exp, scale=-1.0)
        t_rc = kb.inc(inst, "se")
        kb.wait("ve", t_kv, *war["P2"])
        for c in range(NCH):
            init = 0.0 if b == 0 else cSC(c)
            inst = kb.ve.tensor_tensor_scan(
                SC[:, c * BLK:(c + 1) * BLK], cm["ones_f32"][:, 0:BLK],
                KV[:, c * BLK:(c + 1) * BLK], init, ALU.mult, ALU.add)
        t_sc = kb.inc(inst, "ve")
        kb.wait("ve", t_sc)
        inst = kb.ve.tensor_copy(CAR[:, NCH:2 * NCH], chs(SC, BLK - 1, BLK))
        kb.inc(inst, "ve")
        kb.wait("gp", t_sc, t_rc, *war["P3"])
        inst = kb.gp.tensor_mul(SS[:, :], SC[:, :], K[:, :])
        t_ss = kb.inc(inst, "gp")
        t_Q = proj(Q, w["wq1t"], w["bq1p"], AF.Sigmoid, 1.0, extra=war["Q"])
        kb.wait("gp", t_ss, t_Q)
        inst = kb.gp.tensor_mul(OA[:, :], SS[:, :], Q[:, :])
        t_oa = kb.inc(inst, "gp")
        kb.wait("gp", t_oa)
        inst = kb.gp.tensor_tensor(chs(E, 1, BLK), chs(V, 1, BLK),
                                   chs(OA, 0, BLK - 1), ALU.subtract)
        kb.inc(inst, "gp")
        if b == 0:
            inst = kb.gp.memset(col0(E), 0.0)
        else:
            inst = kb.gp.tensor_tensor(
                col0(E), col0(V),
                ap(CAR, 3 * NCH, [[6 * NCH, 128], [1, NCH], [1, 1]]),
                ALU.subtract)
        t_e = kb.inc(inst, "gp")
        t_K2 = proj(K2, w["wk2t"], w["bk2p"], AF.Sigmoid, K2_SCALE,
                    extra=war["K2"])
        war["XB"] = (kb.tok("te"),)
        kb.wait("gp", t_e, t_K2, t_sc)
        inst = kb.gp.tensor_tensor(chs(K2E, 1, BLK), chs(K2, 0, BLK - 1),
                                   chs(E, 1, BLK), ALU.mult)
        kb.inc(inst, "gp")
        if b == 0:
            inst = kb.gp.memset(col0(K2E), 0.0)
        else:
            inst = kb.gp.tensor_tensor(
                col0(K2E),
                ap(CAR, 4 * NCH, [[6 * NCH, 128], [1, NCH], [1, 1]]),
                col0(E), ALU.mult)
        t_k2e = kb.inc(inst, "gp")
        war["V"] = (t_e,)
        kb.wait("ve", t_e)
        inst = kb.ve.tensor_copy(cOA, chs(OA, BLK - 1, BLK))
        kb.inc(inst, "ve")
        kb.wait("ve", t_k2e)
        inst = kb.ve.tensor_copy(cK2, chs(K2, BLK - 1, BLK))
        t_carK2 = kb.inc(inst, "ve")
        war["K2"] = (t_carK2, t_k2e)
        kb.wait("ve", t_k2e, t_ss)
        for c in range(NCH):
            init = 0.0 if b == 0 else cS2(c)
            inst = kb.ve.tensor_tensor_scan(
                S2[:, c * BLK:(c + 1) * BLK], cm["ones_f32"][:, 0:BLK],
                K2E[:, c * BLK:(c + 1) * BLK], init, ALU.mult, ALU.add)
        t_s2 = kb.inc(inst, "ve")
        war["P1"] = (t_s2,)
        kb.wait("ve", t_s2)
        inst = kb.ve.tensor_copy(CAR[:, 2 * NCH:3 * NCH], chs(S2, BLK - 1, BLK))
        kb.inc(inst, "ve")
        kb.wait("ve", t_s2, t_oa, t_ss)
        inst = kb.ve.scalar_tensor_tensor(
            chs(O32, 1, BLK), chs(Q, 0, BLK - 1), Q2_SCALE,
            chs(S2, 1, BLK), ALU.mult, ALU.mult)
        kb.inc(inst, "ve")
        if b == 0:
            inst = kb.ve.memset(col0(O32), 0.0)
        else:
            inst = kb.ve.scalar_tensor_tensor(
                col0(O32), ap(CAR, 5 * NCH, [[6 * NCH, 128], [1, NCH], [1, 1]]),
                Q2_SCALE, col0(S2), ALU.mult, ALU.mult)
        kb.inc(inst, "ve")
        inst = kb.ve.tensor_copy(cQ, chs(Q, BLK - 1, BLK))
        t_carQ = kb.inc(inst, "ve")
        war["Q"] = (t_carQ, t_oa)
        kb.wait("ve", *war["OBF"], kb.tok("ve"))
        inst = kb.ve.tensor_add(OBF[:, :], O32[:, :], OA[:, :])
        t_o = kb.inc(inst, "ve")
        war["P5"] = (t_o, t_e)
        war["P2"] = (t_o,)
        war["P3"] = (t_k2e,)
        war["P4"] = (t_o,)

        if b == 0 and bufs.get("dbg"):
            kb.wait("sy", t_o, t_e, t_kc, t_sc, t_ev[-1], t_K, t_Q, t_K2)
            for nm, tt in [("dbg_v", V), ("dbg_k", OA), ("dbg_q", Q),
                           ("dbg_kc", KC), ("dbg_sc", SC), ("dbg_e", E),
                           ("dbg_obf", OBF), ("dbg_xb", XB),
                           ("dbg_hb", HB), ("dbg_st", st["t"])]:
                if nm in bufs["dbg"]:
                    inst = kb.sy.dma_start(bufs["dbg"][nm][:, :], tt[:, :])
                    dtk = kb.inc(inst, "ds", 16)
            for e in ("te", "ve", "se", "gp", "sy"):
                kb.wait(e, dtk)
        otk = None
        for ti, tc in enumerate(tcs):
            for nh in range(2):
                kb.wait("te", t_o, w["tok"])
                for cc in range(NCH):
                    inst = kb.te.matmul(
                        ps_o[:, nh * 512:(nh + 1) * 512],
                        OBF[:, cc * BLK + ti * 128:cc * BLK + ti * 128 + 128],
                        w["wprojt"][:, cc * D + nh * 512:cc * D + (nh + 1) * 512],
                        start=(cc == 0), stop=False)
                inst = kb.te.matmul(
                    ps_o[:, nh * 512:(nh + 1) * 512],
                    cm["ones_bf"][0:1, 0:128],
                    w["bproj_row"][0:1, nh * 512:(nh + 1) * 512],
                    start=False, stop=True)
                t_mm = kb.inc(inst, "te")
                kb.wait("ve", t_mm)
                hap = hb_chunk(b, ti)[:, nh * 512:(nh + 1) * 512]
                inst = kb.ve.scalar_tensor_tensor(
                    hap, ps_o[:, nh * 512:(nh + 1) * 512], 1.0, hap,
                    ALU.mult, ALU.add)
                t_res = kb.inc(inst, "ve")
                kb.wait("te", t_res)
            kb.wait("sy", t_res)
            inst = kb.sy.dma_start(h_out[tc * 128:(tc + 1) * 128, :],
                                   hb_chunk(b, ti))
            otk = kb.inc(inst, "dho", 16)
        out_tok[b] = (otk,)
        war["OBF"] = (kb.tok("te"),)
        war["ST"] = (kb.tok("ve"),)
        if no_prefetch:
            issue_load(b + 1)
        if barrier:
            for e in ("te", "ve", "se", "gp"):
                kb.inc(kb.engs[e].engine_nop() if e in ("ve", "gp")
                       else (kb.te.nop() if e == "te" else kb.se.nop()), e)
            toks = tuple(kb.tok(e) for e in ("te", "ve", "se", "gp"))
            for e in ("te", "ve", "se", "gp"):
                kb.wait(e, *toks)

    bufs["dep_h"] = (kb.tok("dho"),)
    return bufs["dep_h"]



def build_ffn(kb, cm, wh, h_in, h_out, st, bufs=None):
    nc = kb.nc
    kb.phase_sync()
    with ExitStack() as stk:
        sbl = lambda n, shp, dt: stk.enter_context(nc.sbuf_tensor(n, shp, dt))
        kbsb, kb.sb = kb.sb, sbl
        w = {}
        w["wrt"], _ = load_w_bf16(kb, "fw_r", wh["wrt"])
        w["wkt"], _ = load_w_bf16(kb, "fw_k", wh["wkt"])
        w["wvt"], _ = load_w_bf16(kb, "fw_v", wh["wvt"])
        w["murp"], _ = load_bias_part(kb, "fb_mr", wh["mur"])
        w["mukp"], _ = load_bias_part(kb, "fb_mk", wh["muk"])
        kb.sb = kbsb
        alltoks = tuple(kb.tok(s) for s in ("ds", "dg", "dsl", "ve"))
        for e in ("te", "ve", "se", "gp"):
            kb.wait(e, *alltoks)
        w["tok"] = alltoks[0]
        bufs = dict(bufs or {})
        bufs.setdefault("dep_h", None)
        bufs["xh32"] = sbl("f_xh32", [128, D], F32)
        bufs["xh32b"] = sbl("f_xh32b", [128, D], F32)
        bufs["hb"] = sbl("f_hb", [128, 2 * (BLK // 128) * D], F32)
        for nm, dt in (("ab_obf", BF16), ("ab_v", F32), ("ab_k", F32),
                       ("ab_kv", F32)):
            bufs[nm] = sbl("f" + nm, [128, FB], dt)
        bufs["ffn_xtp"] = sbl("f_xtp", [128, NCH * (BLK + 1)], BF16)
        bufs["ffn_xr"] = sbl("f_xr", [128, FB], BF16)
        bufs["ffn_xk"] = sbl("f_xk", [128, FB], BF16)
        bufs["ffn_ka2"] = sbl("f_ka2", [128, FB], BF16)
        bufs["ps_a"] = stk.enter_context(nc.psum_tensor("f_psa", [128, 512], F32))
        bufs["ps_a2"] = stk.enter_context(nc.psum_tensor("f_psa2", [128, 512], F32))
        bufs["ps_b2"] = stk.enter_context(nc.psum_tensor("f_psb2", [128, 512], F32))
        bufs["ps_b"] = stk.enter_context(nc.psum_tensor("f_psb", [128, 512], F32))
        bufs["ps_c"] = stk.enter_context(nc.psum_tensor("f_psc", [128, 1024], F32))
        ret = _ffn_body(kb, cm, w, h_in, h_out, st, bufs)
    kb.phase_sync()
    return ret


def _ffn_body(kb, cm, w, h_in, h_out, st, bufs):
    CPB = BLK // 128
    BP = BLK + 1
    FBP = NCH * BP
    XH = bufs["xh32"]
    HB = bufs["hb"]
    XTP = bufs["ffn_xtp"]
    DIF = bufs["ab_obf"]
    XR = bufs["ffn_xr"]
    XK = bufs["ffn_xk"]
    R = bufs["ab_v"]
    KA = bufs["ab_k"]
    KA2 = bufs["ffn_ka2"]
    TMP = bufs["ab_kv"]
    ps_t = bufs["ps_a"]
    XHs = [bufs["xh32"], bufs["xh32b"]]
    PTs = [bufs["ps_a"], bufs["ps_a2"]]
    ps_p = bufs["ps_b"]
    PPs = [bufs["ps_b"], bufs.get("ps_b2", bufs["ps_b"])]
    ps_o = bufs["ps_c"]

    def chp(lo, n):
        return ap(XTP, lo, [[FBP, 128], [BP, NCH], [1, n]])

    def chs(t3, lo, n):
        return ap(t3, lo, [[FB, 128], [BLK, NCH], [1, n]])

    def hb_chunk(b, i):
        base = (b % 2) * CPB * D
        return HB[:, base + i * D: base + (i + 1) * D]

    ld_tok = {}

    def issue_load(b):
        if b >= NBLK:
            return
        kb.wait("sy", *(bufs.get("dep_h") or ()), kb.tok("dho"))
        for i in range(CPB):
            tc = b * CPB + i
            inst = kb.sy.dma_start(hb_chunk(b, i),
                                   h_in[tc * 128:(tc + 1) * 128, :])
            tk = kb.inc(inst, "dhi%d" % (b % 2), 16)
        ld_tok[b] = (tk,)

    issue_load(0)
    war = {k: () for k in ("DIF", "XR", "XK", "R", "KA", "KA2", "TMP", "ST")}
    war["XH"] = [(), ()]
    war["PT"] = [(), ()]
    war["PO"] = [(), ()]
    war["PP"] = [(), ()]
    t_carx = None
    inst = kb.gp.memset(chp(0, 1), 0.0)
    t_z = kb.inc(inst, "gp")
    war["XTP"] = (t_z,)

    for b in range(NBLK):
        t_st = ln_stats_pair(kb, [hb_chunk(b, i) for i in range(CPB)], st,
                             dep=(kb.tok("dhi%d" % (b % 2)),) + war["ST"])
        issue_load(b + 1)
        pend = None
        for ti in range(CPB):
            xh = XHs[ti % 2]
            t_x = ln_apply_one(kb, hb_chunk(b, ti), xh[:, :], st, ti, CPB,
                               dep=t_st + war["XH"][ti % 2])
            for cq in range(2):
                par = (ti * 2 + cq) % 2
                pt = PTs[par]
                kb.wait("te", t_x, *war["PT"][par])
                for k4 in range(4):
                    c = cq * 4 + k4
                    inst = kb.te.matmul(
                        pt[:, k4 * 128:(k4 + 1) * 128],
                        xh[:, c * 128:(c + 1) * 128], cm["id_f32"][:, :],
                        start=(k4 == 0), stop=(k4 == 3), is_transpose=True)
                t_te = kb.inc(inst, "te")
                if pend is not None:
                    (pcq, pti, ppar, pte) = pend
                    kb.wait("ve", pte, *war["XTP"])
                    inst = kb.ve.tensor_copy(
                        ap(XTP, pcq * 4 * BP + pti * 128 + 1,
                           [[FBP, 128], [BP, 4], [1, 128]]),
                        ap(PTs[ppar], 0, [[512, 128], [128, 4], [1, 128]]))
                    war["PT"][ppar] = (kb.inc(inst, "ve"),)
                pend = (cq, ti, par, t_te)
            war["XH"][ti % 2] = (kb.tok("te"),)
        (pcq, pti, ppar, pte) = pend
        kb.wait("ve", pte, *war["XTP"])
        inst = kb.ve.tensor_copy(
            ap(XTP, pcq * 4 * BP + pti * 128 + 1,
               [[FBP, 128], [BP, 4], [1, 128]]),
            ap(PTs[ppar], 0, [[512, 128], [128, 4], [1, 128]]))
        t_xt = kb.inc(inst, "ve")
        war["PT"][ppar] = (t_xt,)

        kb.wait("gp", t_xt, *war["DIF"])
        inst = kb.gp.tensor_tensor(chs(DIF, 0, BLK), chp(1, BLK),
                                   chp(0, BLK), ALU.subtract)
        t_dif = kb.inc(inst, "gp")
        for dst, mu, wk in ((XR, w["murp"], "XR"), (XK, w["mukp"], "XK")):
            kb.wait("ve", t_dif, *war[wk])
            for c in range(NCH):
                inst = kb.ve.scalar_tensor_tensor(
                    dst[:, c * BLK:(c + 1) * BLK],
                    DIF[:, c * BLK:(c + 1) * BLK],
                    mu[:, c:c + 1],
                    ap(XTP, c * BP, [[FBP, 128], [1, BLK]]),
                    ALU.mult, ALU.add)
            war[wk] = (kb.inc(inst, "ve"),)
        t_mix = war["XK"][0]
        kb.wait("gp", t_mix, t_dif)
        inst = kb.gp.tensor_copy(chp(0, 1), chp(BLK, 1))
        t_carx = kb.inc(inst, "gp")
        war["XTP"] = (t_carx, t_mix)
        war["DIF"] = (t_mix,)

        t_r = None
        for ti in range(CPB):
            for nh in range(2):
                kb.wait("te", war["XR"][0], w["tok"], *war["PO"][nh])
                for cc in range(NCH):
                    inst = kb.te.matmul(
                        ps_o[:, nh * 512:(nh + 1) * 512],
                        XR[:, cc * BLK + ti * 128:cc * BLK + ti * 128 + 128],
                        w["wrt"][:, cc * D + nh * 512:cc * D + (nh + 1) * 512],
                        start=(cc == 0), stop=(cc == NCH - 1))
                t_mm = kb.inc(inst, "te")
                kb.wait("se", t_mm, *war["R"])
                inst = kb.se.activation(
                    R[:, ti * D + nh * 512:ti * D + (nh + 1) * 512],
                    ps_o[:, nh * 512:(nh + 1) * 512], AF.Sigmoid)
                t_r = kb.inc(inst, "se")
                war["PO"][nh] = (t_r,)
        war["R"] = ()

        t_ka2 = None
        for ec in range(NCH):
            par = ec % 2
            kb.wait("te", t_mix, w["tok"], *war["PP"][par])
            for cc in range(NCH):
                inst = kb.te.matmul(
                    PPs[par][:, 0:BLK],
                    w["wkt"][:, cc * D + ec * 128:cc * D + ec * 128 + 128],
                    XK[:, cc * BLK:(cc + 1) * BLK],
                    start=(cc == 0), stop=(cc == NCH - 1))
            t_mm = kb.inc(inst, "te")
            kb.wait("ve", t_mm, *war["KA2"])
            inst = kb.ve.tensor_scalar_max(KA[:, ec * BLK:(ec + 1) * BLK],
                                           PPs[par][:, 0:BLK], 0.0)
            t_ka = kb.inc(inst, "ve")
            war["PP"][par] = (t_ka,)
            kb.wait("se", t_ka)
            inst = kb.se.activation(KA2[:, ec * BLK:(ec + 1) * BLK],
                                    KA[:, ec * BLK:(ec + 1) * BLK], AF.Square)
            t_ka2 = kb.inc(inst, "se")
        war["KA"] = (kb.tok("se"),)
        war["KA2"] = ()

        for ti, tc in enumerate(range(b * CPB, (b + 1) * CPB)):
            for nh in range(2):
                kb.wait("te", t_ka2, w["tok"], *war["PO"][nh])
                for cc in range(NCH):
                    inst = kb.te.matmul(
                        ps_o[:, nh * 512:(nh + 1) * 512],
                        KA2[:, cc * BLK + ti * 128:cc * BLK + ti * 128 + 128],
                        w["wvt"][:, cc * D + nh * 512:cc * D + (nh + 1) * 512],
                        start=(cc == 0), stop=(cc == NCH - 1))
                t_mm = kb.inc(inst, "te")
                kb.wait("ve", t_mm, t_r)
                inst = kb.ve.tensor_tensor(
                    TMP[:, ti * D + nh * 512:ti * D + (nh + 1) * 512],
                    ps_o[:, nh * 512:(nh + 1) * 512],
                    R[:, ti * D + nh * 512:ti * D + (nh + 1) * 512], ALU.mult)
                t_p = kb.inc(inst, "ve")
                war["PO"][nh] = (t_p,)
                kb.wait("ve", t_p)
                hap = hb_chunk(b, ti)[:, nh * 512:(nh + 1) * 512]
                inst = kb.ve.tensor_tensor(
                    hap, hap,
                    TMP[:, ti * D + nh * 512:ti * D + (nh + 1) * 512], ALU.add)
                t_res = kb.inc(inst, "ve")
            kb.wait("sy", t_res)
            inst = kb.sy.dma_start(h_out[tc * 128:(tc + 1) * 128, :],
                                   hb_chunk(b, ti))
            otk = kb.inc(inst, "dho", 16)
        war["TMP"] = (kb.tok("ve"),)
        war["ST"] = (kb.tok("ve"),)
        war["KA2"] = (kb.tok("te"),)
        war["R"] = (kb.tok("ve"),)
        war["XR"] = (kb.tok("te"),)
        war["XK"] = (kb.tok("te"),)

    bufs["dep_h"] = (kb.tok("dho"),)
    return bufs["dep_h"]



ASHIFT = 4.0


def aa_ln_transpose_pass(kb, cm, h_in, st, xat, XH, ps_t, hb, dep_h):
    CPB = 2
    war_xh = ()
    war_st = ()
    last_use = {0: (), 1: ()}

    def issue_load(b):
        if b >= NTC // CPB:
            return
        kb.wait("sy", *(dep_h or ()), kb.tok("dho"), *last_use[b % 2])
        for i in range(CPB):
            tc = b * CPB + i
            inst = kb.sy.dma_start(
                hb[:, ((b % 2) * CPB + i) * D:((b % 2) * CPB + i + 1) * D],
                h_in[tc * 128:(tc + 1) * 128, :])
            kb.inc(inst, "dhi%d" % (b % 2), 16)

    issue_load(0)
    t_ev = None
    for b in range(NTC // CPB):
        t_st = ln_stats_pair(
            kb, [hb[:, ((b % 2) * CPB + i) * D:((b % 2) * CPB + i + 1) * D]
                 for i in range(CPB)], st,
            dep=(kb.tok("dhi%d" % (b % 2)),) + war_st)
        for ti in range(CPB):
            tc = b * CPB + ti
            t_x = ln_apply_one(
                kb, hb[:, ((b % 2) * CPB + ti) * D:((b % 2) * CPB + ti + 1) * D],
                XH[:, :], st, ti, CPB, dep=t_st + war_xh)
            for cq in range(2):
                kb.wait("te", t_x, kb.tok("ve"))
                for k4 in range(4):
                    c = cq * 4 + k4
                    inst = kb.te.matmul(
                        ps_t[:, k4 * 128:(k4 + 1) * 128],
                        XH[:, c * 128:(c + 1) * 128], cm["id_f32"][:, :],
                        start=(k4 == 0), stop=(k4 == 3), is_transpose=True)
                t_te = kb.inc(inst, "te")
                kb.wait("ve", t_te)
                inst = kb.ve.tensor_copy(
                    ap(xat, (cq * 4) * S + tc * 128,
                       [[NCH * S, 128], [S, 4], [1, 128]]),
                    ap(ps_t, 0, [[1024, 128], [128, 4], [1, 128]]))
                t_ev = kb.inc(inst, "ve")
                kb.wait("te", t_ev)
            war_xh = (kb.tok("te"),)
        war_st = (kb.tok("ve"),)
        last_use[b % 2] = (kb.tok("ve"), kb.tok("se"))
        issue_load(b + 1)
    return t_ev


def build_attn_a(kb, cm, wh, scr, h_in, h_out, st, bufs=None):
    nc = kb.nc
    kb.phase_sync()
    with ExitStack() as pstk:
        kbsb, kb.sb = kb.sb, (lambda n, shp, dt:
                              pstk.enter_context(nc.sbuf_tensor(n, shp, dt)))
        w = dict(wh)
        w["bqp"], _ = load_bias_part(kb, "aab_q", wh["bq"], scale=QK_SCALE)
        w["bkp"], _ = load_bias_part(kb, "aab_k", wh["bk"])
        w["bv_row"], _ = load_bias_row(kb, "aab_v", wh["bv"])
        w["bo_row"], _ = load_bias_row(kb, "aab_o", wh["bo"])
        kb.sb = kbsb
        bufs = dict(bufs or {})
        bufs.setdefault("dep_h", None)
        sb = lambda n, shp, dt: pstk.enter_context(nc.sbuf_tensor(n, shp, dt))
        psum = lambda n, shp: pstk.enter_context(nc.psum_tensor(n, shp, F32))
        ps_ctx = psum("aa_ctx", [128, 2048])
        ps_s0 = psum("aa_s0", [128, 1024])
        ps_s1 = psum("aa_s1", [128, 1024])
        v_aug = bufs.get("v_aug") or sb("aa_vaug", [128, 16640], BF16)
        alltoks = tuple(kb.tok(s) for s in ("ds", "dg", "dsl", "ve"))
        for e in ("te", "ve", "se", "gp"):
            kb.wait(e, *alltoks)
        rd_all = sb("aa_rd", [16, S], BF16)
        t_q_spill = t_k_spill = t_rd = t_vev = None

        with ExitStack() as pb:
            xat = pb.enter_context(nc.sbuf_tensor("aa_xat", [128, NCH * S], BF16))
            hb = pb.enter_context(nc.sbuf_tensor("aa_hb", [128, 4 * D], F32))
            XH = pb.enter_context(nc.sbuf_tensor("aa_xhln", [128, D], F32))
            wstg = pb.enter_context(nc.sbuf_tensor("aa_wstg", [128, D], F32))
            kbsb, kb.sb = kb.sb, (lambda n, shp, dt:
                                  pb.enter_context(nc.sbuf_tensor(n, shp, dt)))
            wqs, _ = load_w_bf16(kb, "aa_wqs", w["wq"], scale=QK_SCALE, stg=wstg)
            wks, _ = load_w_bf16(kb, "aa_wks", w["wk"])
            wvs, tw_v = load_w_bf16(kb, "aa_wvs", w["wv"])
            kb.sb = kbsb
            w_tok = (kb.tok("dg"), kb.tok("ve"), tw_v)
            t_xat = aa_ln_transpose_pass(kb, cm, h_in, st, xat, XH, ps_s0, hb,
                                         bufs.get("dep_h"))
            if bufs.get("dbgx") is not None:
                kb.wait("sy", t_xat)
                for ci in range(NCH):
                    inst = kb.sy.dma_start(
                        bufs["dbgx"][ci * 128:(ci + 1) * 128, :],
                        ap(xat, ci * S, [[NCH * S, 128], [1, S]]))
                    kb.inc(inst, "ds", 16)
            inst = kb.gp.memset(
                ap(v_aug, 64, [[16640, 128], [65, 256], [1, 1]]), 1.0)
            t_vones = kb.inc(inst, "gp")
            qko = pb.enter_context(nc.sbuf_tensor("aa_qko", [128, 2 * S], BF16))
            war_qko = [(), ()]
            for which, (bias, dst) in enumerate(
                    ((w["bqp"], scr["q_s"]), (w["bkp"], scr["k_s"]))):
                wt_sb = wqs if which == 0 else wks
                for ec in range(NCH):
                    buf = ec % 2
                    t_ev = None
                    for nf in range(4):
                        kb.wait("te", t_xat, *w_tok, *war_qko[buf])
                        for cc in range(NCH):
                            inst = kb.te.matmul(
                                ps_s0[:, 0:512],
                                wt_sb[:, cc * D + ec * 128:cc * D + ec * 128 + 128],
                                xat[:, cc * S + nf * 512:cc * S + (nf + 1) * 512],
                                start=(cc == 0), stop=(cc == NCH - 1))
                        t_mm = kb.inc(inst, "te")
                        kb.wait("se", t_mm)
                        inst = kb.se.activation(
                            qko[:, buf * S + nf * 512:buf * S + (nf + 1) * 512],
                            ps_s0[:, 0:512], AF.Identity, bias=bias[:, ec:ec + 1])
                        t_ev = kb.inc(inst, "se")
                        kb.wait("te", t_ev)
                        if which == 0:
                            kb.wait("te", scr["tok_dvec"])
                            inst = kb.te.matmul(
                                ps_ctx[0:16, nf * 512:(nf + 1) * 512],
                                scr["dvecblk"][:, ec * 16:(ec + 1) * 16],
                                qko[:, buf * S + nf * 512:buf * S + (nf + 1) * 512],
                                start=(ec == 0), stop=(ec == NCH - 1),
                                skip_group_check=True)
                            kb.inc(inst, "te")
                    kb.wait("sy", t_ev, kb.tok("te"))
                    inst = kb.sy.dma_start(
                        dst[ec * 128:(ec + 1) * 128, :],
                        qko[:, buf * S:(buf + 1) * S])
                    tk = kb.inc(inst, "dsp%d" % buf, 16)
                    war_qko[buf] = (tk,)
                if which == 0:
                    t_q_spill = (kb.tok("dsp0"), kb.tok("dsp1"))
                else:
                    t_k_spill = (kb.tok("dsp0"), kb.tok("dsp1"))
            kb.wait("ve", kb.tok("te"))
            inst = kb.ve.tensor_copy(rd_all[0:16, :], ps_ctx[0:16, :])
            t_rd = kb.inc(inst, "ve")

            for tc in range(NTC):
                t_mm = None
                for nh in range(2):
                    kb.wait("te", t_xat, *w_tok, t_vones, t_rd, kb.tok("ve"))
                    for cc in range(NCH):
                        inst = kb.te.matmul(
                            ps_s1[:, nh * 512:(nh + 1) * 512],
                            xat[:, cc * S + tc * 128:cc * S + tc * 128 + 128],
                            wvs[:, cc * D + nh * 512:cc * D + (nh + 1) * 512],
                            start=(cc == 0), stop=False)
                    inst = kb.te.matmul(
                        ps_s1[:, nh * 512:(nh + 1) * 512],
                        cm["ones_bf"][0:1, 0:128],
                        w["bv_row"][0:1, nh * 512:(nh + 1) * 512],
                        start=False, stop=True)
                    t_mm = kb.inc(inst, "te")
                kb.wait("ve", t_mm)
                inst = kb.ve.tensor_copy(
                    ap(v_aug, tc * 1040, [[16640, 128], [65, 16], [1, 64]]),
                    ap(ps_s1, 0, [[1024, 128], [64, 16], [1, 64]]))
                t_vev = kb.inc(inst, "ve")
                kb.wait("te", t_vev)
        kb.phase_sync()

        with ExitStack() as pd:
            sbd = lambda n, shp, dt: pd.enter_context(nc.sbuf_tensor(n, shp, dt))
            stg = sbd("aa_stg", [128, 16 * TW], BF16)
            shn = sbd("aa_shn", [128, 32 * SHW], BF16)
            qh = [sbd("aa_qh%d" % i, [64, S], BF16) for i in range(2)]
            kh = [sbd("aa_kh%d" % i, [64, S], BF16) for i in range(2)]
            pkh = [sbd("aa_pkh%d" % i, [64, TW], BF16) for i in range(2)]
            pqh = [sbd("aa_pqh%d" % i, [64, TW], BF16) for i in range(2)]
            P = [sbd("aa_p%d" % i, [128, 1024], BF16) for i in range(2)]
            kdsat = sbd("aa_kdsat", [128, 32], F32)
            rd0 = sbd("aa_rd0", [1, 2 * S], BF16)
            zrow = sbd("aa_zrow", [1, S], F32)
            zrec = zrow
            c0 = sbd("aa_c0", [128, 2], F32)

            hld = {}

            def head_load(h):
                if h >= H:
                    return
                bb = h % 2
                kb.wait("sy", *hld.get(h - 2, {}).get("war", ()),
                        *t_q_spill, *t_k_spill)
                toks = []
                for dstt, srct in ((qh[bb], scr["q_s"]), (kh[bb], scr["k_s"])):
                    inst = kb.sy.dma_start(dstt[0:64, :],
                                           srct[h * 64:(h + 1) * 64, :])
                    toks.append(kb.inc(inst, "dt%d" % bb, 16))
                kb.wait("sy", scr["tok_tab"])
                inst = kb.sy.dma_start(pkh[bb][0:64, :],
                                       scr["pkrev_s"][h * 64:(h + 1) * 64, :])
                toks.append(kb.inc(inst, "dt%d" % bb, 16))
                inst = kb.sy.dma_start(pqh[bb][0:64, :],
                                       scr["pq_s"][h * 64:(h + 1) * 64, :])
                toks.append(kb.inc(inst, "dt%d" % bb, 16))
                kb.wait("sy", t_rd)
                inst = kb.sy.dma_start(rd0[0:1, bb * S:(bb + 1) * S],
                                       rd_all[h:h + 1, :])
                toks.append(kb.inc(inst, "dt%d" % bb, 16))
                hld[h] = {"tok": (toks[-1],), "war": ()}

            head_load(0)
            war_stg = ()
            war_shn = ()
            war_tab = [[(), ()], [(), ()]]
            war_P = [(), ()]
            war_ctx = ()
            war_ps = ()
            war_kds = ()
            war_ctxn = ()
            t_sh = None

            for h in range(H):
                b = h % 2
                head_load(h + 1)
                tq = kb.tok("dt%d" % b)

                pieces = ((0, 512, ps_s0, 0), (512, 512, ps_s0, 512),
                          (1024, 256, ps_s1, 0))
                war_pc = [tuple(war_ps), tuple(war_ps), tuple(war_ps)]
                for kind in range(2):
                    srct = qh[b] if kind == 0 else kh[b]
                    tab = pkh[b] if kind == 0 else pqh[b]
                    ev = "ve" if kind == 0 else "se"
                    eng = kb.ve if kind == 0 else kb.se
                    t_ev = None
                    ctoks = []
                    for i in range(NTC):
                        tc0 = None
                        for pidx, (p0, pn, pst, off) in enumerate(pieces):
                            kb.wait("te", tq, *war_stg, *war_pc[pidx])
                            inst = kb.te.matmul(
                                pst[:, off:off + pn],
                                srct[:, i * 128:(i + 1) * 128],
                                tab[:, p0:p0 + pn],
                                start=True, stop=True, skip_group_check=True)
                            t_mm = kb.inc(inst, "te")
                            kb.wait(ev, t_mm, *war_stg,
                                    *(war_kds if kind == 1 else ()))
                            dst_sl = stg[:, i * TW + p0:i * TW + p0 + pn]
                            if kind == 0:
                                if pidx == 0:
                                    inst = eng.tensor_copy(c0[:, 0:1],
                                                           ps_s0[:, 0:1])
                                    tc0 = kb.inc(inst, ev)
                                    inst = eng.tensor_scalar(
                                        dst_sl, pst[:, off:off + pn],
                                        ps_s0[:, 0:1], None, ALU.subtract)
                                else:
                                    kb.wait(ev, tc0)
                                    inst = eng.tensor_scalar(
                                        dst_sl, pst[:, off:off + pn],
                                        c0[:, 0:1], None, ALU.subtract)
                            else:
                                inst = eng.activation(dst_sl,
                                                      pst[:, off:off + pn],
                                                      AF.Identity)
                                if pidx == 0:
                                    inst = eng.activation(
                                        kdsat[:, 2 * i:2 * i + 1],
                                        ps_s0[:, 127:128], AF.Identity,
                                        bias=cm["negshift"][:, 0:1])
                                elif pidx == 2:
                                    inst = eng.activation(
                                        kdsat[:, 2 * i + 1:2 * i + 2],
                                        ps_s1[:, 129:130], AF.Identity,
                                        bias=cm["negshift"][:, 0:1])
                            t_ev = kb.inc(inst, ev)
                            war_pc[pidx] = (t_ev,)
                        if i % 4 == 3:
                            ctoks.append(t_ev)
                    tb = scr["tab_s"][kind][h % 2]
                    off0 = 127 if kind == 0 else 128
                    for c4 in range(4):
                        kb.wait("sy", ctoks[c4],
                                *(war_tab[kind][h % 2] if c4 == 0 else ()))
                        inst = kb.sy.dma_start(
                            ap(tb, c4 * 4 * TW, [[16 * TW, 128], [1, 4 * TW]]),
                            ap(stg, c4 * 4 * TW,
                               [[16 * TW, 128], [1, 4 * TW]]))
                        t_st = kb.inc(inst, "dtb", 16)
                        kb.wait("sy", t_st, *(war_shn if c4 == 0 else ()))
                        inst = kb.sy.dma_start(
                            ap(shn, kind * 16 * SHW + c4 * 4 * SHW,
                               [[32 * SHW, 128], [1, 4 * SHW]]),
                            ap(tb, off0 + c4 * 4 * TW,
                               [[16 * TW - 1, 128], [TW, 4], [1, SHW]]))
                        t_sh = kb.inc(inst, "dsh", 16)
                    war_stg = (t_st,)
                    war_tab[kind][h % 2] = (t_sh,)
                    if kind == 1:
                        t_kdse = t_ev

                no_c2p = bool(os.environ.get("AA_NO_C2P"))
                no_p2c = bool(os.environ.get("AA_NO_P2C"))
                t_sh_w = ("dsh", t_sh[1])
                war_half = [war_pc[0] + war_pc[1], tuple(war_pc[2])]
                prev = None
                it = 0
                for j in range(NTC):
                    lo_near = 128 * max(0, j - 4)
                    hi_near = 128 * min(NTC, j + 5)
                    for half, pss in ((0, ps_s0), (1, ps_s1)):
                        ip = it % 2
                        s0 = half * 1024
                        fr_end = min(max(lo_near, s0), s0 + 1024)
                        nr_end = min(max(hi_near, s0), s0 + 1024)
                        kb.wait("te", tq, t_rd, *war_half[half], t_sh_w)
                        for nf in range(2):
                            inst = kb.te.matmul(
                                pss[:, nf * 512:(nf + 1) * 512],
                                kh[b][:, j * 128:(j + 1) * 128],
                                qh[b][:, s0 + nf * 512:s0 + (nf + 1) * 512],
                                start=True, stop=True, skip_group_check=True)
                        if fr_end < nr_end:
                            z0 = (fr_end - s0) + (s0 - 128 * j) + 512
                            pos = fr_end - s0
                            wdt = nr_end - fr_end
                            while (not no_p2c) and wdt > 0:
                                wn = min(512 - (pos % 512), wdt)
                                inst = kb.te.matmul(
                                    pss[:, pos:pos + wn],
                                    cm["id_bf"][:, :],
                                    ap(shn, (16 + j) * SHW + z0,
                                       [[32 * SHW, 128], [1, wn]]),
                                    start=False, stop=True,
                                    skip_group_check=True)
                                z0 += wn; pos += wn; wdt -= wn
                            for i in (() if no_c2p else
                                      range(max(0, j - 4), min(NTC, j + 5))):
                                si = 128 * i
                                if si < s0 or si >= s0 + 1024:
                                    continue
                                t = i - j
                                inst = kb.te.matmul(
                                    pss[:, si - s0:si - s0 + 128],
                                    ap(shn, i * SHW + 128 * (4 - t),
                                       [[32 * SHW, 128], [1, 128]]),
                                    cm["id_bf"][:, :],
                                    start=False, stop=True,
                                    skip_group_check=True)
                        if s0 < fr_end and not no_c2p:
                            pos = 0
                            wdt = fr_end - s0
                            while wdt > 0:
                                wn = min(512 - (pos % 512), wdt)
                                inst = kb.te.matmul(
                                    pss[:, pos:pos + wn],
                                    cm["ones_bf"][0:1, 0:128],
                                    rd0[0:1, b * S + s0 + pos:
                                        b * S + s0 + pos + wn],
                                    start=False, stop=True,
                                    skip_group_check=True)
                                pos += wn; wdt -= wn
                        t_bias = kb.inc(inst, "te")
                        kb.wait("se", t_bias, t_kdse, *war_P[ip])
                        if s0 < fr_end:
                            inst = kb.se.activation(
                                P[ip][:, 0:fr_end - s0],
                                pss[:, 0:fr_end - s0], AF.Exp,
                                bias=(cm["negshift"][:, 0:1] if no_p2c else
                                      kdsat[:, 2 * j:2 * j + 1]))
                        if fr_end < nr_end:
                            inst = kb.se.activation(
                                P[ip][:, fr_end - s0:nr_end - s0],
                                pss[:, fr_end - s0:nr_end - s0],
                                AF.Exp, bias=cm["negshift"][:, 0:1])
                        if nr_end < s0 + 1024:
                            inst = kb.se.activation(
                                P[ip][:, nr_end - s0:1024],
                                pss[:, nr_end - s0:1024],
                                AF.Exp,
                                bias=(cm["negshift"][:, 0:1] if no_p2c else
                                      kdsat[:, 2 * j + 1:2 * j + 2]))
                        t_exp = kb.inc(inst, "se")
                        war_half[half] = (t_exp,)
                        if prev is not None:
                            pj, pip, ps0, pt_exp = prev
                            kb.wait("te", pt_exp,
                                    *((t_vev,) + war_ctx if pj == 0 else ()))
                            for nf in range(2):
                                inst = kb.te.matmul(
                                    ps_ctx[0:65, ps0 + nf * 512:
                                           ps0 + (nf + 1) * 512],
                                    v_aug[:, pj * 1040 + h * 65:
                                          pj * 1040 + (h + 1) * 65],
                                    P[pip][:, nf * 512:(nf + 1) * 512],
                                    start=(pj == 0), stop=(pj == NTC - 1),
                                    skip_group_check=True)
                            t_pv = kb.inc(inst, "te")
                            war_P[pip] = (t_pv,)
                        prev = (j, ip, s0, t_exp)
                        it += 1
                pj, pip, ps0, pt_exp = prev
                kb.wait("te", pt_exp)
                for nf in range(2):
                    inst = kb.te.matmul(
                        ps_ctx[0:65, ps0 + nf * 512:ps0 + (nf + 1) * 512],
                        v_aug[:, pj * 1040 + h * 65:pj * 1040 + (h + 1) * 65],
                        P[pip][:, nf * 512:(nf + 1) * 512],
                        start=(pj == 0), stop=(pj == NTC - 1),
                        skip_group_check=True)
                t_pv = kb.inc(inst, "te")
                war_P[pip] = (t_pv,)

                kb.wait("se", t_pv)
                inst = kb.se.activation(zrow[0:1, :], ps_ctx[64:65, :], AF.Ln)
                t_z = kb.inc(inst, "se")
                kb.wait("se", t_z)
                inst = kb.se.activation(zrec[0:1, :], zrow[0:1, :], AF.Exp,
                                        scale=-1.0)
                t_z = kb.inc(inst, "se")
                war_kds = (t_z,)
                kb.wait("te", t_z)
                for nf in range(4):
                    pst = ps_s0 if nf < 2 else ps_s1
                    inst = kb.te.matmul(
                        pst[0:64, (nf % 2) * 512:(nf % 2 + 1) * 512],
                        cm["ones_f32"][0:1, 0:64],
                        zrec[0:1, nf * 512:(nf + 1) * 512],
                        start=True, stop=True, skip_group_check=True)
                t_bc = kb.inc(inst, "te")
                ctxn_b = stg[0:64, 16 * TW - 2048:16 * TW]
                bcast_b = stg[0:64, 16 * TW - 4096:16 * TW - 2048]
                kb.wait("ve", t_bc, *war_stg, *war_ctxn)
                inst = kb.ve.tensor_copy(bcast_b[:, 0:1024], ps_s0[0:64, :])
                inst = kb.ve.tensor_copy(bcast_b[:, 1024:2048], ps_s1[0:64, :])
                tcp = kb.inc(inst, "ve")
                kb.wait("ve", tcp)
                inst = kb.ve.tensor_tensor(ctxn_b[:, 0:1024],
                                           ps_ctx[0:64, 0:1024],
                                           bcast_b[:, 0:1024], ALU.mult)
                inst = kb.ve.tensor_tensor(ctxn_b[:, 1024:2048],
                                           ps_ctx[0:64, 1024:2048],
                                           bcast_b[:, 1024:2048], ALU.mult)
                t_cn = kb.inc(inst, "ve")
                war_ctx = (t_cn,)
                war_ps = war_half[0] + war_half[1] + (t_cn,)
                kb.wait("sy", t_cn)
                inst = kb.sy.dma_start(scr["ctxn_s"][h * 64:(h + 1) * 64, :],
                                       ctxn_b[:, :])
                tk = kb.inc(inst, "dcx", 16)
                war_ctxn = (tk,)
                war_stg = war_stg + (tk,)
                hld[h]["war"] = (t_cn,)
                war_shn = (kb.tok("te"),)
        kb.phase_sync()

        with ExitStack() as p6:
            ctf = p6.enter_context(nc.sbuf_tensor("aa_ctf", [128, NCH * S], BF16))
            hb2 = p6.enter_context(nc.sbuf_tensor("aa_hb2", [128, 4 * D], F32))
            kbsb, kb.sb = kb.sb, (lambda n, shp, dt:
                                  p6.enter_context(nc.sbuf_tensor(n, shp, dt)))
            wos, tw_o = load_w_bf16(kb, "aa_wos", w["wo"])
            kb.sb = kbsb
            t_cld = None
            for cc in range(NCH):
                kb.wait("sy", kb.tok("dcx"))
                inst = kb.sy.dma_start(
                    ap(ctf, cc * S, [[NCH * S, 128], [1, S]]),
                    scr["ctxn_s"][cc * 128:(cc + 1) * 128, :])
                t_cld = kb.inc(inst, "dct", 16)

            ld = {}

            def issue_load6(bk):
                if bk >= NTC // 2:
                    return
                kb.wait("sy", kb.tok("dho"))
                for i in range(2):
                    tc = bk * 2 + i
                    inst = kb.sy.dma_start(
                        hb2[:, ((bk % 2) * 2 + i) * D:((bk % 2) * 2 + i + 1) * D],
                        h_in[tc * 128:(tc + 1) * 128, :])
                    kb.inc(inst, "dhi%d" % (bk % 2), 16)

            issue_load6(0)
            for bk in range(NTC // 2):
                ldt = (kb.tok("dhi%d" % (bk % 2)),)
                issue_load6(bk + 1)
                for i in range(2):
                    tc = bk * 2 + i
                    for nh in range(2):
                        kb.wait("te", t_cld, tw_o, kb.tok("ve"))
                        for cc in range(NCH):
                            inst = kb.te.matmul(
                                ps_s0[:, nh * 512:(nh + 1) * 512],
                                ctf[:, cc * S + tc * 128:cc * S + tc * 128 + 128],
                                wos[:, cc * D + nh * 512:cc * D + (nh + 1) * 512],
                                start=(cc == 0), stop=False)
                        inst = kb.te.matmul(
                            ps_s0[:, nh * 512:(nh + 1) * 512],
                            cm["ones_bf"][0:1, 0:128],
                            w["bo_row"][0:1, nh * 512:(nh + 1) * 512],
                            start=False, stop=True)
                        t_mm = kb.inc(inst, "te")
                        kb.wait("ve", t_mm, *ldt)
                        hap = hb2[:, (((bk % 2) * 2 + i) * D + nh * 512):
                                  (((bk % 2) * 2 + i) * D + (nh + 1) * 512)]
                        inst = kb.ve.scalar_tensor_tensor(
                            hap, ps_s0[:, nh * 512:(nh + 1) * 512], 1.0, hap,
                            ALU.mult, ALU.add)
                        t_res = kb.inc(inst, "ve")
                        kb.wait("te", t_res)
                    kb.wait("sy", t_res)
                    inst = kb.sy.dma_start(h_out[tc * 128:(tc + 1) * 128, :],
                                           hb2[:, ((bk % 2) * 2 + i) * D:
                                               ((bk % 2) * 2 + i + 1) * D])
                    kb.inc(inst, "dho", 16)
        pass
    kb.phase_sync()
    return (kb.tok("dho"),)


def build_attn_a_setup(kb, cm, io, scr, st):
    nc = kb.nc
    lngp, _ = load_bias_part(kb, "lngp", io["lng"])
    lnbp, t_lnb = load_bias_part(kb, "lnbp", io["lnb"])
    lnbp_bf = kb.sb("lnbp_bf", [128, NCH], BF16)
    kb.wait("ve", t_lnb)
    inst = kb.ve.tensor_copy(lnbp_bf[:, :], lnbp[:, :])
    kb.inc(inst, "ve")
    bpkp, t_bpk = load_bias_part(kb, "bpkp", io["bpk"], scale=PK_SCALE)
    bpqp, t_bpq = load_bias_part(kb, "bpqp", io["bpq"], scale=PQ_SCALE)
    scr["dvecblk"] = kb.sb("dvecblk", [128, 128], BF16)
    inst = kb.gp.memset(scr["dvecblk"][:, :], 0.0)
    t_dv = kb.inc(inst, "gp")

    with ExitStack() as ss:
        sbs = lambda n, shp, dt: ss.enter_context(nc.sbuf_tensor(n, shp, dt))
        lnt = sbs("su_lnt", [128, NCH * TW], BF16)
        hbuf = sbs("su_hb", [128, D], F32)
        XH = sbs("su_xh", [128, D], F32)
        stage = sbs("su_stage", [128, D], F32)
        wsb = sbs("su_wsb", [128, NCH * D], BF16)
        wtmp = sbs("su_wtmp", [128, D], BF16)
        beff = sbs("su_beff", [128, NCH], F32)
        tb = sbs("su_tb", [128, TW], BF16)
        ps_a = ss.enter_context(nc.psum_tensor("su_psa", [128, 1024], F32))
        ps_b = ss.enter_context(nc.psum_tensor("su_psb", [128, 1024], F32))

        pieces = ((0, 512, ps_a, 0), (512, 512, ps_a, 512), (1024, 256, ps_b, 0))
        for kind, (rel, wp, bscaled, t_bs, scale, dst) in enumerate((
                (io["relrev"], io["wpk"], bpkp, t_bpk, PK_SCALE, scr["pkrev_s"]),
                (io["relfwd"], io["wpq"], bpqp, t_bpq, PQ_SCALE, scr["pq_s"]))):
            t_ev = None
            for rc in range(10):
                kb.wait("sy", kb.tok("te"), kb.tok("ve"), kb.tok("se"))
                inst = kb.sy.dma_start(hbuf[:, :], rel[rc * 128:(rc + 1) * 128, :])
                tk = kb.inc(inst, "dt0", 16)
                t_st = ln_stats_pair(kb, [hbuf[:, :]], st, dep=(tk,))
                t_x = ln_apply_one(kb, hbuf[:, :], XH[:, :], st, 0, 1, dep=t_st)
                for cq in range(2):
                    kb.wait("te", t_x, kb.tok("ve"))
                    for k4 in range(4):
                        c = cq * 4 + k4
                        inst = kb.te.matmul(
                            ps_a[:, k4 * 128:(k4 + 1) * 128],
                            XH[:, c * 128:(c + 1) * 128], cm["id_f32"][:, :],
                            start=(k4 == 0), stop=(k4 == 3), is_transpose=True)
                    t_te = kb.inc(inst, "te")
                    kb.wait("ve", t_te)
                    inst = kb.ve.tensor_copy(
                        ap(lnt, cq * 4 * TW + rc * 128,
                           [[NCH * TW, 128], [TW, 4], [1, 128]]),
                        ap(ps_a, 0, [[1024, 128], [128, 4], [1, 128]]))
                    t_ev = kb.inc(inst, "ve")
                    kb.wait("te", t_ev)
            if io.get("dbg") and kind == 0:
                kb.wait("sy", t_ev)
                for ci in range(NCH):
                    inst = kb.sy.dma_start(
                        io["dbg"]["lnt"][ci * 128:(ci + 1) * 128, :],
                        ap(lnt, ci * TW, [[NCH * TW, 128], [1, TW]]))
                    kb.inc(inst, "dct", 16)
            t_w = None
            for cc in range(NCH):
                kb.wait("sy", kb.tok("ve"), kb.tok("te"))
                inst = kb.sy.dma_start(stage[:, :], wp[cc * 128:(cc + 1) * 128, :])
                tk = kb.inc(inst, "dsl", 16)
                kb.wait("ve", tk)
                inst = kb.ve.tensor_scalar(wsb[:, cc * D:(cc + 1) * D], stage[:, :],
                                           lngp[:, cc:cc + 1], scale,
                                           ALU.mult, ALU.mult)
                kb.inc(inst, "ve")
                if cc > 0:
                    kb.wait("ve", t_bm)
                inst = kb.ve.tensor_scalar_mul(wtmp[:, :], stage[:, :], scale)
                t_w = kb.inc(inst, "ve")
                kb.wait("te", t_w)
                for ec in range(NCH):
                    inst = kb.te.matmul(
                        ps_b[:, 512 + ec:512 + ec + 1],
                        wtmp[:, ec * 128:(ec + 1) * 128],
                        lnbp_bf[:, cc:cc + 1],
                        start=(cc == 0 and ec == 0), stop=(cc == NCH - 1),
                        skip_group_check=True)
                t_bm = kb.inc(inst, "te")
            kb.wait("ve", t_bm, t_bs)
            inst = kb.ve.tensor_tensor(beff[:, :], ps_b[0:128, 512:512 + NCH],
                                       bscaled[:, :], ALU.add)
            t_be = kb.inc(inst, "ve")
            if io.get("dbg") and kind == 0:
                kb.wait("sy", t_be, t_w)
                inst = kb.sy.dma_start(io["dbg"]["beff"][:, :], beff[:, :])
                kb.inc(inst, "dct", 16)
                inst = kb.sy.dma_start(io["dbg"]["wsb"][:, :], wsb[:, :])
                kb.inc(inst, "dct", 16)
            t_tb = None
            for ec in range(NCH):
                kb.wait("te", t_ev, t_w, t_be, t_tb)
                for (p0, pn, pst, off) in pieces:
                    for cc in range(NCH):
                        inst = kb.te.matmul(
                            pst[:, off:off + pn],
                            wsb[:, cc * D + ec * 128:cc * D + ec * 128 + 128],
                            ap(lnt, cc * TW + p0, [[NCH * TW, 128], [1, pn]]),
                            start=(cc == 0), stop=(cc == NCH - 1),
                            skip_group_check=True)
                t_mm = kb.inc(inst, "te")
                kb.wait("se", t_mm, t_be, kb.tok("ds"))
                for (p0, pn, pst, off) in pieces:
                    inst = kb.se.activation(tb[:, p0:p0 + pn], pst[:, off:off + pn],
                                            AF.Identity, bias=beff[:, ec:ec + 1])
                t_tb = kb.inc(inst, "se")
                if kind == 0:
                    kb.wait("ve", t_tb, t_dv)
                    inst = kb.ve.tensor_tensor(stage[:, 0:1], tb[:, TW - 1:TW],
                                               tb[:, 0:1], ALU.subtract)
                    t_sub = kb.inc(inst, "ve")
                    kb.wait("ve", t_sub)
                    inst = kb.ve.tensor_copy(
                        scr["dvecblk"][0:64, ec * 16 + 2 * ec:ec * 16 + 2 * ec + 1],
                        stage[0:64, 0:1])
                    inst = kb.ve.tensor_copy(
                        scr["dvecblk"][64:128, ec * 16 + 2 * ec + 1:ec * 16 + 2 * ec + 2],
                        stage[64:128, 0:1])
                    t_dvw = kb.inc(inst, "ve")
                kb.wait("sy", t_tb, (t_dvw if kind == 0 else None))
                inst = kb.sy.dma_start(dst[ec * 128:(ec + 1) * 128, :], tb[:, :])
                tk = kb.inc(inst, "ds", 16)
                kb.wait("se", tk)
    kb.phase_sync()
    scr["tok_tab"] = kb.tok("ds")
    scr["tok_dvec"] = kb.tok("ve")
    return scr



def build_program():
    nc = bass.Bass("TRN2", target_bir_lowering=False,
                   detect_race_conditions=False)
    hb = {}
    hb["x"] = nc.dram_tensor("x", [SEQS_PER_CORE * S, D], F32, kind="ExternalInput")
    for nm in ("aa_wq", "aa_wk", "aa_wv", "aa_wo", "aa_wpk", "aa_wpq"):
        hb[nm] = nc.dram_tensor(nm, [D, D], F32, kind="ExternalInput")
    for nm in ("aa_bq", "aa_bk", "aa_bv", "aa_bo", "aa_bpk", "aa_bpq",
               "aa_lng", "aa_lnb", "f_mur", "f_muk",
               "b_bq1", "b_bk1", "b_bk2", "b_bproj"):
        hb[nm] = nc.dram_tensor(nm, [D], F32, kind="ExternalInput")
    for nm in ("aa_relfwd", "aa_relrev"):
        hb[nm] = nc.dram_tensor(nm, [TW, D], F32, kind="ExternalInput")
    for nm in ("f_wrt", "f_wkt", "f_wvt", "b_wq1t", "b_wk1t", "b_wk2t",
               "b_wprojt"):
        hb[nm] = nc.dram_tensor(nm, [D, D], F32, kind="ExternalInput")
    y = nc.dram_tensor("y", [SEQS_PER_CORE * S, D], F32, kind="ExternalOutput")
    h_a = nc.dram_tensor("h_a", [S, D], F32, kind="Internal")
    h_b = nc.dram_tensor("h_b", [S, D], F32, kind="Internal")
    scr = {
        "q_s": nc.dram_tensor("q_s", [D, S], BF16, kind="Internal"),
        "k_s": nc.dram_tensor("k_s", [D, S], BF16, kind="Internal"),
        "pkrev_s": nc.dram_tensor("pkrev_s", [D, TW], BF16, kind="Internal"),
        "pq_s": nc.dram_tensor("pq_s", [D, TW], BF16, kind="Internal"),
        "ctxn_s": nc.dram_tensor("ctxn_s", [D, S], BF16, kind="Internal"),
        "tab_s": [[nc.dram_tensor("tab_s%d_%d" % (k, p), [128, 16 * TW],
                                  BF16, kind="Internal") for p in range(2)]
                  for k in range(2)],
    }
    with ExitStack() as stack:
        kb = KB(nc, stack)
        cm = build_consts(kb)
        st = {"sq": kb.sb("ln_sq", [128, D], F32),
              "t": kb.sb("ln_t", [128, 24], F32)}
        io = {"relrev": hb["aa_relrev"], "relfwd": hb["aa_relfwd"],
              "wpk": hb["aa_wpk"], "wpq": hb["aa_wpq"], "bpk": hb["aa_bpk"],
              "bpq": hb["aa_bpq"], "lng": hb["aa_lng"], "lnb": hb["aa_lnb"]}
        build_attn_a_setup(kb, cm, io, scr, st)
        wa = {"wq": hb["aa_wq"], "wk": hb["aa_wk"], "wv": hb["aa_wv"],
              "wo": hb["aa_wo"], "bq": hb["aa_bq"], "bk": hb["aa_bk"],
              "bv": hb["aa_bv"], "bo": hb["aa_bo"]}
        wf = {"wrt": hb["f_wrt"], "wkt": hb["f_wkt"], "wvt": hb["f_wvt"],
              "mur": hb["f_mur"], "muk": hb["f_muk"]}
        wb = {"wq1t": hb["b_wq1t"], "wk1t": hb["b_wk1t"], "wk2t": hb["b_wk2t"],
              "wprojt": hb["b_wprojt"], "bq1": hb["b_bq1"], "bk1": hb["b_bk1"],
              "bk2": hb["b_bk2"], "bproj": hb["b_bproj"]}
        dep = None
        for sq in range(SEQS_PER_CORE):
            x_sl = hb["x"][sq * S:(sq + 1) * S, :]
            y_sl = y[sq * S:(sq + 1) * S, :]
            dep = build_attn_a(kb, cm, wa, scr, x_sl, h_a, st,
                               {"dep_h": dep})
            dep = build_ffn(kb, cm, wf, h_a, h_b, st, {"dep_h": dep})
            dep = build_attn_b(kb, cm, wb, h_b, h_a, st, {"dep_h": dep})
            dep = build_ffn(kb, cm, wf, h_a, y_sl, st, {"dep_h": dep})
        kb.wait("sy", kb.tok("dho"))
    return nc


_NC_CACHE = {}


def kernel(**inputs):
    g = lambda n: np.ascontiguousarray(inputs[n])
    t = lambda n: np.ascontiguousarray(np.transpose(inputs[n]))
    f = _bucket_f()
    rel = inputs["L0_attn_a_rel_emb"]
    relfwd = np.ascontiguousarray(rel[f])
    relrev = np.ascontiguousarray(rel[f[::-1]])
    common = {
        "aa_wq": g("L0_attn_a_Wq"), "aa_wk": g("L0_attn_a_Wk"),
        "aa_wv": g("L0_attn_a_Wv"), "aa_wo": g("L0_attn_a_Wo"),
        "aa_wpk": g("L0_attn_a_Wpk"), "aa_wpq": g("L0_attn_a_Wpq"),
        "aa_bq": g("L0_attn_a_bq"), "aa_bk": g("L0_attn_a_bk"),
        "aa_bv": g("L0_attn_a_bv"), "aa_bo": g("L0_attn_a_bo"),
        "aa_bpk": g("L0_attn_a_bpk"), "aa_bpq": g("L0_attn_a_bpq"),
        "aa_lng": g("L0_attn_a_ln_g"), "aa_lnb": g("L0_attn_a_ln_b"),
        "aa_relfwd": relfwd, "aa_relrev": relrev,
        "f_wrt": t("L0_ffn_W_r"), "f_wkt": t("L0_ffn_W_k"),
        "f_wvt": t("L0_ffn_W_v"),
        "f_mur": g("L0_ffn_mu_r"), "f_muk": g("L0_ffn_mu_k"),
        "b_wq1t": t("L0_attn_b_Wq1"), "b_wk1t": t("L0_attn_b_Wk1"),
        "b_wk2t": t("L0_attn_b_Wk2"), "b_wprojt": t("L0_attn_b_Wproj"),
        "b_bq1": g("L0_attn_b_bq1"), "b_bk1": g("L0_attn_b_bk1"),
        "b_bk2": g("L0_attn_b_bk2"), "b_bproj": g("L0_attn_b_bproj"),
    }
    xs = inputs["L0_attn_a_hidden_states"]
    B = xs.shape[0]
    per = B // NB_CORES
    if "nc" not in _NC_CACHE:
        _NC_CACHE["nc"] = build_program()
    nc = _NC_CACHE["nc"]
    in_maps = []
    for c in range(NB_CORES):
        m = dict(common)
        m["x"] = np.ascontiguousarray(
            xs[c * per:(c + 1) * per].reshape(per * S, D))
        in_maps.append(m)
    res = run_bass_kernel_spmd(nc, in_maps, core_ids=list(range(NB_CORES)))
    out = np.empty((B, S, D), dtype=np.float32)
    for c in range(NB_CORES):
        out[c * per:(c + 1) * per] = res.results[c]["y"].reshape(per, S, D)
    return out


# version 10
# speedup vs baseline: 2.8246x; 1.0664x over previous
import math
import os
from contextlib import ExitStack

import numpy as np
import ml_dtypes
import concourse.bass as bass
import concourse.mybir as mybir
from concourse.bass_utils import run_bass_kernel_spmd

F32 = mybir.dt.float32
BF16 = mybir.dt.bfloat16
AF = mybir.ActivationFunctionType
ALU = mybir.AluOpType
AXX = mybir.AxisListType.X

D = 1024
S = 2048
NB_CORES = 8
SEQS_PER_CORE = 4
H = 16
HD = 64
NCH = D // 128
NTC = S // 128
TW = 1280
SHW = 1152
BLK = 256
NBLK = S // BLK
FB = NCH * BLK

QK_SCALE = 1.0 / math.sqrt(HD * 3.0)
PK_SCALE = math.sqrt(1.5)
PQ_SCALE = 1.0 / math.sqrt(HD * 2.0)
K2_SCALE = 0.02 / math.sqrt(D)
Q2_SCALE = 0.02 / math.sqrt(D)
LN_EPS = 1e-5


def ap(t, off, dims):
    return bass.AP(t, off, dims)



def _bucket_f():
    mid, max_pos = 128, 512
    d = np.arange(-640, 640)
    sign = np.sign(d).astype(np.float32)
    abs_pos = np.where((d < mid) & (d > -mid), mid - 1, np.abs(d)).astype(np.float32)
    log_pos = np.ceil(
        np.log(abs_pos / np.float32(mid)) / np.log((max_pos - 1) / mid)
        * np.float32(mid - 1)).astype(np.float32) + mid
    buck = np.where(abs_pos <= mid, d.astype(np.float32), log_pos * sign)
    return np.clip(buck.astype(np.int64) + 256, 0, 511)



class KB:
    def __init__(self, nc, stack):
        self.nc = nc
        self.stack = stack
        self._un = [0]
        _osb, _ops = nc.sbuf_tensor, nc.psum_tensor
        def _usb(name, shape, dtype):
            self._un[0] += 1
            return _osb(f"{name}_u{self._un[0]}", shape, dtype)
        def _ups(name, shape, dtype):
            self._un[0] += 1
            return _ops(f"{name}_u{self._un[0]}", shape, dtype)
        nc.sbuf_tensor = _usb
        nc.psum_tensor = _ups
        self.sems = {}
        self.cnt = {}
        self.seen = {}
        for name in ("te", "ve", "se", "gp", "ds", "dg", "dq", "dsl",
                     "dhi0", "dhi1", "dho", "dsp0", "dsp1", "dt0", "dt1",
                     "dsh", "dcx", "dct", "dtb"):
            self.sems[name] = stack.enter_context(nc.semaphore(name))
            self.cnt[name] = 0
        self.engs = {"te": nc.tensor, "ve": nc.vector, "se": nc.scalar,
                     "gp": nc.gpsimd, "sy": nc.sync}

    def inc(self, inst, s, k=1):
        self.cnt[s] += k
        inst.then_inc(self.sems[s], k)
        return (s, self.cnt[s])

    def tok(self, s):
        return (s, self.cnt[s])

    def wait(self, eng, *toks):
        e = self.engs[eng]
        for t in toks:
            if not t:
                continue
            s, v = t
            if v <= 0:
                continue
            if self.seen.get((eng, s), 0) < v:
                e.wait_ge(self.sems[s], v)
                self.seen[(eng, s)] = v

    @property
    def te(self): return self.nc.tensor
    @property
    def ve(self): return self.nc.vector
    @property
    def se(self): return self.nc.scalar
    @property
    def gp(self): return self.nc.gpsimd
    @property
    def sy(self): return self.nc.sync

    def sb(self, name, shape, dtype):
        return self.stack.enter_context(self.nc.sbuf_tensor(name, shape, dtype))

    def phase_sync(self):
        toks = tuple(self.tok(s) for s in self.sems)
        for e in ("te", "ve", "se", "gp", "sy"):
            self.wait(e, *toks)

    def psum(self, name, shape):
        return self.stack.enter_context(self.nc.psum_tensor(name, shape, F32))


def build_consts(kb):
    cm = {}
    cm["id_bf"] = kb.sb("id_bf", [128, 128], BF16)
    cm["id_f32"] = kb.sb("id_f32", [128, 128], F32)
    cm["ones_bf"] = kb.sb("ones_bf", [128, 512], BF16)
    cm["ones_f32"] = kb.sb("ones_f32", [128, 512], F32)
    g = kb.gp
    for tgt in (cm["id_bf"], cm["id_f32"]):
        tk = kb.inc(g.memset(tgt[:, :], 1.0), "gp")
        kb.wait("gp", tk)
        tk = kb.inc(g.affine_select(
            out=tgt[:, :], in_=tgt[:, :], compare_op=ALU.is_ge,
            fill=0.0, base=0, pattern=[[-1, 128]], channel_multiplier=1), "gp")
        kb.wait("gp", tk)
        tk = kb.inc(g.affine_select(
            out=tgt[:, :], in_=tgt[:, :], compare_op=ALU.is_ge,
            fill=0.0, base=0, pattern=[[1, 128]], channel_multiplier=-1), "gp")
        kb.wait("gp", tk)
    g.memset(cm["ones_bf"][:, :], 1.0)
    g.memset(cm["ones_f32"][:, :], 1.0)
    cm["negshift"] = kb.sb("negshift", [128, 1], F32)
    inst = g.memset(cm["negshift"][:, :], -ASHIFT)
    cm["tok"] = kb.inc(inst, "gp")
    cm["actscr"] = kb.sb("actscr", [1, 8], F32)
    tk = kb.inc(kb.gp.memset(cm["actscr"][0:1, :], 0.5), "gp")
    kb.wait("se", tk)
    seen = set()
    from concourse.hw_specs import get_activation_tables
    tabs = list(get_activation_tables(kb.nc.m.arch).values())
    for f in (AF.Square, AF.Sqrt, AF.Exp, AF.Ln, AF.Sigmoid, AF.Copy,
              AF.Identity, AF.Relu):
        for idx, funcs in enumerate(tabs):
            if f in funcs:
                if idx not in seen:
                    seen.add(idx)
                    kb.se.activation(cm["actscr"][0:1, 0:1],
                                     cm["actscr"][0:1, 1:2], f)
                break
    kb.inc(kb.se.nop(), "se")
    return cm



def load_w_bf16(kb, name, hbm, scale=None, stg=None):
    t = kb.sb(name, [128, NCH * D], BF16)
    tok = None
    if scale is None:
        for c in range(NCH):
            inst = kb.gp.dma_start(t[:, c * D:(c + 1) * D],
                                   hbm[c * 128:(c + 1) * 128, :])
            tok = kb.inc(inst, "dg", 16)
    else:
        for c in range(NCH):
            kb.wait("sy", kb.tok("dq"))
            inst = kb.sy.dma_start(stg[:, :D], hbm[c * 128:(c + 1) * 128, :])
            dt_ = kb.inc(inst, "dsl", 16)
            kb.wait("ve", dt_)
            inst = kb.ve.tensor_scalar_mul(t[:, c * D:(c + 1) * D], stg[:, :D], scale)
            tok = kb.inc(inst, "ve")
            kb.inc(kb.ve.engine_nop(), "dq")
    return t, tok


def load_bias_part(kb, name, hbm, scale=None):
    t = kb.sb(name, [128, NCH], F32)
    if scale is None:
        inst = kb.sy.dma_start(ap(t, 0, [[NCH, 128], [1, NCH]]),
                               ap(hbm, 0, [[1, 128], [128, NCH]]),
                               allow_slow_non_contiguous=True)
        return t, kb.inc(inst, "ds", 16)
    kb.wait("sy", kb.tok("dq"))
    inst = kb.sy.dma_start(ap(t, 0, [[NCH, 128], [1, NCH]]),
                           ap(hbm, 0, [[1, 128], [128, NCH]]),
                           allow_slow_non_contiguous=True)
    tok = kb.inc(inst, "dsl", 16)
    kb.wait("ve", tok)
    inst = kb.ve.tensor_scalar_mul(t[:, :], t[:, :], scale)
    tok = kb.inc(inst, "ve")
    kb.inc(kb.ve.engine_nop(), "dq")
    return t, tok


def load_bias_row(kb, name, hbm):
    t = kb.sb(name, [1, D], BF16)
    inst = kb.gp.dma_start(t[0:1, :], hbm[None, :])
    return t, kb.inc(inst, "dg", 16)



def ln_stats_pair(kb, h_aps, st, dep=()):
    n = len(h_aps)
    assert n <= 8
    se, ve = kb.se, kb.ve
    sq, t = st["sq"], st["t"]
    t_se = t_rd = None
    for c, h_ap in enumerate(h_aps):
        kb.wait("se", *dep)
        inst = se.activation(sq[:, :], h_ap, AF.Square,
                             accum_out=t[:, 8 + c:8 + c + 1])
        t_se = kb.inc(inst, "se")
        kb.wait("ve", *dep)
        inst = ve.reduce_sum(t[:, c:c + 1], h_ap, axis=AXX)
        t_rd = kb.inc(inst, "ve")
    kb.wait("ve", t_se, t_rd)
    s1 = t[:, 0:n]
    s2 = t[:, 8:8 + n]
    scr = t[:, 16:16 + n]
    inst = ve.tensor_tensor(scr, s1, s1, ALU.mult)
    t1 = kb.inc(inst, "ve")
    inst = ve.tensor_scalar_mul(s1, s1, -1.0 / D)
    t_m = kb.inc(inst, "ve")
    kb.wait("ve", t1)
    inst = ve.scalar_tensor_tensor(s2, scr, -1.0 / D, s2, ALU.mult, ALU.add)
    t2 = kb.inc(inst, "ve")
    kb.wait("ve", t2)
    inst = ve.tensor_scalar(s2, s2, 1.0 / D, LN_EPS, ALU.mult, ALU.add)
    t3 = kb.inc(inst, "ve")
    kb.wait("ve", t3)
    inst = ve.reciprocal(s2, s2)
    t_ve = kb.inc(inst, "ve")
    kb.wait("se", t_ve)
    inst = se.activation(s2, s2, AF.Sqrt)
    t_rs = kb.inc(inst, "se")
    return (t_rs, t_m)


def ln_apply_one(kb, h_ap, out_ap, st, c, n, dep=()):
    t = st["t"]
    kb.wait("ve", *dep)
    inst = kb.ve.tensor_scalar(out_ap, h_ap, t[:, c:c + 1], t[:, 8 + c:8 + c + 1],
                               ALU.add, ALU.mult)
    return kb.inc(inst, "ve")



def build_attn_b(kb, cm, wh, h_in, h_out, st, bufs=None):
    nc = kb.nc
    kb.phase_sync()
    with ExitStack() as stk:
        sbl = lambda n, shp, dt: stk.enter_context(nc.sbuf_tensor(n, shp, dt))
        kbsb, kb.sb = kb.sb, sbl
        w = {}
        w["wq1t"], _ = load_w_bf16(kb, "abw_q1", wh["wq1t"])
        w["wk1t"], _ = load_w_bf16(kb, "abw_k1", wh["wk1t"])
        w["wk2t"], _ = load_w_bf16(kb, "abw_k2", wh["wk2t"])
        w["wprojt"], _ = load_w_bf16(kb, "abw_pj", wh["wprojt"])
        w["bq1p"], _ = load_bias_part(kb, "abb_q1", wh["bq1"])
        w["bk1p"], _ = load_bias_part(kb, "abb_k1", wh["bk1"])
        w["bk2p"], _ = load_bias_part(kb, "abb_k2", wh["bk2"], scale=K2_SCALE)
        w["bproj_row"], _ = load_bias_row(kb, "abb_pj", wh["bproj"])
        kb.sb = kbsb
        alltoks = tuple(kb.tok(s) for s in ("ds", "dg", "dsl", "ve"))
        for e in ("te", "ve", "se", "gp"):
            kb.wait(e, *alltoks)
        w["tok"] = alltoks[0]
        bufs = dict(bufs or {})
        bufs.setdefault("dep_h", None)
        bufs["xh32"] = sbl("ab_xh32", [128, D], F32)
        bufs["hb"] = sbl("ab_hb", [128, 2 * (BLK // 128) * D], F32)
        for nm, dt in (("ab_xb", BF16), ("ab_v", F32), ("ab_k", F32),
                       ("ab_q", F32), ("ab_k2", F32), ("ab_kv", F32),
                       ("ab_kc", F32), ("ab_sc", F32), ("ab_ss", F32),
                       ("ab_obf", BF16)):
            bufs[nm] = sbl(nm, [128, FB], dt)
        bufs["ab_car"] = sbl("ab_car", [128, 6 * NCH], F32)
        bufs["ps_a"] = stk.enter_context(nc.psum_tensor("ab_psa", [128, 512], F32))
        bufs["ps_a2"] = stk.enter_context(nc.psum_tensor("ab_psa2", [128, 512], F32))
        bufs["ps_b2"] = stk.enter_context(nc.psum_tensor("ab_psb2", [128, 512], F32))
        bufs["ps_b"] = stk.enter_context(nc.psum_tensor("ab_psb", [128, 512], F32))
        bufs["ps_c"] = stk.enter_context(nc.psum_tensor("ab_psc", [128, 1024], F32))
        ret = _attn_b_body(kb, cm, w, h_in, h_out, st, bufs)
    kb.phase_sync()
    return ret


def _attn_b_body(kb, cm, w, h_in, h_out, st, bufs):
    nc = kb.nc
    CPB = BLK // 128
    XH = bufs["xh32"]
    HB = bufs["hb"]
    XB = bufs["ab_xb"]
    V = bufs["ab_v"]
    K = bufs["ab_k"]
    Q = bufs["ab_q"]
    K2 = bufs["ab_k2"]
    KV = bufs["ab_kv"]
    KC = bufs["ab_kc"]
    SC = bufs["ab_sc"]
    SS = bufs["ab_ss"]
    OA = KC
    E = SS
    K2E = KV
    S2 = K
    O32 = SC
    OBF = bufs["ab_obf"]
    CAR = bufs["ab_car"]
    ps_t = bufs["ps_a"]
    PTs = [bufs["ps_a"], bufs.get("ps_a2", bufs["ps_a"])]
    ps_p = bufs["ps_b"]
    PPs = [bufs["ps_b"], bufs.get("ps_b2", bufs["ps_b"])]
    war_pt = [(), ()]
    war_pp = [(), ()]
    ps_o = bufs["ps_c"]

    cKC = lambda c: CAR[:, 0 * NCH + c:0 * NCH + c + 1]
    cSC = lambda c: CAR[:, 1 * NCH + c:1 * NCH + c + 1]
    cS2 = lambda c: CAR[:, 2 * NCH + c:2 * NCH + c + 1]
    cOA = CAR[:, 3 * NCH:4 * NCH]
    cK2 = CAR[:, 4 * NCH:5 * NCH]
    cQ = CAR[:, 5 * NCH:6 * NCH]

    def chs(t3, lo, hi):
        return ap(t3, lo, [[FB, 128], [BLK, NCH], [1, hi - lo]])

    col0 = lambda t3: ap(t3, 0, [[FB, 128], [BLK, NCH], [1, 1]])

    def hb_chunk(b, i):
        base = (b % 2) * CPB * D
        return HB[:, base + i * D: base + (i + 1) * D]

    ld_tok = {}
    out_tok = {}

    def issue_load(b):
        if b >= NBLK:
            return
        kb.wait("sy", *(bufs.get("dep_h") or ()), kb.tok("dho"))
        s = "dhi%d" % (b % 2)
        for i in range(CPB):
            tc = b * CPB + i
            inst = kb.sy.dma_start(hb_chunk(b, i),
                                   h_in[tc * 128:(tc + 1) * 128, :])
            tk = kb.inc(inst, s, 16)
        ld_tok[b] = (tk,)

    no_prefetch = bool(os.environ.get("AB_NO_PREFETCH"))
    barrier = bool(os.environ.get("AB_BARRIER"))
    issue_load(0)

    war = {k: () for k in ("XH", "XB", "V", "Q", "K2", "P1", "P2", "P3", "P4",
                           "P5", "OBF", "ST")}

    for b in range(NBLK):
        tcs = [b * CPB + i for i in range(CPB)]

        t_st = ln_stats_pair(kb, [hb_chunk(b, i) for i in range(CPB)], st,
                             dep=(kb.tok("dhi%d" % (b % 2)),) + war["ST"])
        if not no_prefetch:
            issue_load(b + 1)
        t_ev = []
        pend = None

        def xb_evac(pcq, pti, ppar, pte):
            kb.wait("ve", pte, *war["V"], *war["XB"])
            inst = kb.ve.tensor_copy(
                ap(V, pcq * 4 * BLK + pti * 128,
                   [[FB, 128], [BLK, 4], [1, 128]]),
                ap(PTs[ppar], 0, [[512, 128], [128, 4], [1, 128]]))
            kb.inc(inst, "ve")
            inst = kb.ve.tensor_copy(
                ap(XB, pcq * 4 * BLK + pti * 128,
                   [[FB, 128], [BLK, 4], [1, 128]]),
                ap(PTs[ppar], 0, [[512, 128], [128, 4], [1, 128]]))
            t = kb.inc(inst, "ve")
            t_ev.append(t)
            war_pt[ppar] = (t,)

        for ti in range(CPB):
            t_x = ln_apply_one(kb, hb_chunk(b, ti), XH[:, :], st, ti, CPB,
                               dep=t_st + war["XH"])
            for cq in range(2):
                par = (ti * 2 + cq) % 2
                kb.wait("te", t_x, *war_pt[par])
                for k4 in range(4):
                    c = cq * 4 + k4
                    inst = kb.te.matmul(
                        PTs[par][:, k4 * 128:(k4 + 1) * 128],
                        XH[:, c * 128:(c + 1) * 128], cm["id_f32"][:, :],
                        start=(k4 == 0), stop=(k4 == 3), is_transpose=True)
                t_te = kb.inc(inst, "te")
                if pend is not None:
                    xb_evac(*pend)
                pend = (cq, ti, par, t_te)
            war["XH"] = (kb.tok("te"),)
        xb_evac(*pend)
        t_xb = t_ev[-1]

        def proj(dst, wt, biasp, func, scale, extra=()):
            tk = None
            for ec in range(NCH):
                par = ec % 2
                kb.wait("te", t_xb, w["tok"], *extra, *war_pp[par])
                for cc in range(NCH):
                    inst = kb.te.matmul(
                        PPs[par][:, 0:BLK],
                        wt[:, cc * D + ec * 128:cc * D + ec * 128 + 128],
                        XB[:, cc * BLK:(cc + 1) * BLK],
                        start=(cc == 0), stop=(cc == NCH - 1))
                t_mm = kb.inc(inst, "te")
                kb.wait("se", t_mm)
                inst = kb.se.activation(dst[:, ec * BLK:(ec + 1) * BLK],
                                        PPs[par][:, 0:BLK], func,
                                        bias=biasp[:, ec:ec + 1], scale=scale)
                tk = kb.inc(inst, "se")
                war_pp[par] = (tk,)
            return tk

        t_K = proj(K, w["wk1t"], w["bk1p"], AF.Exp, 1.0, extra=war["P4"])
        kb.wait("ve", t_K, *war["P5"])
        for c in range(NCH):
            init = 1e-6 if b == 0 else cKC(c)
            inst = kb.ve.tensor_tensor_scan(
                KC[:, c * BLK:(c + 1) * BLK], cm["ones_f32"][:, 0:BLK],
                K[:, c * BLK:(c + 1) * BLK], init, ALU.mult, ALU.add)
        t_kc = kb.inc(inst, "ve")
        kb.wait("ve", t_kc)
        inst = kb.ve.tensor_copy(CAR[:, 0:NCH], chs(KC, BLK - 1, BLK))
        kb.inc(inst, "ve")
        kb.wait("gp", t_K, t_ev[-1], *war["P1"])
        inst = kb.gp.tensor_mul(KV[:, :], K[:, :], V[:, :])
        t_kv = kb.inc(inst, "gp")
        kb.wait("se", t_kc, t_kv)
        inst = kb.se.activation(K[:, :], KC[:, :], AF.Ln)
        t_ln = kb.inc(inst, "se")
        kb.wait("se", t_ln)
        inst = kb.se.activation(K[:, :], K[:, :], AF.Exp, scale=-1.0)
        t_rc = kb.inc(inst, "se")
        kb.wait("ve", t_kv, *war["P2"])
        for c in range(NCH):
            init = 0.0 if b == 0 else cSC(c)
            inst = kb.ve.tensor_tensor_scan(
                SC[:, c * BLK:(c + 1) * BLK], cm["ones_f32"][:, 0:BLK],
                KV[:, c * BLK:(c + 1) * BLK], init, ALU.mult, ALU.add)
        t_sc = kb.inc(inst, "ve")
        kb.wait("ve", t_sc)
        inst = kb.ve.tensor_copy(CAR[:, NCH:2 * NCH], chs(SC, BLK - 1, BLK))
        kb.inc(inst, "ve")
        kb.wait("gp", t_sc, t_rc, *war["P3"])
        inst = kb.gp.tensor_mul(SS[:, :], SC[:, :], K[:, :])
        t_ss = kb.inc(inst, "gp")
        t_Q = proj(Q, w["wq1t"], w["bq1p"], AF.Sigmoid, 1.0, extra=war["Q"])
        kb.wait("gp", t_ss, t_Q)
        inst = kb.gp.tensor_mul(OA[:, :], SS[:, :], Q[:, :])
        t_oa = kb.inc(inst, "gp")
        kb.wait("gp", t_oa)
        inst = kb.gp.tensor_tensor(chs(E, 1, BLK), chs(V, 1, BLK),
                                   chs(OA, 0, BLK - 1), ALU.subtract)
        kb.inc(inst, "gp")
        if b == 0:
            inst = kb.gp.memset(col0(E), 0.0)
        else:
            inst = kb.gp.tensor_tensor(
                col0(E), col0(V),
                ap(CAR, 3 * NCH, [[6 * NCH, 128], [1, NCH], [1, 1]]),
                ALU.subtract)
        t_e = kb.inc(inst, "gp")
        t_K2 = proj(K2, w["wk2t"], w["bk2p"], AF.Sigmoid, K2_SCALE,
                    extra=war["K2"])
        war["XB"] = (kb.tok("te"),)
        kb.wait("gp", t_e, t_K2, t_sc)
        inst = kb.gp.tensor_tensor(chs(K2E, 1, BLK), chs(K2, 0, BLK - 1),
                                   chs(E, 1, BLK), ALU.mult)
        kb.inc(inst, "gp")
        if b == 0:
            inst = kb.gp.memset(col0(K2E), 0.0)
        else:
            inst = kb.gp.tensor_tensor(
                col0(K2E),
                ap(CAR, 4 * NCH, [[6 * NCH, 128], [1, NCH], [1, 1]]),
                col0(E), ALU.mult)
        t_k2e = kb.inc(inst, "gp")
        war["V"] = (t_e,)
        kb.wait("ve", t_e)
        inst = kb.ve.tensor_copy(cOA, chs(OA, BLK - 1, BLK))
        kb.inc(inst, "ve")
        kb.wait("ve", t_k2e)
        inst = kb.ve.tensor_copy(cK2, chs(K2, BLK - 1, BLK))
        t_carK2 = kb.inc(inst, "ve")
        war["K2"] = (t_carK2, t_k2e)
        kb.wait("ve", t_k2e, t_ss)
        for c in range(NCH):
            init = 0.0 if b == 0 else cS2(c)
            inst = kb.ve.tensor_tensor_scan(
                S2[:, c * BLK:(c + 1) * BLK], cm["ones_f32"][:, 0:BLK],
                K2E[:, c * BLK:(c + 1) * BLK], init, ALU.mult, ALU.add)
        t_s2 = kb.inc(inst, "ve")
        war["P1"] = (t_s2,)
        kb.wait("ve", t_s2)
        inst = kb.ve.tensor_copy(CAR[:, 2 * NCH:3 * NCH], chs(S2, BLK - 1, BLK))
        kb.inc(inst, "ve")
        kb.wait("ve", t_s2, t_oa, t_ss)
        inst = kb.ve.scalar_tensor_tensor(
            chs(O32, 1, BLK), chs(Q, 0, BLK - 1), Q2_SCALE,
            chs(S2, 1, BLK), ALU.mult, ALU.mult)
        kb.inc(inst, "ve")
        if b == 0:
            inst = kb.ve.memset(col0(O32), 0.0)
        else:
            inst = kb.ve.scalar_tensor_tensor(
                col0(O32), ap(CAR, 5 * NCH, [[6 * NCH, 128], [1, NCH], [1, 1]]),
                Q2_SCALE, col0(S2), ALU.mult, ALU.mult)
        kb.inc(inst, "ve")
        inst = kb.ve.tensor_copy(cQ, chs(Q, BLK - 1, BLK))
        t_carQ = kb.inc(inst, "ve")
        war["Q"] = (t_carQ, t_oa)
        kb.wait("ve", *war["OBF"], kb.tok("ve"))
        inst = kb.ve.tensor_add(OBF[:, :], O32[:, :], OA[:, :])
        t_o = kb.inc(inst, "ve")
        war["P5"] = (t_o, t_e)
        war["P2"] = (t_o,)
        war["P3"] = (t_k2e,)
        war["P4"] = (t_o,)

        if b == 0 and bufs.get("dbg"):
            kb.wait("sy", t_o, t_e, t_kc, t_sc, t_ev[-1], t_K, t_Q, t_K2)
            for nm, tt in [("dbg_v", V), ("dbg_k", OA), ("dbg_q", Q),
                           ("dbg_kc", KC), ("dbg_sc", SC), ("dbg_e", E),
                           ("dbg_obf", OBF), ("dbg_xb", XB),
                           ("dbg_hb", HB), ("dbg_st", st["t"])]:
                if nm in bufs["dbg"]:
                    inst = kb.sy.dma_start(bufs["dbg"][nm][:, :], tt[:, :])
                    dtk = kb.inc(inst, "ds", 16)
            for e in ("te", "ve", "se", "gp", "sy"):
                kb.wait(e, dtk)
        otk = None
        for ti, tc in enumerate(tcs):
            for nh in range(2):
                kb.wait("te", t_o, w["tok"])
                for cc in range(NCH):
                    inst = kb.te.matmul(
                        ps_o[:, nh * 512:(nh + 1) * 512],
                        OBF[:, cc * BLK + ti * 128:cc * BLK + ti * 128 + 128],
                        w["wprojt"][:, cc * D + nh * 512:cc * D + (nh + 1) * 512],
                        start=(cc == 0), stop=False)
                inst = kb.te.matmul(
                    ps_o[:, nh * 512:(nh + 1) * 512],
                    cm["ones_bf"][0:1, 0:128],
                    w["bproj_row"][0:1, nh * 512:(nh + 1) * 512],
                    start=False, stop=True)
                t_mm = kb.inc(inst, "te")
                kb.wait("ve", t_mm)
                hap = hb_chunk(b, ti)[:, nh * 512:(nh + 1) * 512]
                inst = kb.ve.scalar_tensor_tensor(
                    hap, ps_o[:, nh * 512:(nh + 1) * 512], 1.0, hap,
                    ALU.mult, ALU.add)
                t_res = kb.inc(inst, "ve")
                kb.wait("te", t_res)
            kb.wait("sy", t_res)
            inst = kb.sy.dma_start(h_out[tc * 128:(tc + 1) * 128, :],
                                   hb_chunk(b, ti))
            otk = kb.inc(inst, "dho", 16)
        out_tok[b] = (otk,)
        war["OBF"] = (kb.tok("te"),)
        war["ST"] = (kb.tok("ve"),)
        if no_prefetch:
            issue_load(b + 1)
        if barrier:
            for e in ("te", "ve", "se", "gp"):
                kb.inc(kb.engs[e].engine_nop() if e in ("ve", "gp")
                       else (kb.te.nop() if e == "te" else kb.se.nop()), e)
            toks = tuple(kb.tok(e) for e in ("te", "ve", "se", "gp"))
            for e in ("te", "ve", "se", "gp"):
                kb.wait(e, *toks)

    bufs["dep_h"] = (kb.tok("dho"),)
    return bufs["dep_h"]



def build_ffn(kb, cm, wh, h_in, h_out, st, bufs=None):
    nc = kb.nc
    kb.phase_sync()
    with ExitStack() as stk:
        sbl = lambda n, shp, dt: stk.enter_context(nc.sbuf_tensor(n, shp, dt))
        kbsb, kb.sb = kb.sb, sbl
        w = {}
        w["wrt"], _ = load_w_bf16(kb, "fw_r", wh["wrt"])
        w["wkt"], _ = load_w_bf16(kb, "fw_k", wh["wkt"])
        w["wvt"], _ = load_w_bf16(kb, "fw_v", wh["wvt"])
        w["murp"], _ = load_bias_part(kb, "fb_mr", wh["mur"])
        w["mukp"], _ = load_bias_part(kb, "fb_mk", wh["muk"])
        kb.sb = kbsb
        alltoks = tuple(kb.tok(s) for s in ("ds", "dg", "dsl", "ve"))
        for e in ("te", "ve", "se", "gp"):
            kb.wait(e, *alltoks)
        w["tok"] = alltoks[0]
        bufs = dict(bufs or {})
        bufs.setdefault("dep_h", None)
        bufs["xh32"] = sbl("f_xh32", [128, D], F32)
        bufs["xh32b"] = sbl("f_xh32b", [128, D], F32)
        bufs["hb"] = sbl("f_hb", [128, 2 * (BLK // 128) * D], F32)
        for nm, dt in (("ab_obf", BF16), ("ab_v", F32), ("ab_k", F32),
                       ("ab_kv", F32)):
            bufs[nm] = sbl("f" + nm, [128, FB], dt)
        bufs["ffn_xtp"] = sbl("f_xtp", [128, NCH * (BLK + 1)], BF16)
        bufs["ffn_xr"] = sbl("f_xr", [128, FB], BF16)
        bufs["ffn_xk"] = sbl("f_xk", [128, FB], BF16)
        bufs["ffn_ka2"] = sbl("f_ka2", [128, FB], BF16)
        bufs["ps_a"] = stk.enter_context(nc.psum_tensor("f_psa", [128, 512], F32))
        bufs["ps_a2"] = stk.enter_context(nc.psum_tensor("f_psa2", [128, 512], F32))
        bufs["ps_b2"] = stk.enter_context(nc.psum_tensor("f_psb2", [128, 512], F32))
        bufs["ps_b"] = stk.enter_context(nc.psum_tensor("f_psb", [128, 512], F32))
        bufs["ps_c"] = stk.enter_context(nc.psum_tensor("f_psc", [128, 1024], F32))
        ret = _ffn_body(kb, cm, w, h_in, h_out, st, bufs)
    kb.phase_sync()
    return ret


def _ffn_body(kb, cm, w, h_in, h_out, st, bufs):
    CPB = BLK // 128
    BP = BLK + 1
    FBP = NCH * BP
    XH = bufs["xh32"]
    HB = bufs["hb"]
    XTP = bufs["ffn_xtp"]
    DIF = bufs["ab_obf"]
    XR = bufs["ffn_xr"]
    XK = bufs["ffn_xk"]
    R = bufs["ab_v"]
    KA = bufs["ab_k"]
    KA2 = bufs["ffn_ka2"]
    TMP = bufs["ab_kv"]
    ps_t = bufs["ps_a"]
    XHs = [bufs["xh32"], bufs["xh32b"]]
    PTs = [bufs["ps_a"], bufs["ps_a2"]]
    ps_p = bufs["ps_b"]
    PPs = [bufs["ps_b"], bufs.get("ps_b2", bufs["ps_b"])]
    ps_o = bufs["ps_c"]

    def chp(lo, n):
        return ap(XTP, lo, [[FBP, 128], [BP, NCH], [1, n]])

    def chs(t3, lo, n):
        return ap(t3, lo, [[FB, 128], [BLK, NCH], [1, n]])

    def hb_chunk(b, i):
        base = (b % 2) * CPB * D
        return HB[:, base + i * D: base + (i + 1) * D]

    ld_tok = {}

    def issue_load(b):
        if b >= NBLK:
            return
        kb.wait("sy", *(bufs.get("dep_h") or ()), kb.tok("dho"))
        for i in range(CPB):
            tc = b * CPB + i
            inst = kb.sy.dma_start(hb_chunk(b, i),
                                   h_in[tc * 128:(tc + 1) * 128, :])
            tk = kb.inc(inst, "dhi%d" % (b % 2), 16)
        ld_tok[b] = (tk,)

    issue_load(0)
    war = {k: () for k in ("DIF", "XR", "XK", "R", "KA", "KA2", "TMP", "ST")}
    war["XH"] = [(), ()]
    war["PT"] = [(), ()]
    war["PO"] = [(), ()]
    war["PP"] = [(), ()]
    t_carx = None
    inst = kb.gp.memset(chp(0, 1), 0.0)
    t_z = kb.inc(inst, "gp")
    war["XTP"] = (t_z,)

    for b in range(NBLK):
        t_st = ln_stats_pair(kb, [hb_chunk(b, i) for i in range(CPB)], st,
                             dep=(kb.tok("dhi%d" % (b % 2)),) + war["ST"])
        issue_load(b + 1)
        pend = None
        for ti in range(CPB):
            xh = XHs[ti % 2]
            t_x = ln_apply_one(kb, hb_chunk(b, ti), xh[:, :], st, ti, CPB,
                               dep=t_st + war["XH"][ti % 2])
            for cq in range(2):
                par = (ti * 2 + cq) % 2
                pt = PTs[par]
                kb.wait("te", t_x, *war["PT"][par])
                for k4 in range(4):
                    c = cq * 4 + k4
                    inst = kb.te.matmul(
                        pt[:, k4 * 128:(k4 + 1) * 128],
                        xh[:, c * 128:(c + 1) * 128], cm["id_f32"][:, :],
                        start=(k4 == 0), stop=(k4 == 3), is_transpose=True)
                t_te = kb.inc(inst, "te")
                if pend is not None:
                    (pcq, pti, ppar, pte) = pend
                    kb.wait("ve", pte, *war["XTP"])
                    inst = kb.ve.tensor_copy(
                        ap(XTP, pcq * 4 * BP + pti * 128 + 1,
                           [[FBP, 128], [BP, 4], [1, 128]]),
                        ap(PTs[ppar], 0, [[512, 128], [128, 4], [1, 128]]))
                    war["PT"][ppar] = (kb.inc(inst, "ve"),)
                pend = (cq, ti, par, t_te)
            war["XH"][ti % 2] = (kb.tok("te"),)
        (pcq, pti, ppar, pte) = pend
        kb.wait("ve", pte, *war["XTP"])
        inst = kb.ve.tensor_copy(
            ap(XTP, pcq * 4 * BP + pti * 128 + 1,
               [[FBP, 128], [BP, 4], [1, 128]]),
            ap(PTs[ppar], 0, [[512, 128], [128, 4], [1, 128]]))
        t_xt = kb.inc(inst, "ve")
        war["PT"][ppar] = (t_xt,)

        kb.wait("gp", t_xt, *war["DIF"])
        inst = kb.gp.tensor_tensor(chs(DIF, 0, BLK), chp(1, BLK),
                                   chp(0, BLK), ALU.subtract)
        t_dif = kb.inc(inst, "gp")
        for dst, mu, wk in ((XR, w["murp"], "XR"), (XK, w["mukp"], "XK")):
            kb.wait("ve", t_dif, *war[wk])
            for c in range(NCH):
                inst = kb.ve.scalar_tensor_tensor(
                    dst[:, c * BLK:(c + 1) * BLK],
                    DIF[:, c * BLK:(c + 1) * BLK],
                    mu[:, c:c + 1],
                    ap(XTP, c * BP, [[FBP, 128], [1, BLK]]),
                    ALU.mult, ALU.add)
            war[wk] = (kb.inc(inst, "ve"),)
        t_mix = war["XK"][0]
        kb.wait("gp", t_mix, t_dif)
        inst = kb.gp.tensor_copy(chp(0, 1), chp(BLK, 1))
        t_carx = kb.inc(inst, "gp")
        war["XTP"] = (t_carx, t_mix)
        war["DIF"] = (t_mix,)

        t_r = None
        for ti in range(CPB):
            for nh in range(2):
                kb.wait("te", war["XR"][0], w["tok"], *war["PO"][nh])
                for cc in range(NCH):
                    inst = kb.te.matmul(
                        ps_o[:, nh * 512:(nh + 1) * 512],
                        XR[:, cc * BLK + ti * 128:cc * BLK + ti * 128 + 128],
                        w["wrt"][:, cc * D + nh * 512:cc * D + (nh + 1) * 512],
                        start=(cc == 0), stop=(cc == NCH - 1))
                t_mm = kb.inc(inst, "te")
                kb.wait("se", t_mm, *war["R"])
                inst = kb.se.activation(
                    R[:, ti * D + nh * 512:ti * D + (nh + 1) * 512],
                    ps_o[:, nh * 512:(nh + 1) * 512], AF.Sigmoid)
                t_r = kb.inc(inst, "se")
                war["PO"][nh] = (t_r,)
        war["R"] = ()

        t_ka2 = None
        for ec in range(NCH):
            par = ec % 2
            kb.wait("te", t_mix, w["tok"], *war["PP"][par])
            for cc in range(NCH):
                inst = kb.te.matmul(
                    PPs[par][:, 0:BLK],
                    w["wkt"][:, cc * D + ec * 128:cc * D + ec * 128 + 128],
                    XK[:, cc * BLK:(cc + 1) * BLK],
                    start=(cc == 0), stop=(cc == NCH - 1))
            t_mm = kb.inc(inst, "te")
            kb.wait("ve", t_mm, *war["KA2"])
            inst = kb.ve.tensor_scalar_max(KA[:, ec * BLK:(ec + 1) * BLK],
                                           PPs[par][:, 0:BLK], 0.0)
            t_ka = kb.inc(inst, "ve")
            war["PP"][par] = (t_ka,)
            kb.wait("se", t_ka)
            inst = kb.se.activation(KA2[:, ec * BLK:(ec + 1) * BLK],
                                    KA[:, ec * BLK:(ec + 1) * BLK], AF.Square)
            t_ka2 = kb.inc(inst, "se")
        war["KA"] = (kb.tok("se"),)
        war["KA2"] = ()

        for ti, tc in enumerate(range(b * CPB, (b + 1) * CPB)):
            for nh in range(2):
                kb.wait("te", t_ka2, w["tok"], *war["PO"][nh])
                for cc in range(NCH):
                    inst = kb.te.matmul(
                        ps_o[:, nh * 512:(nh + 1) * 512],
                        KA2[:, cc * BLK + ti * 128:cc * BLK + ti * 128 + 128],
                        w["wvt"][:, cc * D + nh * 512:cc * D + (nh + 1) * 512],
                        start=(cc == 0), stop=(cc == NCH - 1))
                t_mm = kb.inc(inst, "te")
                kb.wait("ve", t_mm, t_r)
                inst = kb.ve.tensor_tensor(
                    TMP[:, ti * D + nh * 512:ti * D + (nh + 1) * 512],
                    ps_o[:, nh * 512:(nh + 1) * 512],
                    R[:, ti * D + nh * 512:ti * D + (nh + 1) * 512], ALU.mult)
                t_p = kb.inc(inst, "ve")
                war["PO"][nh] = (t_p,)
                kb.wait("ve", t_p)
                hap = hb_chunk(b, ti)[:, nh * 512:(nh + 1) * 512]
                inst = kb.ve.tensor_tensor(
                    hap, hap,
                    TMP[:, ti * D + nh * 512:ti * D + (nh + 1) * 512], ALU.add)
                t_res = kb.inc(inst, "ve")
            kb.wait("sy", t_res)
            inst = kb.sy.dma_start(h_out[tc * 128:(tc + 1) * 128, :],
                                   hb_chunk(b, ti))
            otk = kb.inc(inst, "dho", 16)
        war["TMP"] = (kb.tok("ve"),)
        war["ST"] = (kb.tok("ve"),)
        war["KA2"] = (kb.tok("te"),)
        war["R"] = (kb.tok("ve"),)
        war["XR"] = (kb.tok("te"),)
        war["XK"] = (kb.tok("te"),)

    bufs["dep_h"] = (kb.tok("dho"),)
    return bufs["dep_h"]



ASHIFT = 4.0


def aa_ln_transpose_pass(kb, cm, h_in, st, xat, XH, ps_t, hb, dep_h):
    CPB = 2
    war_xh = ()
    war_st = ()
    last_use = {0: (), 1: ()}

    def issue_load(b):
        if b >= NTC // CPB:
            return
        kb.wait("sy", *(dep_h or ()), kb.tok("dho"), *last_use[b % 2])
        for i in range(CPB):
            tc = b * CPB + i
            inst = kb.sy.dma_start(
                hb[:, ((b % 2) * CPB + i) * D:((b % 2) * CPB + i + 1) * D],
                h_in[tc * 128:(tc + 1) * 128, :])
            kb.inc(inst, "dhi%d" % (b % 2), 16)

    issue_load(0)
    t_ev = None
    for b in range(NTC // CPB):
        t_st = ln_stats_pair(
            kb, [hb[:, ((b % 2) * CPB + i) * D:((b % 2) * CPB + i + 1) * D]
                 for i in range(CPB)], st,
            dep=(kb.tok("dhi%d" % (b % 2)),) + war_st)
        for ti in range(CPB):
            tc = b * CPB + ti
            t_x = ln_apply_one(
                kb, hb[:, ((b % 2) * CPB + ti) * D:((b % 2) * CPB + ti + 1) * D],
                XH[:, :], st, ti, CPB, dep=t_st + war_xh)
            for cq in range(2):
                kb.wait("te", t_x, kb.tok("ve"))
                for k4 in range(4):
                    c = cq * 4 + k4
                    inst = kb.te.matmul(
                        ps_t[:, k4 * 128:(k4 + 1) * 128],
                        XH[:, c * 128:(c + 1) * 128], cm["id_f32"][:, :],
                        start=(k4 == 0), stop=(k4 == 3), is_transpose=True)
                t_te = kb.inc(inst, "te")
                kb.wait("ve", t_te)
                inst = kb.ve.tensor_copy(
                    ap(xat, (cq * 4) * S + tc * 128,
                       [[NCH * S, 128], [S, 4], [1, 128]]),
                    ap(ps_t, 0, [[1024, 128], [128, 4], [1, 128]]))
                t_ev = kb.inc(inst, "ve")
                kb.wait("te", t_ev)
            war_xh = (kb.tok("te"),)
        war_st = (kb.tok("ve"),)
        last_use[b % 2] = (kb.tok("ve"), kb.tok("se"))
        issue_load(b + 1)
    return t_ev


def build_attn_a(kb, cm, wh, scr, h_in, h_out, st, bufs=None):
    nc = kb.nc
    kb.phase_sync()
    with ExitStack() as pstk:
        kbsb, kb.sb = kb.sb, (lambda n, shp, dt:
                              pstk.enter_context(nc.sbuf_tensor(n, shp, dt)))
        w = dict(wh)
        w["bqp"], _ = load_bias_part(kb, "aab_q", wh["bq"], scale=QK_SCALE)
        w["bkp"], _ = load_bias_part(kb, "aab_k", wh["bk"])
        w["bv_row"], _ = load_bias_row(kb, "aab_v", wh["bv"])
        w["bo_row"], _ = load_bias_row(kb, "aab_o", wh["bo"])
        kb.sb = kbsb
        bufs = dict(bufs or {})
        bufs.setdefault("dep_h", None)
        sb = lambda n, shp, dt: pstk.enter_context(nc.sbuf_tensor(n, shp, dt))
        psum = lambda n, shp: pstk.enter_context(nc.psum_tensor(n, shp, F32))
        ps_ctx = psum("aa_ctx", [128, 2048])
        ps_s0 = psum("aa_s0", [128, 1024])
        ps_s1 = psum("aa_s1", [128, 1024])
        v_aug = bufs.get("v_aug") or sb("aa_vaug", [128, 16640], BF16)
        alltoks = tuple(kb.tok(s) for s in ("ds", "dg", "dsl", "ve"))
        for e in ("te", "ve", "se", "gp"):
            kb.wait(e, *alltoks)
        rd_all = sb("aa_rd", [16, S], BF16)
        t_q_spill = t_k_spill = t_rd = t_vev = None

        with ExitStack() as pb:
            xat = pb.enter_context(nc.sbuf_tensor("aa_xat", [128, NCH * S], BF16))
            hb = pb.enter_context(nc.sbuf_tensor("aa_hb", [128, 4 * D], F32))
            XH = pb.enter_context(nc.sbuf_tensor("aa_xhln", [128, D], F32))
            wstg = pb.enter_context(nc.sbuf_tensor("aa_wstg", [128, D], F32))
            kbsb, kb.sb = kb.sb, (lambda n, shp, dt:
                                  pb.enter_context(nc.sbuf_tensor(n, shp, dt)))
            wqs, _ = load_w_bf16(kb, "aa_wqs", w["wq"], scale=QK_SCALE, stg=wstg)
            wks, _ = load_w_bf16(kb, "aa_wks", w["wk"])
            wvs, tw_v = load_w_bf16(kb, "aa_wvs", w["wv"])
            kb.sb = kbsb
            w_tok = (kb.tok("dg"), kb.tok("ve"), tw_v)
            t_xat = aa_ln_transpose_pass(kb, cm, h_in, st, xat, XH, ps_s0, hb,
                                         bufs.get("dep_h"))
            if bufs.get("dbgx") is not None:
                kb.wait("sy", t_xat)
                for ci in range(NCH):
                    inst = kb.sy.dma_start(
                        bufs["dbgx"][ci * 128:(ci + 1) * 128, :],
                        ap(xat, ci * S, [[NCH * S, 128], [1, S]]))
                    kb.inc(inst, "ds", 16)
            inst = kb.gp.memset(
                ap(v_aug, 64, [[16640, 128], [65, 256], [1, 1]]), 1.0)
            t_vones = kb.inc(inst, "gp")
            qko = pb.enter_context(nc.sbuf_tensor("aa_qko", [128, 2 * S], BF16))
            war_qko = [(), ()]
            for which, (bias, dst) in enumerate(
                    ((w["bqp"], scr["q_s"]), (w["bkp"], scr["k_s"]))):
                wt_sb = wqs if which == 0 else wks
                for ec in range(NCH):
                    buf = ec % 2
                    t_ev = None
                    for nf in range(4):
                        kb.wait("te", t_xat, *w_tok, *war_qko[buf])
                        for cc in range(NCH):
                            inst = kb.te.matmul(
                                ps_s0[:, 0:512],
                                wt_sb[:, cc * D + ec * 128:cc * D + ec * 128 + 128],
                                xat[:, cc * S + nf * 512:cc * S + (nf + 1) * 512],
                                start=(cc == 0), stop=(cc == NCH - 1))
                        t_mm = kb.inc(inst, "te")
                        kb.wait("se", t_mm)
                        inst = kb.se.activation(
                            qko[:, buf * S + nf * 512:buf * S + (nf + 1) * 512],
                            ps_s0[:, 0:512], AF.Identity, bias=bias[:, ec:ec + 1])
                        t_ev = kb.inc(inst, "se")
                        kb.wait("te", t_ev)
                        if which == 0:
                            kb.wait("te", scr["tok_dvec"])
                            inst = kb.te.matmul(
                                ps_ctx[0:16, nf * 512:(nf + 1) * 512],
                                scr["dvecblk"][:, ec * 16:(ec + 1) * 16],
                                qko[:, buf * S + nf * 512:buf * S + (nf + 1) * 512],
                                start=(ec == 0), stop=(ec == NCH - 1),
                                skip_group_check=True)
                            kb.inc(inst, "te")
                    kb.wait("sy", t_ev, kb.tok("te"))
                    inst = kb.sy.dma_start(
                        dst[ec * 128:(ec + 1) * 128, :],
                        qko[:, buf * S:(buf + 1) * S])
                    tk = kb.inc(inst, "dsp%d" % buf, 16)
                    war_qko[buf] = (tk,)
                if which == 0:
                    t_q_spill = (kb.tok("dsp0"), kb.tok("dsp1"))
                else:
                    t_k_spill = (kb.tok("dsp0"), kb.tok("dsp1"))
            kb.wait("ve", kb.tok("te"))
            inst = kb.ve.tensor_copy(rd_all[0:16, :], ps_ctx[0:16, :])
            t_rd = kb.inc(inst, "ve")

            for tc in range(NTC):
                t_mm = None
                for nh in range(2):
                    kb.wait("te", t_xat, *w_tok, t_vones, t_rd, kb.tok("ve"))
                    for cc in range(NCH):
                        inst = kb.te.matmul(
                            ps_s1[:, nh * 512:(nh + 1) * 512],
                            xat[:, cc * S + tc * 128:cc * S + tc * 128 + 128],
                            wvs[:, cc * D + nh * 512:cc * D + (nh + 1) * 512],
                            start=(cc == 0), stop=False)
                    inst = kb.te.matmul(
                        ps_s1[:, nh * 512:(nh + 1) * 512],
                        cm["ones_bf"][0:1, 0:128],
                        w["bv_row"][0:1, nh * 512:(nh + 1) * 512],
                        start=False, stop=True)
                    t_mm = kb.inc(inst, "te")
                kb.wait("ve", t_mm)
                inst = kb.ve.tensor_copy(
                    ap(v_aug, tc * 1040, [[16640, 128], [65, 16], [1, 64]]),
                    ap(ps_s1, 0, [[1024, 128], [64, 16], [1, 64]]))
                t_vev = kb.inc(inst, "ve")
                kb.wait("te", t_vev)
        kb.phase_sync()

        with ExitStack() as pd:
            sbd = lambda n, shp, dt: pd.enter_context(nc.sbuf_tensor(n, shp, dt))
            stg = sbd("aa_stg", [128, 16 * TW], BF16)
            shn = sbd("aa_shn", [128, 32 * SHW], BF16)
            qh = [sbd("aa_qh%d" % i, [64, S], BF16) for i in range(2)]
            kh = [sbd("aa_kh%d" % i, [64, S], BF16) for i in range(2)]
            pkh = [sbd("aa_pkh%d" % i, [64, TW], BF16) for i in range(2)]
            pqh = [sbd("aa_pqh%d" % i, [64, TW], BF16) for i in range(2)]
            P = [sbd("aa_p%d" % i, [128, 1024], BF16) for i in range(2)]
            kdsat = sbd("aa_kdsat", [128, 32], F32)
            rd0 = sbd("aa_rd0", [1, 2 * S], BF16)
            zrow = sbd("aa_zrow", [1, S], F32)
            zrec = zrow
            c0 = sbd("aa_c0", [128, 2], F32)

            hld = {}

            def head_load(h):
                if h >= H:
                    return
                bb = h % 2
                kb.wait("sy", *hld.get(h - 2, {}).get("war", ()),
                        *t_q_spill, *t_k_spill)
                toks = []
                for dstt, srct in ((qh[bb], scr["q_s"]), (kh[bb], scr["k_s"])):
                    inst = kb.sy.dma_start(dstt[0:64, :],
                                           srct[h * 64:(h + 1) * 64, :])
                    toks.append(kb.inc(inst, "dt%d" % bb, 16))
                kb.wait("sy", scr["tok_tab"])
                inst = kb.sy.dma_start(pkh[bb][0:64, :],
                                       scr["pkrev_s"][h * 64:(h + 1) * 64, :])
                toks.append(kb.inc(inst, "dt%d" % bb, 16))
                inst = kb.sy.dma_start(pqh[bb][0:64, :],
                                       scr["pq_s"][h * 64:(h + 1) * 64, :])
                toks.append(kb.inc(inst, "dt%d" % bb, 16))
                kb.wait("sy", t_rd)
                inst = kb.sy.dma_start(rd0[0:1, bb * S:(bb + 1) * S],
                                       rd_all[h:h + 1, :])
                toks.append(kb.inc(inst, "dt%d" % bb, 16))
                hld[h] = {"tok": (toks[-1],), "war": ()}

            head_load(0)
            war_stg = ()
            war_shn = ()
            war_tab = [[(), ()], [(), ()]]
            war_P = [(), ()]
            war_ctx = ()
            war_ps = ()
            war_kds = ()
            war_ctxn = ()
            t_sh = None

            for h in range(H):
                b = h % 2
                head_load(h + 1)
                tq = kb.tok("dt%d" % b)

                pieces = ((0, 512, ps_s0, 0), (512, 512, ps_s0, 512),
                          (1024, 256, ps_s1, 0))
                war_pc = [tuple(war_ps), tuple(war_ps), tuple(war_ps)]
                rtoks = [[], []]
                for kind in range(2):
                    srct = qh[b] if kind == 0 else kh[b]
                    tab = pkh[b] if kind == 0 else pqh[b]
                    ev = "ve" if kind == 0 else "se"
                    eng = kb.ve if kind == 0 else kb.se
                    t_ev = None
                    ctoks = []
                    for i in range(NTC):
                        tc0 = None
                        for pidx, (p0, pn, pst, off) in enumerate(pieces):
                            kb.wait("te", tq, *war_stg, *war_pc[pidx])
                            inst = kb.te.matmul(
                                pst[:, off:off + pn],
                                srct[:, i * 128:(i + 1) * 128],
                                tab[:, p0:p0 + pn],
                                start=True, stop=True, skip_group_check=True)
                            t_mm = kb.inc(inst, "te")
                            kb.wait(ev, t_mm, *war_stg,
                                    *(war_kds if kind == 1 else ()))
                            dst_sl = stg[:, i * TW + p0:i * TW + p0 + pn]
                            if kind == 0:
                                if pidx == 0:
                                    inst = eng.tensor_copy(c0[:, 0:1],
                                                           ps_s0[:, 0:1])
                                    tc0 = kb.inc(inst, ev)
                                    inst = eng.tensor_scalar(
                                        dst_sl, pst[:, off:off + pn],
                                        ps_s0[:, 0:1], None, ALU.subtract)
                                else:
                                    kb.wait(ev, tc0)
                                    inst = eng.tensor_scalar(
                                        dst_sl, pst[:, off:off + pn],
                                        c0[:, 0:1], None, ALU.subtract)
                            else:
                                inst = eng.activation(dst_sl,
                                                      pst[:, off:off + pn],
                                                      AF.Identity)
                                if pidx == 0:
                                    inst = eng.activation(
                                        kdsat[:, 2 * i:2 * i + 1],
                                        ps_s0[:, 127:128], AF.Identity,
                                        bias=cm["negshift"][:, 0:1])
                                elif pidx == 2:
                                    inst = eng.activation(
                                        kdsat[:, 2 * i + 1:2 * i + 2],
                                        ps_s1[:, 129:130], AF.Identity,
                                        bias=cm["negshift"][:, 0:1])
                            t_ev = kb.inc(inst, ev)
                            war_pc[pidx] = (t_ev,)
                        if i % 4 == 3:
                            ctoks.append(t_ev)
                    tb = scr["tab_s"][kind][h % 2]
                    off0 = 127 if kind == 0 else 128
                    for c4 in range(4):
                        kb.wait("sy", ctoks[c4],
                                *(war_tab[kind][h % 2] if c4 == 0 else ()))
                        inst = kb.sy.dma_start(
                            ap(tb, c4 * 4 * TW, [[16 * TW, 128], [1, 4 * TW]]),
                            ap(stg, c4 * 4 * TW,
                               [[16 * TW, 128], [1, 4 * TW]]))
                        t_st = kb.inc(inst, "dtb", 16)
                        kb.wait("sy", t_st, *(war_shn if c4 == 0 else ()))
                        inst = kb.sy.dma_start(
                            ap(shn, kind * 16 * SHW + c4 * 4 * SHW,
                               [[32 * SHW, 128], [1, 4 * SHW]]),
                            ap(tb, off0 + c4 * 4 * TW,
                               [[16 * TW - 1, 128], [TW, 4], [1, SHW]]))
                        t_sh = kb.inc(inst, "dsh", 16)
                        rtoks[kind].append(t_sh)
                    war_stg = (t_st,)
                    war_tab[kind][h % 2] = (t_sh,)
                    if kind == 1:
                        t_kdse = t_ev

                no_c2p = bool(os.environ.get("AA_NO_C2P"))
                no_p2c = bool(os.environ.get("AA_NO_P2C"))
                war_half = [war_pc[0] + war_pc[1], tuple(war_pc[2])]
                prev = None
                it = 0
                for j in range(NTC):
                    lo_near = 128 * max(0, j - 4)
                    hi_near = 128 * min(NTC, j + 5)
                    for half, pss in ((0, ps_s0), (1, ps_s1)):
                        ip = it % 2
                        s0 = half * 1024
                        fr_end = min(max(lo_near, s0), s0 + 1024)
                        nr_end = min(max(hi_near, s0), s0 + 1024)
                        kb.wait("te", tq, t_rd, *war_half[half],
                                rtoks[1][min(3, j // 4)])
                        for nf in range(2):
                            inst = kb.te.matmul(
                                pss[:, nf * 512:(nf + 1) * 512],
                                kh[b][:, j * 128:(j + 1) * 128],
                                qh[b][:, s0 + nf * 512:s0 + (nf + 1) * 512],
                                start=True, stop=True, skip_group_check=True)
                        if fr_end < nr_end:
                            z0 = (fr_end - s0) + (s0 - 128 * j) + 512
                            pos = fr_end - s0
                            wdt = nr_end - fr_end
                            while (not no_p2c) and wdt > 0:
                                wn = min(512 - (pos % 512), wdt)
                                inst = kb.te.matmul(
                                    pss[:, pos:pos + wn],
                                    cm["id_bf"][:, :],
                                    ap(shn, (16 + j) * SHW + z0,
                                       [[32 * SHW, 128], [1, wn]]),
                                    start=False, stop=True,
                                    skip_group_check=True)
                                z0 += wn; pos += wn; wdt -= wn
                            for i in (() if no_c2p else
                                      range(max(0, j - 4), min(NTC, j + 5))):
                                si = 128 * i
                                if si < s0 or si >= s0 + 1024:
                                    continue
                                t = i - j
                                inst = kb.te.matmul(
                                    pss[:, si - s0:si - s0 + 128],
                                    ap(shn, i * SHW + 128 * (4 - t),
                                       [[32 * SHW, 128], [1, 128]]),
                                    cm["id_bf"][:, :],
                                    start=False, stop=True,
                                    skip_group_check=True)
                        if s0 < fr_end and not no_c2p:
                            pos = 0
                            wdt = fr_end - s0
                            while wdt > 0:
                                wn = min(512 - (pos % 512), wdt)
                                inst = kb.te.matmul(
                                    pss[:, pos:pos + wn],
                                    cm["ones_bf"][0:1, 0:128],
                                    rd0[0:1, b * S + s0 + pos:
                                        b * S + s0 + pos + wn],
                                    start=False, stop=True,
                                    skip_group_check=True)
                                pos += wn; wdt -= wn
                        t_bias = kb.inc(inst, "te")
                        kb.wait("se", t_bias, t_kdse, *war_P[ip])
                        if s0 < fr_end:
                            inst = kb.se.activation(
                                P[ip][:, 0:fr_end - s0],
                                pss[:, 0:fr_end - s0], AF.Exp,
                                bias=(cm["negshift"][:, 0:1] if no_p2c else
                                      kdsat[:, 2 * j:2 * j + 1]))
                        if fr_end < nr_end:
                            inst = kb.se.activation(
                                P[ip][:, fr_end - s0:nr_end - s0],
                                pss[:, fr_end - s0:nr_end - s0],
                                AF.Exp, bias=cm["negshift"][:, 0:1])
                        if nr_end < s0 + 1024:
                            inst = kb.se.activation(
                                P[ip][:, nr_end - s0:1024],
                                pss[:, nr_end - s0:1024],
                                AF.Exp,
                                bias=(cm["negshift"][:, 0:1] if no_p2c else
                                      kdsat[:, 2 * j + 1:2 * j + 2]))
                        t_exp = kb.inc(inst, "se")
                        war_half[half] = (t_exp,)
                        if prev is not None:
                            pj, pip, ps0, pt_exp = prev
                            kb.wait("te", pt_exp,
                                    *((t_vev,) + war_ctx if pj == 0 else ()))
                            for nf in range(2):
                                inst = kb.te.matmul(
                                    ps_ctx[0:65, ps0 + nf * 512:
                                           ps0 + (nf + 1) * 512],
                                    v_aug[:, pj * 1040 + h * 65:
                                          pj * 1040 + (h + 1) * 65],
                                    P[pip][:, nf * 512:(nf + 1) * 512],
                                    start=(pj == 0), stop=(pj == NTC - 1),
                                    skip_group_check=True)
                            t_pv = kb.inc(inst, "te")
                            war_P[pip] = (t_pv,)
                        prev = (j, ip, s0, t_exp)
                        it += 1
                pj, pip, ps0, pt_exp = prev
                kb.wait("te", pt_exp)
                for nf in range(2):
                    inst = kb.te.matmul(
                        ps_ctx[0:65, ps0 + nf * 512:ps0 + (nf + 1) * 512],
                        v_aug[:, pj * 1040 + h * 65:pj * 1040 + (h + 1) * 65],
                        P[pip][:, nf * 512:(nf + 1) * 512],
                        start=(pj == 0), stop=(pj == NTC - 1),
                        skip_group_check=True)
                t_pv = kb.inc(inst, "te")
                war_P[pip] = (t_pv,)

                kb.wait("se", t_pv)
                inst = kb.se.activation(zrow[0:1, :], ps_ctx[64:65, :], AF.Ln)
                t_z = kb.inc(inst, "se")
                kb.wait("se", t_z)
                inst = kb.se.activation(zrec[0:1, :], zrow[0:1, :], AF.Exp,
                                        scale=-1.0)
                t_z = kb.inc(inst, "se")
                war_kds = (t_z,)
                kb.wait("te", t_z)
                for nf in range(4):
                    pst = ps_s0 if nf < 2 else ps_s1
                    inst = kb.te.matmul(
                        pst[0:64, (nf % 2) * 512:(nf % 2 + 1) * 512],
                        cm["ones_f32"][0:1, 0:64],
                        zrec[0:1, nf * 512:(nf + 1) * 512],
                        start=True, stop=True, skip_group_check=True)
                t_bc = kb.inc(inst, "te")
                ctxn_b = stg[0:64, 16 * TW - 2048:16 * TW]
                bcast_b = stg[0:64, 16 * TW - 4096:16 * TW - 2048]
                kb.wait("ve", t_bc, *war_stg, *war_ctxn)
                inst = kb.ve.tensor_copy(bcast_b[:, 0:1024], ps_s0[0:64, :])
                inst = kb.ve.tensor_copy(bcast_b[:, 1024:2048], ps_s1[0:64, :])
                tcp = kb.inc(inst, "ve")
                kb.wait("ve", tcp)
                inst = kb.ve.tensor_tensor(ctxn_b[:, 0:1024],
                                           ps_ctx[0:64, 0:1024],
                                           bcast_b[:, 0:1024], ALU.mult)
                inst = kb.ve.tensor_tensor(ctxn_b[:, 1024:2048],
                                           ps_ctx[0:64, 1024:2048],
                                           bcast_b[:, 1024:2048], ALU.mult)
                t_cn = kb.inc(inst, "ve")
                war_ctx = (t_cn,)
                war_ps = war_half[0] + war_half[1] + (t_cn,)
                kb.wait("sy", t_cn)
                inst = kb.sy.dma_start(scr["ctxn_s"][h * 64:(h + 1) * 64, :],
                                       ctxn_b[:, :])
                tk = kb.inc(inst, "dcx", 16)
                war_ctxn = (tk,)
                war_stg = war_stg + (tk,)
                hld[h]["war"] = (t_cn,)
                war_shn = (kb.tok("te"),)
        kb.phase_sync()

        with ExitStack() as p6:
            ctf = p6.enter_context(nc.sbuf_tensor("aa_ctf", [128, NCH * S], BF16))
            hb2 = p6.enter_context(nc.sbuf_tensor("aa_hb2", [128, 4 * D], F32))
            kbsb, kb.sb = kb.sb, (lambda n, shp, dt:
                                  p6.enter_context(nc.sbuf_tensor(n, shp, dt)))
            wos, tw_o = load_w_bf16(kb, "aa_wos", w["wo"])
            kb.sb = kbsb
            t_cld = None
            for cc in range(NCH):
                kb.wait("sy", kb.tok("dcx"))
                inst = kb.sy.dma_start(
                    ap(ctf, cc * S, [[NCH * S, 128], [1, S]]),
                    scr["ctxn_s"][cc * 128:(cc + 1) * 128, :])
                t_cld = kb.inc(inst, "dct", 16)

            ld = {}

            def issue_load6(bk):
                if bk >= NTC // 2:
                    return
                kb.wait("sy", kb.tok("dho"))
                for i in range(2):
                    tc = bk * 2 + i
                    inst = kb.sy.dma_start(
                        hb2[:, ((bk % 2) * 2 + i) * D:((bk % 2) * 2 + i + 1) * D],
                        h_in[tc * 128:(tc + 1) * 128, :])
                    kb.inc(inst, "dhi%d" % (bk % 2), 16)

            issue_load6(0)
            for bk in range(NTC // 2):
                ldt = (kb.tok("dhi%d" % (bk % 2)),)
                issue_load6(bk + 1)
                for i in range(2):
                    tc = bk * 2 + i
                    for nh in range(2):
                        kb.wait("te", t_cld, tw_o, kb.tok("ve"))
                        for cc in range(NCH):
                            inst = kb.te.matmul(
                                ps_s0[:, nh * 512:(nh + 1) * 512],
                                ctf[:, cc * S + tc * 128:cc * S + tc * 128 + 128],
                                wos[:, cc * D + nh * 512:cc * D + (nh + 1) * 512],
                                start=(cc == 0), stop=False)
                        inst = kb.te.matmul(
                            ps_s0[:, nh * 512:(nh + 1) * 512],
                            cm["ones_bf"][0:1, 0:128],
                            w["bo_row"][0:1, nh * 512:(nh + 1) * 512],
                            start=False, stop=True)
                        t_mm = kb.inc(inst, "te")
                        kb.wait("ve", t_mm, *ldt)
                        hap = hb2[:, (((bk % 2) * 2 + i) * D + nh * 512):
                                  (((bk % 2) * 2 + i) * D + (nh + 1) * 512)]
                        inst = kb.ve.scalar_tensor_tensor(
                            hap, ps_s0[:, nh * 512:(nh + 1) * 512], 1.0, hap,
                            ALU.mult, ALU.add)
                        t_res = kb.inc(inst, "ve")
                        kb.wait("te", t_res)
                    kb.wait("sy", t_res)
                    inst = kb.sy.dma_start(h_out[tc * 128:(tc + 1) * 128, :],
                                           hb2[:, ((bk % 2) * 2 + i) * D:
                                               ((bk % 2) * 2 + i + 1) * D])
                    kb.inc(inst, "dho", 16)
        pass
    kb.phase_sync()
    return (kb.tok("dho"),)


def build_attn_a_setup(kb, cm, io, scr, st):
    nc = kb.nc
    lngp, _ = load_bias_part(kb, "lngp", io["lng"])
    lnbp, t_lnb = load_bias_part(kb, "lnbp", io["lnb"])
    lnbp_bf = kb.sb("lnbp_bf", [128, NCH], BF16)
    kb.wait("ve", t_lnb)
    inst = kb.ve.tensor_copy(lnbp_bf[:, :], lnbp[:, :])
    kb.inc(inst, "ve")
    bpkp, t_bpk = load_bias_part(kb, "bpkp", io["bpk"], scale=PK_SCALE)
    bpqp, t_bpq = load_bias_part(kb, "bpqp", io["bpq"], scale=PQ_SCALE)
    scr["dvecblk"] = kb.sb("dvecblk", [128, 128], BF16)
    inst = kb.gp.memset(scr["dvecblk"][:, :], 0.0)
    t_dv = kb.inc(inst, "gp")

    with ExitStack() as ss:
        sbs = lambda n, shp, dt: ss.enter_context(nc.sbuf_tensor(n, shp, dt))
        lnt = sbs("su_lnt", [128, NCH * TW], BF16)
        hbuf = sbs("su_hb", [128, D], F32)
        XH = sbs("su_xh", [128, D], F32)
        stage = sbs("su_stage", [128, D], F32)
        wsb = sbs("su_wsb", [128, NCH * D], BF16)
        wtmp = sbs("su_wtmp", [128, D], BF16)
        beff = sbs("su_beff", [128, NCH], F32)
        tb = sbs("su_tb", [128, TW], BF16)
        ps_a = ss.enter_context(nc.psum_tensor("su_psa", [128, 1024], F32))
        ps_b = ss.enter_context(nc.psum_tensor("su_psb", [128, 1024], F32))

        pieces = ((0, 512, ps_a, 0), (512, 512, ps_a, 512), (1024, 256, ps_b, 0))
        for kind, (rel, wp, bscaled, t_bs, scale, dst) in enumerate((
                (io["relrev"], io["wpk"], bpkp, t_bpk, PK_SCALE, scr["pkrev_s"]),
                (io["relfwd"], io["wpq"], bpqp, t_bpq, PQ_SCALE, scr["pq_s"]))):
            t_ev = None
            for rc in range(10):
                kb.wait("sy", kb.tok("te"), kb.tok("ve"), kb.tok("se"))
                inst = kb.sy.dma_start(hbuf[:, :], rel[rc * 128:(rc + 1) * 128, :])
                tk = kb.inc(inst, "dt0", 16)
                t_st = ln_stats_pair(kb, [hbuf[:, :]], st, dep=(tk,))
                t_x = ln_apply_one(kb, hbuf[:, :], XH[:, :], st, 0, 1, dep=t_st)
                for cq in range(2):
                    kb.wait("te", t_x, kb.tok("ve"))
                    for k4 in range(4):
                        c = cq * 4 + k4
                        inst = kb.te.matmul(
                            ps_a[:, k4 * 128:(k4 + 1) * 128],
                            XH[:, c * 128:(c + 1) * 128], cm["id_f32"][:, :],
                            start=(k4 == 0), stop=(k4 == 3), is_transpose=True)
                    t_te = kb.inc(inst, "te")
                    kb.wait("ve", t_te)
                    inst = kb.ve.tensor_copy(
                        ap(lnt, cq * 4 * TW + rc * 128,
                           [[NCH * TW, 128], [TW, 4], [1, 128]]),
                        ap(ps_a, 0, [[1024, 128], [128, 4], [1, 128]]))
                    t_ev = kb.inc(inst, "ve")
                    kb.wait("te", t_ev)
            if io.get("dbg") and kind == 0:
                kb.wait("sy", t_ev)
                for ci in range(NCH):
                    inst = kb.sy.dma_start(
                        io["dbg"]["lnt"][ci * 128:(ci + 1) * 128, :],
                        ap(lnt, ci * TW, [[NCH * TW, 128], [1, TW]]))
                    kb.inc(inst, "dct", 16)
            t_w = None
            for cc in range(NCH):
                kb.wait("sy", kb.tok("ve"), kb.tok("te"))
                inst = kb.sy.dma_start(stage[:, :], wp[cc * 128:(cc + 1) * 128, :])
                tk = kb.inc(inst, "dsl", 16)
                kb.wait("ve", tk)
                inst = kb.ve.tensor_scalar(wsb[:, cc * D:(cc + 1) * D], stage[:, :],
                                           lngp[:, cc:cc + 1], scale,
                                           ALU.mult, ALU.mult)
                kb.inc(inst, "ve")
                if cc > 0:
                    kb.wait("ve", t_bm)
                inst = kb.ve.tensor_scalar_mul(wtmp[:, :], stage[:, :], scale)
                t_w = kb.inc(inst, "ve")
                kb.wait("te", t_w)
                for ec in range(NCH):
                    inst = kb.te.matmul(
                        ps_b[:, 512 + ec:512 + ec + 1],
                        wtmp[:, ec * 128:(ec + 1) * 128],
                        lnbp_bf[:, cc:cc + 1],
                        start=(cc == 0 and ec == 0), stop=(cc == NCH - 1),
                        skip_group_check=True)
                t_bm = kb.inc(inst, "te")
            kb.wait("ve", t_bm, t_bs)
            inst = kb.ve.tensor_tensor(beff[:, :], ps_b[0:128, 512:512 + NCH],
                                       bscaled[:, :], ALU.add)
            t_be = kb.inc(inst, "ve")
            if io.get("dbg") and kind == 0:
                kb.wait("sy", t_be, t_w)
                inst = kb.sy.dma_start(io["dbg"]["beff"][:, :], beff[:, :])
                kb.inc(inst, "dct", 16)
                inst = kb.sy.dma_start(io["dbg"]["wsb"][:, :], wsb[:, :])
                kb.inc(inst, "dct", 16)
            t_tb = None
            for ec in range(NCH):
                kb.wait("te", t_ev, t_w, t_be, t_tb)
                for (p0, pn, pst, off) in pieces:
                    for cc in range(NCH):
                        inst = kb.te.matmul(
                            pst[:, off:off + pn],
                            wsb[:, cc * D + ec * 128:cc * D + ec * 128 + 128],
                            ap(lnt, cc * TW + p0, [[NCH * TW, 128], [1, pn]]),
                            start=(cc == 0), stop=(cc == NCH - 1),
                            skip_group_check=True)
                t_mm = kb.inc(inst, "te")
                kb.wait("se", t_mm, t_be, kb.tok("ds"))
                for (p0, pn, pst, off) in pieces:
                    inst = kb.se.activation(tb[:, p0:p0 + pn], pst[:, off:off + pn],
                                            AF.Identity, bias=beff[:, ec:ec + 1])
                t_tb = kb.inc(inst, "se")
                if kind == 0:
                    kb.wait("ve", t_tb, t_dv)
                    inst = kb.ve.tensor_tensor(stage[:, 0:1], tb[:, TW - 1:TW],
                                               tb[:, 0:1], ALU.subtract)
                    t_sub = kb.inc(inst, "ve")
                    kb.wait("ve", t_sub)
                    inst = kb.ve.tensor_copy(
                        scr["dvecblk"][0:64, ec * 16 + 2 * ec:ec * 16 + 2 * ec + 1],
                        stage[0:64, 0:1])
                    inst = kb.ve.tensor_copy(
                        scr["dvecblk"][64:128, ec * 16 + 2 * ec + 1:ec * 16 + 2 * ec + 2],
                        stage[64:128, 0:1])
                    t_dvw = kb.inc(inst, "ve")
                kb.wait("sy", t_tb, (t_dvw if kind == 0 else None))
                inst = kb.sy.dma_start(dst[ec * 128:(ec + 1) * 128, :], tb[:, :])
                tk = kb.inc(inst, "ds", 16)
                kb.wait("se", tk)
    kb.phase_sync()
    scr["tok_tab"] = kb.tok("ds")
    scr["tok_dvec"] = kb.tok("ve")
    return scr



def build_program():
    nc = bass.Bass("TRN2", target_bir_lowering=False,
                   detect_race_conditions=False)
    hb = {}
    hb["x"] = nc.dram_tensor("x", [SEQS_PER_CORE * S, D], F32, kind="ExternalInput")
    for nm in ("aa_wq", "aa_wk", "aa_wv", "aa_wo", "aa_wpk", "aa_wpq"):
        hb[nm] = nc.dram_tensor(nm, [D, D], F32, kind="ExternalInput")
    for nm in ("aa_bq", "aa_bk", "aa_bv", "aa_bo", "aa_bpk", "aa_bpq",
               "aa_lng", "aa_lnb", "f_mur", "f_muk",
               "b_bq1", "b_bk1", "b_bk2", "b_bproj"):
        hb[nm] = nc.dram_tensor(nm, [D], F32, kind="ExternalInput")
    for nm in ("aa_relfwd", "aa_relrev"):
        hb[nm] = nc.dram_tensor(nm, [TW, D], F32, kind="ExternalInput")
    for nm in ("f_wrt", "f_wkt", "f_wvt", "b_wq1t", "b_wk1t", "b_wk2t",
               "b_wprojt"):
        hb[nm] = nc.dram_tensor(nm, [D, D], F32, kind="ExternalInput")
    y = nc.dram_tensor("y", [SEQS_PER_CORE * S, D], F32, kind="ExternalOutput")
    h_a = nc.dram_tensor("h_a", [S, D], F32, kind="Internal")
    h_b = nc.dram_tensor("h_b", [S, D], F32, kind="Internal")
    scr = {
        "q_s": nc.dram_tensor("q_s", [D, S], BF16, kind="Internal"),
        "k_s": nc.dram_tensor("k_s", [D, S], BF16, kind="Internal"),
        "pkrev_s": nc.dram_tensor("pkrev_s", [D, TW], BF16, kind="Internal"),
        "pq_s": nc.dram_tensor("pq_s", [D, TW], BF16, kind="Internal"),
        "ctxn_s": nc.dram_tensor("ctxn_s", [D, S], BF16, kind="Internal"),
        "tab_s": [[nc.dram_tensor("tab_s%d_%d" % (k, p), [128, 16 * TW],
                                  BF16, kind="Internal") for p in range(2)]
                  for k in range(2)],
    }
    with ExitStack() as stack:
        kb = KB(nc, stack)
        cm = build_consts(kb)
        st = {"sq": kb.sb("ln_sq", [128, D], F32),
              "t": kb.sb("ln_t", [128, 24], F32)}
        io = {"relrev": hb["aa_relrev"], "relfwd": hb["aa_relfwd"],
              "wpk": hb["aa_wpk"], "wpq": hb["aa_wpq"], "bpk": hb["aa_bpk"],
              "bpq": hb["aa_bpq"], "lng": hb["aa_lng"], "lnb": hb["aa_lnb"]}
        build_attn_a_setup(kb, cm, io, scr, st)
        wa = {"wq": hb["aa_wq"], "wk": hb["aa_wk"], "wv": hb["aa_wv"],
              "wo": hb["aa_wo"], "bq": hb["aa_bq"], "bk": hb["aa_bk"],
              "bv": hb["aa_bv"], "bo": hb["aa_bo"]}
        wf = {"wrt": hb["f_wrt"], "wkt": hb["f_wkt"], "wvt": hb["f_wvt"],
              "mur": hb["f_mur"], "muk": hb["f_muk"]}
        wb = {"wq1t": hb["b_wq1t"], "wk1t": hb["b_wk1t"], "wk2t": hb["b_wk2t"],
              "wprojt": hb["b_wprojt"], "bq1": hb["b_bq1"], "bk1": hb["b_bk1"],
              "bk2": hb["b_bk2"], "bproj": hb["b_bproj"]}
        dep = None
        for sq in range(SEQS_PER_CORE):
            x_sl = hb["x"][sq * S:(sq + 1) * S, :]
            y_sl = y[sq * S:(sq + 1) * S, :]
            dep = build_attn_a(kb, cm, wa, scr, x_sl, h_a, st,
                               {"dep_h": dep})
            dep = build_ffn(kb, cm, wf, h_a, h_b, st, {"dep_h": dep})
            dep = build_attn_b(kb, cm, wb, h_b, h_a, st, {"dep_h": dep})
            dep = build_ffn(kb, cm, wf, h_a, y_sl, st, {"dep_h": dep})
        kb.wait("sy", kb.tok("dho"))
    return nc


_NC_CACHE = {}


def kernel(**inputs):
    g = lambda n: np.ascontiguousarray(inputs[n])
    t = lambda n: np.ascontiguousarray(np.transpose(inputs[n]))
    f = _bucket_f()
    rel = inputs["L0_attn_a_rel_emb"]
    relfwd = np.ascontiguousarray(rel[f])
    relrev = np.ascontiguousarray(rel[f[::-1]])
    common = {
        "aa_wq": g("L0_attn_a_Wq"), "aa_wk": g("L0_attn_a_Wk"),
        "aa_wv": g("L0_attn_a_Wv"), "aa_wo": g("L0_attn_a_Wo"),
        "aa_wpk": g("L0_attn_a_Wpk"), "aa_wpq": g("L0_attn_a_Wpq"),
        "aa_bq": g("L0_attn_a_bq"), "aa_bk": g("L0_attn_a_bk"),
        "aa_bv": g("L0_attn_a_bv"), "aa_bo": g("L0_attn_a_bo"),
        "aa_bpk": g("L0_attn_a_bpk"), "aa_bpq": g("L0_attn_a_bpq"),
        "aa_lng": g("L0_attn_a_ln_g"), "aa_lnb": g("L0_attn_a_ln_b"),
        "aa_relfwd": relfwd, "aa_relrev": relrev,
        "f_wrt": t("L0_ffn_W_r"), "f_wkt": t("L0_ffn_W_k"),
        "f_wvt": t("L0_ffn_W_v"),
        "f_mur": g("L0_ffn_mu_r"), "f_muk": g("L0_ffn_mu_k"),
        "b_wq1t": t("L0_attn_b_Wq1"), "b_wk1t": t("L0_attn_b_Wk1"),
        "b_wk2t": t("L0_attn_b_Wk2"), "b_wprojt": t("L0_attn_b_Wproj"),
        "b_bq1": g("L0_attn_b_bq1"), "b_bk1": g("L0_attn_b_bk1"),
        "b_bk2": g("L0_attn_b_bk2"), "b_bproj": g("L0_attn_b_bproj"),
    }
    xs = inputs["L0_attn_a_hidden_states"]
    B = xs.shape[0]
    per = B // NB_CORES
    if "nc" not in _NC_CACHE:
        _NC_CACHE["nc"] = build_program()
    nc = _NC_CACHE["nc"]
    in_maps = []
    for c in range(NB_CORES):
        m = dict(common)
        m["x"] = np.ascontiguousarray(
            xs[c * per:(c + 1) * per].reshape(per * S, D))
        in_maps.append(m)
    res = run_bass_kernel_spmd(nc, in_maps, core_ids=list(range(NB_CORES)))
    out = np.empty((B, S, D), dtype=np.float32)
    for c in range(NB_CORES):
        out[c * per:(c + 1) * per] = res.results[c]["y"].reshape(per, S, D)
    return out


# version 11 (final)
# speedup vs baseline: 2.8684x; 1.0155x over previous
import math
import os
from contextlib import ExitStack

import numpy as np
import ml_dtypes
import concourse.bass as bass
import concourse.mybir as mybir
from concourse.bass_utils import run_bass_kernel_spmd

F32 = mybir.dt.float32
BF16 = mybir.dt.bfloat16
AF = mybir.ActivationFunctionType
ALU = mybir.AluOpType
AXX = mybir.AxisListType.X

D = 1024
S = 2048
NB_CORES = 8
SEQS_PER_CORE = 4
H = 16
HD = 64
NCH = D // 128
NTC = S // 128
TW = 1280
SHW = 1152
BLK = 256
NBLK = S // BLK
FB = NCH * BLK

QK_SCALE = 1.0 / math.sqrt(HD * 3.0)
PK_SCALE = math.sqrt(1.5)
PQ_SCALE = 1.0 / math.sqrt(HD * 2.0)
K2_SCALE = 0.02 / math.sqrt(D)
Q2_SCALE = 0.02 / math.sqrt(D)
LN_EPS = 1e-5


def ap(t, off, dims):
    return bass.AP(t, off, dims)



def _bucket_f():
    mid, max_pos = 128, 512
    d = np.arange(-640, 640)
    sign = np.sign(d).astype(np.float32)
    abs_pos = np.where((d < mid) & (d > -mid), mid - 1, np.abs(d)).astype(np.float32)
    log_pos = np.ceil(
        np.log(abs_pos / np.float32(mid)) / np.log((max_pos - 1) / mid)
        * np.float32(mid - 1)).astype(np.float32) + mid
    buck = np.where(abs_pos <= mid, d.astype(np.float32), log_pos * sign)
    return np.clip(buck.astype(np.int64) + 256, 0, 511)



class KB:
    def __init__(self, nc, stack):
        self.nc = nc
        self.stack = stack
        self._un = [0]
        _osb, _ops = nc.sbuf_tensor, nc.psum_tensor
        def _usb(name, shape, dtype):
            self._un[0] += 1
            return _osb(f"{name}_u{self._un[0]}", shape, dtype)
        def _ups(name, shape, dtype):
            self._un[0] += 1
            return _ops(f"{name}_u{self._un[0]}", shape, dtype)
        nc.sbuf_tensor = _usb
        nc.psum_tensor = _ups
        self.sems = {}
        self.cnt = {}
        self.seen = {}
        for name in ("te", "ve", "se", "gp", "ds", "dg", "dq", "dsl",
                     "dhi0", "dhi1", "dho", "dsp0", "dsp1", "dt0", "dt1",
                     "dsh", "dcx", "dct", "dtb"):
            self.sems[name] = stack.enter_context(nc.semaphore(name))
            self.cnt[name] = 0
        self.engs = {"te": nc.tensor, "ve": nc.vector, "se": nc.scalar,
                     "gp": nc.gpsimd, "sy": nc.sync}

    def inc(self, inst, s, k=1):
        self.cnt[s] += k
        inst.then_inc(self.sems[s], k)
        return (s, self.cnt[s])

    def tok(self, s):
        return (s, self.cnt[s])

    def wait(self, eng, *toks):
        e = self.engs[eng]
        for t in toks:
            if not t:
                continue
            s, v = t
            if v <= 0:
                continue
            if self.seen.get((eng, s), 0) < v:
                e.wait_ge(self.sems[s], v)
                self.seen[(eng, s)] = v

    @property
    def te(self): return self.nc.tensor
    @property
    def ve(self): return self.nc.vector
    @property
    def se(self): return self.nc.scalar
    @property
    def gp(self): return self.nc.gpsimd
    @property
    def sy(self): return self.nc.sync

    def sb(self, name, shape, dtype):
        return self.stack.enter_context(self.nc.sbuf_tensor(name, shape, dtype))

    def phase_sync(self):
        toks = tuple(self.tok(s) for s in self.sems)
        for e in ("te", "ve", "se", "gp", "sy"):
            self.wait(e, *toks)

    def psum(self, name, shape):
        return self.stack.enter_context(self.nc.psum_tensor(name, shape, F32))


def build_consts(kb):
    cm = {}
    cm["id_bf"] = kb.sb("id_bf", [128, 128], BF16)
    cm["id_f32"] = kb.sb("id_f32", [128, 128], F32)
    cm["ones_bf"] = kb.sb("ones_bf", [128, 512], BF16)
    cm["ones_f32"] = kb.sb("ones_f32", [128, 512], F32)
    g = kb.gp
    for tgt in (cm["id_bf"], cm["id_f32"]):
        tk = kb.inc(g.memset(tgt[:, :], 1.0), "gp")
        kb.wait("gp", tk)
        tk = kb.inc(g.affine_select(
            out=tgt[:, :], in_=tgt[:, :], compare_op=ALU.is_ge,
            fill=0.0, base=0, pattern=[[-1, 128]], channel_multiplier=1), "gp")
        kb.wait("gp", tk)
        tk = kb.inc(g.affine_select(
            out=tgt[:, :], in_=tgt[:, :], compare_op=ALU.is_ge,
            fill=0.0, base=0, pattern=[[1, 128]], channel_multiplier=-1), "gp")
        kb.wait("gp", tk)
    g.memset(cm["ones_bf"][:, :], 1.0)
    g.memset(cm["ones_f32"][:, :], 1.0)
    cm["negshift"] = kb.sb("negshift", [128, 1], F32)
    inst = g.memset(cm["negshift"][:, :], -ASHIFT)
    cm["tok"] = kb.inc(inst, "gp")
    cm["actscr"] = kb.sb("actscr", [1, 8], F32)
    tk = kb.inc(kb.gp.memset(cm["actscr"][0:1, :], 0.5), "gp")
    kb.wait("se", tk)
    seen = set()
    from concourse.hw_specs import get_activation_tables
    tabs = list(get_activation_tables(kb.nc.m.arch).values())
    for f in (AF.Square, AF.Sqrt, AF.Exp, AF.Ln, AF.Sigmoid, AF.Copy,
              AF.Identity, AF.Relu):
        for idx, funcs in enumerate(tabs):
            if f in funcs:
                if idx not in seen:
                    seen.add(idx)
                    kb.se.activation(cm["actscr"][0:1, 0:1],
                                     cm["actscr"][0:1, 1:2], f)
                break
    kb.inc(kb.se.nop(), "se")
    return cm



def load_w_bf16(kb, name, hbm, scale=None, stg=None):
    t = kb.sb(name, [128, NCH * D], BF16)
    tok = None
    if scale is None:
        for c in range(NCH):
            inst = kb.gp.dma_start(t[:, c * D:(c + 1) * D],
                                   hbm[c * 128:(c + 1) * 128, :])
            tok = kb.inc(inst, "dg", 16)
    else:
        for c in range(NCH):
            kb.wait("sy", kb.tok("dq"))
            inst = kb.sy.dma_start(stg[:, :D], hbm[c * 128:(c + 1) * 128, :])
            dt_ = kb.inc(inst, "dsl", 16)
            kb.wait("ve", dt_)
            inst = kb.ve.tensor_scalar_mul(t[:, c * D:(c + 1) * D], stg[:, :D], scale)
            tok = kb.inc(inst, "ve")
            kb.inc(kb.ve.engine_nop(), "dq")
    return t, tok


def load_bias_part(kb, name, hbm, scale=None):
    t = kb.sb(name, [128, NCH], F32)
    if scale is None:
        inst = kb.sy.dma_start(ap(t, 0, [[NCH, 128], [1, NCH]]),
                               ap(hbm, 0, [[1, 128], [128, NCH]]),
                               allow_slow_non_contiguous=True)
        return t, kb.inc(inst, "ds", 16)
    kb.wait("sy", kb.tok("dq"))
    inst = kb.sy.dma_start(ap(t, 0, [[NCH, 128], [1, NCH]]),
                           ap(hbm, 0, [[1, 128], [128, NCH]]),
                           allow_slow_non_contiguous=True)
    tok = kb.inc(inst, "dsl", 16)
    kb.wait("ve", tok)
    inst = kb.ve.tensor_scalar_mul(t[:, :], t[:, :], scale)
    tok = kb.inc(inst, "ve")
    kb.inc(kb.ve.engine_nop(), "dq")
    return t, tok


def load_bias_row(kb, name, hbm):
    t = kb.sb(name, [1, D], BF16)
    inst = kb.gp.dma_start(t[0:1, :], hbm[None, :])
    return t, kb.inc(inst, "dg", 16)



def ln_stats_pair(kb, h_aps, st, dep=()):
    n = len(h_aps)
    assert n <= 8
    se, ve = kb.se, kb.ve
    sq, t = st["sq"], st["t"]
    t_se = t_rd = None
    for c, h_ap in enumerate(h_aps):
        kb.wait("se", *dep)
        inst = se.activation(sq[:, :], h_ap, AF.Square,
                             accum_out=t[:, 8 + c:8 + c + 1])
        t_se = kb.inc(inst, "se")
        kb.wait("ve", *dep)
        inst = ve.reduce_sum(t[:, c:c + 1], h_ap, axis=AXX)
        t_rd = kb.inc(inst, "ve")
    kb.wait("ve", t_se, t_rd)
    s1 = t[:, 0:n]
    s2 = t[:, 8:8 + n]
    scr = t[:, 16:16 + n]
    inst = ve.tensor_tensor(scr, s1, s1, ALU.mult)
    t1 = kb.inc(inst, "ve")
    inst = ve.tensor_scalar_mul(s1, s1, -1.0 / D)
    t_m = kb.inc(inst, "ve")
    kb.wait("ve", t1)
    inst = ve.scalar_tensor_tensor(s2, scr, -1.0 / D, s2, ALU.mult, ALU.add)
    t2 = kb.inc(inst, "ve")
    kb.wait("ve", t2)
    inst = ve.tensor_scalar(s2, s2, 1.0 / D, LN_EPS, ALU.mult, ALU.add)
    t3 = kb.inc(inst, "ve")
    kb.wait("ve", t3)
    inst = ve.reciprocal(s2, s2)
    t_ve = kb.inc(inst, "ve")
    kb.wait("se", t_ve)
    inst = se.activation(s2, s2, AF.Sqrt)
    t_rs = kb.inc(inst, "se")
    return (t_rs, t_m)


def ln_apply_one(kb, h_ap, out_ap, st, c, n, dep=()):
    t = st["t"]
    kb.wait("ve", *dep)
    inst = kb.ve.tensor_scalar(out_ap, h_ap, t[:, c:c + 1], t[:, 8 + c:8 + c + 1],
                               ALU.add, ALU.mult)
    return kb.inc(inst, "ve")



def build_attn_b(kb, cm, wh, h_in, h_out, st, bufs=None):
    nc = kb.nc
    kb.phase_sync()
    with ExitStack() as stk:
        sbl = lambda n, shp, dt: stk.enter_context(nc.sbuf_tensor(n, shp, dt))
        kbsb, kb.sb = kb.sb, sbl
        w = {}
        w["wq1t"], _ = load_w_bf16(kb, "abw_q1", wh["wq1t"])
        w["wk1t"], _ = load_w_bf16(kb, "abw_k1", wh["wk1t"])
        w["wk2t"], _ = load_w_bf16(kb, "abw_k2", wh["wk2t"])
        w["wprojt"], _ = load_w_bf16(kb, "abw_pj", wh["wprojt"])
        w["bq1p"], _ = load_bias_part(kb, "abb_q1", wh["bq1"])
        w["bk1p"], _ = load_bias_part(kb, "abb_k1", wh["bk1"])
        w["bk2p"], _ = load_bias_part(kb, "abb_k2", wh["bk2"], scale=K2_SCALE)
        w["bproj_row"], _ = load_bias_row(kb, "abb_pj", wh["bproj"])
        kb.sb = kbsb
        alltoks = tuple(kb.tok(s) for s in ("ds", "dg", "dsl", "ve"))
        for e in ("te", "ve", "se", "gp"):
            kb.wait(e, *alltoks)
        w["tok"] = alltoks[0]
        bufs = dict(bufs or {})
        bufs.setdefault("dep_h", None)
        bufs["xh32"] = sbl("ab_xh32", [128, D], F32)
        bufs["hb"] = sbl("ab_hb", [128, 2 * (BLK // 128) * D], F32)
        for nm, dt in (("ab_xb", BF16), ("ab_v", F32), ("ab_k", F32),
                       ("ab_q", F32), ("ab_k2", F32), ("ab_kv", F32),
                       ("ab_kc", F32), ("ab_sc", F32), ("ab_ss", F32),
                       ("ab_obf", BF16)):
            bufs[nm] = sbl(nm, [128, FB], dt)
        bufs["ab_car"] = sbl("ab_car", [128, 6 * NCH], F32)
        bufs["ps_a"] = stk.enter_context(nc.psum_tensor("ab_psa", [128, 512], F32))
        bufs["ps_a2"] = stk.enter_context(nc.psum_tensor("ab_psa2", [128, 512], F32))
        bufs["ps_b2"] = stk.enter_context(nc.psum_tensor("ab_psb2", [128, 512], F32))
        bufs["ps_b"] = stk.enter_context(nc.psum_tensor("ab_psb", [128, 512], F32))
        bufs["ps_c"] = stk.enter_context(nc.psum_tensor("ab_psc", [128, 1024], F32))
        ret = _attn_b_body(kb, cm, w, h_in, h_out, st, bufs)
    kb.phase_sync()
    return ret


def _attn_b_body(kb, cm, w, h_in, h_out, st, bufs):
    nc = kb.nc
    CPB = BLK // 128
    XH = bufs["xh32"]
    HB = bufs["hb"]
    XB = bufs["ab_xb"]
    V = bufs["ab_v"]
    K = bufs["ab_k"]
    Q = bufs["ab_q"]
    K2 = bufs["ab_k2"]
    KV = bufs["ab_kv"]
    KC = bufs["ab_kc"]
    SC = bufs["ab_sc"]
    SS = bufs["ab_ss"]
    OA = KC
    E = SS
    K2E = KV
    S2 = K
    O32 = SC
    OBF = bufs["ab_obf"]
    CAR = bufs["ab_car"]
    ps_t = bufs["ps_a"]
    PTs = [bufs["ps_a"], bufs.get("ps_a2", bufs["ps_a"])]
    ps_p = bufs["ps_b"]
    PPs = [bufs["ps_b"], bufs.get("ps_b2", bufs["ps_b"])]
    war_pt = [(), ()]
    war_pp = [(), ()]
    ps_o = bufs["ps_c"]

    cKC = lambda c: CAR[:, 0 * NCH + c:0 * NCH + c + 1]
    cSC = lambda c: CAR[:, 1 * NCH + c:1 * NCH + c + 1]
    cS2 = lambda c: CAR[:, 2 * NCH + c:2 * NCH + c + 1]
    cOA = CAR[:, 3 * NCH:4 * NCH]
    cK2 = CAR[:, 4 * NCH:5 * NCH]
    cQ = CAR[:, 5 * NCH:6 * NCH]

    def chs(t3, lo, hi):
        return ap(t3, lo, [[FB, 128], [BLK, NCH], [1, hi - lo]])

    col0 = lambda t3: ap(t3, 0, [[FB, 128], [BLK, NCH], [1, 1]])

    def hb_chunk(b, i):
        base = (b % 2) * CPB * D
        return HB[:, base + i * D: base + (i + 1) * D]

    ld_tok = {}
    out_tok = {}

    def issue_load(b):
        if b >= NBLK:
            return
        kb.wait("sy", *(bufs.get("dep_h") or ()), kb.tok("dho"))
        s = "dhi%d" % (b % 2)
        for i in range(CPB):
            tc = b * CPB + i
            inst = kb.sy.dma_start(hb_chunk(b, i),
                                   h_in[tc * 128:(tc + 1) * 128, :])
            tk = kb.inc(inst, s, 16)
        ld_tok[b] = (tk,)

    no_prefetch = bool(os.environ.get("AB_NO_PREFETCH"))
    barrier = bool(os.environ.get("AB_BARRIER"))
    issue_load(0)

    war = {k: () for k in ("XH", "XB", "V", "Q", "K2", "P1", "P2", "P3", "P4",
                           "P5", "OBF", "ST")}

    for b in range(NBLK):
        tcs = [b * CPB + i for i in range(CPB)]

        t_st = ln_stats_pair(kb, [hb_chunk(b, i) for i in range(CPB)], st,
                             dep=(kb.tok("dhi%d" % (b % 2)),) + war["ST"])
        if not no_prefetch:
            issue_load(b + 1)
        t_ev = []
        pend = None

        def xb_evac(pcq, pti, ppar, pte):
            kb.wait("ve", pte, *war["V"], *war["XB"])
            inst = kb.ve.tensor_copy(
                ap(V, pcq * 4 * BLK + pti * 128,
                   [[FB, 128], [BLK, 4], [1, 128]]),
                ap(PTs[ppar], 0, [[512, 128], [128, 4], [1, 128]]))
            kb.inc(inst, "ve")
            inst = kb.ve.tensor_copy(
                ap(XB, pcq * 4 * BLK + pti * 128,
                   [[FB, 128], [BLK, 4], [1, 128]]),
                ap(PTs[ppar], 0, [[512, 128], [128, 4], [1, 128]]))
            t = kb.inc(inst, "ve")
            t_ev.append(t)
            war_pt[ppar] = (t,)

        for ti in range(CPB):
            t_x = ln_apply_one(kb, hb_chunk(b, ti), XH[:, :], st, ti, CPB,
                               dep=t_st + war["XH"])
            for cq in range(2):
                par = (ti * 2 + cq) % 2
                kb.wait("te", t_x, *war_pt[par])
                for k4 in range(4):
                    c = cq * 4 + k4
                    inst = kb.te.matmul(
                        PTs[par][:, k4 * 128:(k4 + 1) * 128],
                        XH[:, c * 128:(c + 1) * 128], cm["id_f32"][:, :],
                        start=(k4 == 0), stop=(k4 == 3), is_transpose=True)
                t_te = kb.inc(inst, "te")
                if pend is not None:
                    xb_evac(*pend)
                pend = (cq, ti, par, t_te)
            war["XH"] = (kb.tok("te"),)
        xb_evac(*pend)
        t_xb = t_ev[-1]

        def proj(dst, wt, biasp, func, scale, extra=()):
            tk = None
            for ec in range(NCH):
                par = ec % 2
                kb.wait("te", t_xb, w["tok"], *extra, *war_pp[par])
                for cc in range(NCH):
                    inst = kb.te.matmul(
                        PPs[par][:, 0:BLK],
                        wt[:, cc * D + ec * 128:cc * D + ec * 128 + 128],
                        XB[:, cc * BLK:(cc + 1) * BLK],
                        start=(cc == 0), stop=(cc == NCH - 1))
                t_mm = kb.inc(inst, "te")
                kb.wait("se", t_mm)
                inst = kb.se.activation(dst[:, ec * BLK:(ec + 1) * BLK],
                                        PPs[par][:, 0:BLK], func,
                                        bias=biasp[:, ec:ec + 1], scale=scale)
                tk = kb.inc(inst, "se")
                war_pp[par] = (tk,)
            return tk

        t_K = proj(K, w["wk1t"], w["bk1p"], AF.Exp, 1.0, extra=war["P4"])
        kb.wait("ve", t_K, *war["P5"])
        for c in range(NCH):
            init = 1e-6 if b == 0 else cKC(c)
            inst = kb.ve.tensor_tensor_scan(
                KC[:, c * BLK:(c + 1) * BLK], cm["ones_f32"][:, 0:BLK],
                K[:, c * BLK:(c + 1) * BLK], init, ALU.mult, ALU.add)
        t_kc = kb.inc(inst, "ve")
        kb.wait("ve", t_kc)
        inst = kb.ve.tensor_copy(CAR[:, 0:NCH], chs(KC, BLK - 1, BLK))
        kb.inc(inst, "ve")
        kb.wait("gp", t_K, t_ev[-1], *war["P1"])
        inst = kb.gp.tensor_mul(KV[:, :], K[:, :], V[:, :])
        t_kv = kb.inc(inst, "gp")
        kb.wait("se", t_kc, t_kv)
        inst = kb.se.activation(K[:, :], KC[:, :], AF.Ln)
        t_ln = kb.inc(inst, "se")
        kb.wait("se", t_ln)
        inst = kb.se.activation(K[:, :], K[:, :], AF.Exp, scale=-1.0)
        t_rc = kb.inc(inst, "se")
        kb.wait("ve", t_kv, *war["P2"])
        for c in range(NCH):
            init = 0.0 if b == 0 else cSC(c)
            inst = kb.ve.tensor_tensor_scan(
                SC[:, c * BLK:(c + 1) * BLK], cm["ones_f32"][:, 0:BLK],
                KV[:, c * BLK:(c + 1) * BLK], init, ALU.mult, ALU.add)
        t_sc = kb.inc(inst, "ve")
        kb.wait("ve", t_sc)
        inst = kb.ve.tensor_copy(CAR[:, NCH:2 * NCH], chs(SC, BLK - 1, BLK))
        kb.inc(inst, "ve")
        kb.wait("gp", t_sc, t_rc, *war["P3"])
        inst = kb.gp.tensor_mul(SS[:, :], SC[:, :], K[:, :])
        t_ss = kb.inc(inst, "gp")
        t_Q = proj(Q, w["wq1t"], w["bq1p"], AF.Sigmoid, 1.0, extra=war["Q"])
        kb.wait("gp", t_ss, t_Q)
        inst = kb.gp.tensor_mul(OA[:, :], SS[:, :], Q[:, :])
        t_oa = kb.inc(inst, "gp")
        kb.wait("gp", t_oa)
        inst = kb.gp.tensor_tensor(chs(E, 1, BLK), chs(V, 1, BLK),
                                   chs(OA, 0, BLK - 1), ALU.subtract)
        kb.inc(inst, "gp")
        if b == 0:
            inst = kb.gp.memset(col0(E), 0.0)
        else:
            inst = kb.gp.tensor_tensor(
                col0(E), col0(V),
                ap(CAR, 3 * NCH, [[6 * NCH, 128], [1, NCH], [1, 1]]),
                ALU.subtract)
        t_e = kb.inc(inst, "gp")
        t_K2 = proj(K2, w["wk2t"], w["bk2p"], AF.Sigmoid, K2_SCALE,
                    extra=war["K2"])
        war["XB"] = (kb.tok("te"),)
        kb.wait("gp", t_e, t_K2, t_sc)
        inst = kb.gp.tensor_tensor(chs(K2E, 1, BLK), chs(K2, 0, BLK - 1),
                                   chs(E, 1, BLK), ALU.mult)
        kb.inc(inst, "gp")
        if b == 0:
            inst = kb.gp.memset(col0(K2E), 0.0)
        else:
            inst = kb.gp.tensor_tensor(
                col0(K2E),
                ap(CAR, 4 * NCH, [[6 * NCH, 128], [1, NCH], [1, 1]]),
                col0(E), ALU.mult)
        t_k2e = kb.inc(inst, "gp")
        war["V"] = (t_e,)
        kb.wait("ve", t_e)
        inst = kb.ve.tensor_copy(cOA, chs(OA, BLK - 1, BLK))
        kb.inc(inst, "ve")
        kb.wait("ve", t_k2e)
        inst = kb.ve.tensor_copy(cK2, chs(K2, BLK - 1, BLK))
        t_carK2 = kb.inc(inst, "ve")
        war["K2"] = (t_carK2, t_k2e)
        kb.wait("ve", t_k2e, t_ss)
        for c in range(NCH):
            init = 0.0 if b == 0 else cS2(c)
            inst = kb.ve.tensor_tensor_scan(
                S2[:, c * BLK:(c + 1) * BLK], cm["ones_f32"][:, 0:BLK],
                K2E[:, c * BLK:(c + 1) * BLK], init, ALU.mult, ALU.add)
        t_s2 = kb.inc(inst, "ve")
        war["P1"] = (t_s2,)
        kb.wait("ve", t_s2)
        inst = kb.ve.tensor_copy(CAR[:, 2 * NCH:3 * NCH], chs(S2, BLK - 1, BLK))
        kb.inc(inst, "ve")
        kb.wait("ve", t_s2, t_oa, t_ss)
        inst = kb.ve.scalar_tensor_tensor(
            chs(O32, 1, BLK), chs(Q, 0, BLK - 1), Q2_SCALE,
            chs(S2, 1, BLK), ALU.mult, ALU.mult)
        kb.inc(inst, "ve")
        if b == 0:
            inst = kb.ve.memset(col0(O32), 0.0)
        else:
            inst = kb.ve.scalar_tensor_tensor(
                col0(O32), ap(CAR, 5 * NCH, [[6 * NCH, 128], [1, NCH], [1, 1]]),
                Q2_SCALE, col0(S2), ALU.mult, ALU.mult)
        kb.inc(inst, "ve")
        inst = kb.ve.tensor_copy(cQ, chs(Q, BLK - 1, BLK))
        t_carQ = kb.inc(inst, "ve")
        war["Q"] = (t_carQ, t_oa)
        kb.wait("ve", *war["OBF"], kb.tok("ve"))
        inst = kb.ve.tensor_add(OBF[:, :], O32[:, :], OA[:, :])
        t_o = kb.inc(inst, "ve")
        war["P5"] = (t_o, t_e)
        war["P2"] = (t_o,)
        war["P3"] = (t_k2e,)
        war["P4"] = (t_o,)

        if b == 0 and bufs.get("dbg"):
            kb.wait("sy", t_o, t_e, t_kc, t_sc, t_ev[-1], t_K, t_Q, t_K2)
            for nm, tt in [("dbg_v", V), ("dbg_k", OA), ("dbg_q", Q),
                           ("dbg_kc", KC), ("dbg_sc", SC), ("dbg_e", E),
                           ("dbg_obf", OBF), ("dbg_xb", XB),
                           ("dbg_hb", HB), ("dbg_st", st["t"])]:
                if nm in bufs["dbg"]:
                    inst = kb.sy.dma_start(bufs["dbg"][nm][:, :], tt[:, :])
                    dtk = kb.inc(inst, "ds", 16)
            for e in ("te", "ve", "se", "gp", "sy"):
                kb.wait(e, dtk)
        otk = None
        for ti, tc in enumerate(tcs):
            for nh in range(2):
                kb.wait("te", t_o, w["tok"])
                for cc in range(NCH):
                    inst = kb.te.matmul(
                        ps_o[:, nh * 512:(nh + 1) * 512],
                        OBF[:, cc * BLK + ti * 128:cc * BLK + ti * 128 + 128],
                        w["wprojt"][:, cc * D + nh * 512:cc * D + (nh + 1) * 512],
                        start=(cc == 0), stop=False)
                inst = kb.te.matmul(
                    ps_o[:, nh * 512:(nh + 1) * 512],
                    cm["ones_bf"][0:1, 0:128],
                    w["bproj_row"][0:1, nh * 512:(nh + 1) * 512],
                    start=False, stop=True)
                t_mm = kb.inc(inst, "te")
                kb.wait("ve", t_mm)
                hap = hb_chunk(b, ti)[:, nh * 512:(nh + 1) * 512]
                inst = kb.ve.scalar_tensor_tensor(
                    hap, ps_o[:, nh * 512:(nh + 1) * 512], 1.0, hap,
                    ALU.mult, ALU.add)
                t_res = kb.inc(inst, "ve")
                kb.wait("te", t_res)
            kb.wait("sy", t_res)
            inst = kb.sy.dma_start(h_out[tc * 128:(tc + 1) * 128, :],
                                   hb_chunk(b, ti))
            otk = kb.inc(inst, "dho", 16)
        out_tok[b] = (otk,)
        war["OBF"] = (kb.tok("te"),)
        war["ST"] = (kb.tok("ve"),)
        if no_prefetch:
            issue_load(b + 1)
        if barrier:
            for e in ("te", "ve", "se", "gp"):
                kb.inc(kb.engs[e].engine_nop() if e in ("ve", "gp")
                       else (kb.te.nop() if e == "te" else kb.se.nop()), e)
            toks = tuple(kb.tok(e) for e in ("te", "ve", "se", "gp"))
            for e in ("te", "ve", "se", "gp"):
                kb.wait(e, *toks)

    bufs["dep_h"] = (kb.tok("dho"),)
    return bufs["dep_h"]



def build_ffn(kb, cm, wh, h_in, h_out, st, bufs=None):
    nc = kb.nc
    kb.phase_sync()
    with ExitStack() as stk:
        sbl = lambda n, shp, dt: stk.enter_context(nc.sbuf_tensor(n, shp, dt))
        kbsb, kb.sb = kb.sb, sbl
        w = {}
        w["wrt"], _ = load_w_bf16(kb, "fw_r", wh["wrt"])
        w["wkt"], _ = load_w_bf16(kb, "fw_k", wh["wkt"])
        w["wvt"], _ = load_w_bf16(kb, "fw_v", wh["wvt"])
        w["murp"], _ = load_bias_part(kb, "fb_mr", wh["mur"])
        w["mukp"], _ = load_bias_part(kb, "fb_mk", wh["muk"])
        kb.sb = kbsb
        alltoks = tuple(kb.tok(s) for s in ("ds", "dg", "dsl", "ve"))
        for e in ("te", "ve", "se", "gp"):
            kb.wait(e, *alltoks)
        w["tok"] = alltoks[0]
        bufs = dict(bufs or {})
        bufs.setdefault("dep_h", None)
        bufs["xh32"] = sbl("f_xh32", [128, D], F32)
        bufs["xh32b"] = sbl("f_xh32b", [128, D], F32)
        bufs["hb"] = sbl("f_hb", [128, 2 * (BLK // 128) * D], F32)
        for nm, dt in (("ab_obf", BF16), ("ab_v", F32), ("ab_k", F32),
                       ("ab_kv", F32)):
            bufs[nm] = sbl("f" + nm, [128, FB], dt)
        bufs["ffn_xtp"] = sbl("f_xtp", [128, NCH * (BLK + 1)], BF16)
        bufs["ffn_xr"] = sbl("f_xr", [128, FB], BF16)
        bufs["ffn_xk"] = sbl("f_xk", [128, FB], BF16)
        bufs["ffn_ka2"] = sbl("f_ka2", [128, FB], BF16)
        bufs["ps_a"] = stk.enter_context(nc.psum_tensor("f_psa", [128, 512], F32))
        bufs["ps_a2"] = stk.enter_context(nc.psum_tensor("f_psa2", [128, 512], F32))
        bufs["ps_b2"] = stk.enter_context(nc.psum_tensor("f_psb2", [128, 512], F32))
        bufs["ps_b"] = stk.enter_context(nc.psum_tensor("f_psb", [128, 512], F32))
        bufs["ps_c"] = stk.enter_context(nc.psum_tensor("f_psc", [128, 1024], F32))
        ret = _ffn_body(kb, cm, w, h_in, h_out, st, bufs)
    kb.phase_sync()
    return ret


def _ffn_body(kb, cm, w, h_in, h_out, st, bufs):
    CPB = BLK // 128
    BP = BLK + 1
    FBP = NCH * BP
    XH = bufs["xh32"]
    HB = bufs["hb"]
    XTP = bufs["ffn_xtp"]
    DIF = bufs["ab_obf"]
    XR = bufs["ffn_xr"]
    XK = bufs["ffn_xk"]
    R = bufs["ab_v"]
    KA = bufs["ab_k"]
    KA2 = bufs["ffn_ka2"]
    TMP = bufs["ab_kv"]
    ps_t = bufs["ps_a"]
    XHs = [bufs["xh32"], bufs["xh32b"]]
    PTs = [bufs["ps_a"], bufs["ps_a2"]]
    ps_p = bufs["ps_b"]
    PPs = [bufs["ps_b"], bufs.get("ps_b2", bufs["ps_b"])]
    ps_o = bufs["ps_c"]

    def chp(lo, n):
        return ap(XTP, lo, [[FBP, 128], [BP, NCH], [1, n]])

    def chs(t3, lo, n):
        return ap(t3, lo, [[FB, 128], [BLK, NCH], [1, n]])

    def hb_chunk(b, i):
        base = (b % 2) * CPB * D
        return HB[:, base + i * D: base + (i + 1) * D]

    ld_tok = {}

    def issue_load(b):
        if b >= NBLK:
            return
        kb.wait("sy", *(bufs.get("dep_h") or ()), kb.tok("dho"))
        for i in range(CPB):
            tc = b * CPB + i
            inst = kb.sy.dma_start(hb_chunk(b, i),
                                   h_in[tc * 128:(tc + 1) * 128, :])
            tk = kb.inc(inst, "dhi%d" % (b % 2), 16)
        ld_tok[b] = (tk,)

    issue_load(0)
    war = {k: () for k in ("DIF", "XR", "XK", "R", "KA", "KA2", "TMP", "ST")}
    war["XH"] = [(), ()]
    war["PT"] = [(), ()]
    war["PO"] = [(), ()]
    war["PP"] = [(), ()]
    t_carx = None
    inst = kb.gp.memset(chp(0, 1), 0.0)
    t_z = kb.inc(inst, "gp")
    war["XTP"] = (t_z,)

    for b in range(NBLK):
        t_st = ln_stats_pair(kb, [hb_chunk(b, i) for i in range(CPB)], st,
                             dep=(kb.tok("dhi%d" % (b % 2)),) + war["ST"])
        issue_load(b + 1)
        pend = None
        for ti in range(CPB):
            xh = XHs[ti % 2]
            t_x = ln_apply_one(kb, hb_chunk(b, ti), xh[:, :], st, ti, CPB,
                               dep=t_st + war["XH"][ti % 2])
            for cq in range(2):
                par = (ti * 2 + cq) % 2
                pt = PTs[par]
                kb.wait("te", t_x, *war["PT"][par])
                for k4 in range(4):
                    c = cq * 4 + k4
                    inst = kb.te.matmul(
                        pt[:, k4 * 128:(k4 + 1) * 128],
                        xh[:, c * 128:(c + 1) * 128], cm["id_f32"][:, :],
                        start=(k4 == 0), stop=(k4 == 3), is_transpose=True)
                t_te = kb.inc(inst, "te")
                if pend is not None:
                    (pcq, pti, ppar, pte) = pend
                    kb.wait("ve", pte, *war["XTP"])
                    inst = kb.ve.tensor_copy(
                        ap(XTP, pcq * 4 * BP + pti * 128 + 1,
                           [[FBP, 128], [BP, 4], [1, 128]]),
                        ap(PTs[ppar], 0, [[512, 128], [128, 4], [1, 128]]))
                    war["PT"][ppar] = (kb.inc(inst, "ve"),)
                pend = (cq, ti, par, t_te)
            war["XH"][ti % 2] = (kb.tok("te"),)
        (pcq, pti, ppar, pte) = pend
        kb.wait("ve", pte, *war["XTP"])
        inst = kb.ve.tensor_copy(
            ap(XTP, pcq * 4 * BP + pti * 128 + 1,
               [[FBP, 128], [BP, 4], [1, 128]]),
            ap(PTs[ppar], 0, [[512, 128], [128, 4], [1, 128]]))
        t_xt = kb.inc(inst, "ve")
        war["PT"][ppar] = (t_xt,)

        kb.wait("gp", t_xt, *war["DIF"])
        inst = kb.gp.tensor_tensor(chs(DIF, 0, BLK), chp(1, BLK),
                                   chp(0, BLK), ALU.subtract)
        t_dif = kb.inc(inst, "gp")
        for dst, mu, wk in ((XR, w["murp"], "XR"), (XK, w["mukp"], "XK")):
            kb.wait("ve", t_dif, *war[wk])
            for c in range(NCH):
                inst = kb.ve.scalar_tensor_tensor(
                    dst[:, c * BLK:(c + 1) * BLK],
                    DIF[:, c * BLK:(c + 1) * BLK],
                    mu[:, c:c + 1],
                    ap(XTP, c * BP, [[FBP, 128], [1, BLK]]),
                    ALU.mult, ALU.add)
            war[wk] = (kb.inc(inst, "ve"),)
        t_mix = war["XK"][0]
        kb.wait("gp", t_mix, t_dif)
        inst = kb.gp.tensor_copy(chp(0, 1), chp(BLK, 1))
        t_carx = kb.inc(inst, "gp")
        war["XTP"] = (t_carx, t_mix)
        war["DIF"] = (t_mix,)

        t_r = None
        for ti in range(CPB):
            for nh in range(2):
                kb.wait("te", war["XR"][0], w["tok"], *war["PO"][nh])
                for cc in range(NCH):
                    inst = kb.te.matmul(
                        ps_o[:, nh * 512:(nh + 1) * 512],
                        XR[:, cc * BLK + ti * 128:cc * BLK + ti * 128 + 128],
                        w["wrt"][:, cc * D + nh * 512:cc * D + (nh + 1) * 512],
                        start=(cc == 0), stop=(cc == NCH - 1))
                t_mm = kb.inc(inst, "te")
                kb.wait("se", t_mm, *war["R"])
                inst = kb.se.activation(
                    R[:, ti * D + nh * 512:ti * D + (nh + 1) * 512],
                    ps_o[:, nh * 512:(nh + 1) * 512], AF.Sigmoid)
                t_r = kb.inc(inst, "se")
                war["PO"][nh] = (t_r,)
        war["R"] = ()

        t_ka2 = None
        for ec in range(NCH):
            par = ec % 2
            kb.wait("te", t_mix, w["tok"], *war["PP"][par])
            for cc in range(NCH):
                inst = kb.te.matmul(
                    PPs[par][:, 0:BLK],
                    w["wkt"][:, cc * D + ec * 128:cc * D + ec * 128 + 128],
                    XK[:, cc * BLK:(cc + 1) * BLK],
                    start=(cc == 0), stop=(cc == NCH - 1))
            t_mm = kb.inc(inst, "te")
            kb.wait("ve", t_mm, *war["KA2"])
            inst = kb.ve.tensor_scalar_max(KA[:, ec * BLK:(ec + 1) * BLK],
                                           PPs[par][:, 0:BLK], 0.0)
            t_ka = kb.inc(inst, "ve")
            war["PP"][par] = (t_ka,)
            kb.wait("se", t_ka)
            inst = kb.se.activation(KA2[:, ec * BLK:(ec + 1) * BLK],
                                    KA[:, ec * BLK:(ec + 1) * BLK], AF.Square)
            t_ka2 = kb.inc(inst, "se")
        war["KA"] = (kb.tok("se"),)
        war["KA2"] = ()

        for ti, tc in enumerate(range(b * CPB, (b + 1) * CPB)):
            for nh in range(2):
                kb.wait("te", t_ka2, w["tok"], *war["PO"][nh])
                for cc in range(NCH):
                    inst = kb.te.matmul(
                        ps_o[:, nh * 512:(nh + 1) * 512],
                        KA2[:, cc * BLK + ti * 128:cc * BLK + ti * 128 + 128],
                        w["wvt"][:, cc * D + nh * 512:cc * D + (nh + 1) * 512],
                        start=(cc == 0), stop=(cc == NCH - 1))
                t_mm = kb.inc(inst, "te")
                kb.wait("ve", t_mm, t_r)
                inst = kb.ve.tensor_tensor(
                    TMP[:, ti * D + nh * 512:ti * D + (nh + 1) * 512],
                    ps_o[:, nh * 512:(nh + 1) * 512],
                    R[:, ti * D + nh * 512:ti * D + (nh + 1) * 512], ALU.mult)
                t_p = kb.inc(inst, "ve")
                war["PO"][nh] = (t_p,)
                kb.wait("ve", t_p)
                hap = hb_chunk(b, ti)[:, nh * 512:(nh + 1) * 512]
                inst = kb.ve.tensor_tensor(
                    hap, hap,
                    TMP[:, ti * D + nh * 512:ti * D + (nh + 1) * 512], ALU.add)
                t_res = kb.inc(inst, "ve")
            kb.wait("sy", t_res)
            inst = kb.sy.dma_start(h_out[tc * 128:(tc + 1) * 128, :],
                                   hb_chunk(b, ti))
            otk = kb.inc(inst, "dho", 16)
        war["TMP"] = (kb.tok("ve"),)
        war["ST"] = (kb.tok("ve"),)
        war["KA2"] = (kb.tok("te"),)
        war["R"] = (kb.tok("ve"),)
        war["XR"] = (kb.tok("te"),)
        war["XK"] = (kb.tok("te"),)

    bufs["dep_h"] = (kb.tok("dho"),)
    return bufs["dep_h"]



ASHIFT = 4.0


def aa_ln_transpose_pass(kb, cm, h_in, st, xat, XH, ps_t, hb, dep_h):
    CPB = 2
    war_xh = ()
    war_st = ()
    last_use = {0: (), 1: ()}

    def issue_load(b):
        if b >= NTC // CPB:
            return
        kb.wait("sy", *(dep_h or ()), kb.tok("dho"), *last_use[b % 2])
        for i in range(CPB):
            tc = b * CPB + i
            inst = kb.sy.dma_start(
                hb[:, ((b % 2) * CPB + i) * D:((b % 2) * CPB + i + 1) * D],
                h_in[tc * 128:(tc + 1) * 128, :])
            kb.inc(inst, "dhi%d" % (b % 2), 16)

    issue_load(0)
    t_ev = None
    for b in range(NTC // CPB):
        t_st = ln_stats_pair(
            kb, [hb[:, ((b % 2) * CPB + i) * D:((b % 2) * CPB + i + 1) * D]
                 for i in range(CPB)], st,
            dep=(kb.tok("dhi%d" % (b % 2)),) + war_st)
        for ti in range(CPB):
            tc = b * CPB + ti
            t_x = ln_apply_one(
                kb, hb[:, ((b % 2) * CPB + ti) * D:((b % 2) * CPB + ti + 1) * D],
                XH[:, :], st, ti, CPB, dep=t_st + war_xh)
            for cq in range(2):
                kb.wait("te", t_x, kb.tok("ve"))
                for k4 in range(4):
                    c = cq * 4 + k4
                    inst = kb.te.matmul(
                        ps_t[:, k4 * 128:(k4 + 1) * 128],
                        XH[:, c * 128:(c + 1) * 128], cm["id_f32"][:, :],
                        start=(k4 == 0), stop=(k4 == 3), is_transpose=True)
                t_te = kb.inc(inst, "te")
                kb.wait("ve", t_te)
                inst = kb.ve.tensor_copy(
                    ap(xat, (cq * 4) * S + tc * 128,
                       [[NCH * S, 128], [S, 4], [1, 128]]),
                    ap(ps_t, 0, [[1024, 128], [128, 4], [1, 128]]))
                t_ev = kb.inc(inst, "ve")
                kb.wait("te", t_ev)
            war_xh = (kb.tok("te"),)
        war_st = (kb.tok("ve"),)
        last_use[b % 2] = (kb.tok("ve"), kb.tok("se"))
        issue_load(b + 1)
    return t_ev


def build_attn_a(kb, cm, wh, scr, h_in, h_out, st, bufs=None):
    nc = kb.nc
    kb.phase_sync()
    with ExitStack() as pstk:
        kbsb, kb.sb = kb.sb, (lambda n, shp, dt:
                              pstk.enter_context(nc.sbuf_tensor(n, shp, dt)))
        w = dict(wh)
        w["bqp"], _ = load_bias_part(kb, "aab_q", wh["bq"], scale=QK_SCALE)
        w["bkp"], _ = load_bias_part(kb, "aab_k", wh["bk"])
        w["bv_row"], _ = load_bias_row(kb, "aab_v", wh["bv"])
        w["bo_row"], _ = load_bias_row(kb, "aab_o", wh["bo"])
        kb.sb = kbsb
        bufs = dict(bufs or {})
        bufs.setdefault("dep_h", None)
        sb = lambda n, shp, dt: pstk.enter_context(nc.sbuf_tensor(n, shp, dt))
        psum = lambda n, shp: pstk.enter_context(nc.psum_tensor(n, shp, F32))
        ps_ctx = psum("aa_ctx", [128, 2048])
        ps_s0 = psum("aa_s0", [128, 1024])
        ps_s1 = psum("aa_s1", [128, 1024])
        v_aug = bufs.get("v_aug") or sb("aa_vaug", [128, 16640], BF16)
        alltoks = tuple(kb.tok(s) for s in ("ds", "dg", "dsl", "ve"))
        for e in ("te", "ve", "se", "gp"):
            kb.wait(e, *alltoks)
        rd_all = sb("aa_rd", [16, S], BF16)
        t_q_spill = t_k_spill = t_rd = t_vev = None

        with ExitStack() as pb:
            xat = pb.enter_context(nc.sbuf_tensor("aa_xat", [128, NCH * S], BF16))
            hb = pb.enter_context(nc.sbuf_tensor("aa_hb", [128, 4 * D], F32))
            XH = pb.enter_context(nc.sbuf_tensor("aa_xhln", [128, D], F32))
            wstg = pb.enter_context(nc.sbuf_tensor("aa_wstg", [128, D], F32))
            kbsb, kb.sb = kb.sb, (lambda n, shp, dt:
                                  pb.enter_context(nc.sbuf_tensor(n, shp, dt)))
            wqs, _ = load_w_bf16(kb, "aa_wqs", w["wq"], scale=QK_SCALE, stg=wstg)
            wks, _ = load_w_bf16(kb, "aa_wks", w["wk"])
            wvs, tw_v = load_w_bf16(kb, "aa_wvs", w["wv"])
            kb.sb = kbsb
            w_tok = (kb.tok("dg"), kb.tok("ve"), tw_v)
            t_xat = aa_ln_transpose_pass(kb, cm, h_in, st, xat, XH, ps_s0, hb,
                                         bufs.get("dep_h"))
            if bufs.get("dbgx") is not None:
                kb.wait("sy", t_xat)
                for ci in range(NCH):
                    inst = kb.sy.dma_start(
                        bufs["dbgx"][ci * 128:(ci + 1) * 128, :],
                        ap(xat, ci * S, [[NCH * S, 128], [1, S]]))
                    kb.inc(inst, "ds", 16)
            inst = kb.gp.memset(
                ap(v_aug, 64, [[16640, 128], [65, 256], [1, 1]]), 1.0)
            t_vones = kb.inc(inst, "gp")
            qko = pb.enter_context(nc.sbuf_tensor("aa_qko", [128, 2 * S], BF16))
            war_qko = [(), ()]
            for which, (bias, dst) in enumerate(
                    ((w["bqp"], scr["q_s"]), (w["bkp"], scr["k_s"]))):
                wt_sb = wqs if which == 0 else wks
                war_qk = [(), ()]
                for ec in range(NCH):
                    buf = ec % 2
                    t_ev = None
                    for nf in range(4):
                        par = nf % 2 if which == 1 else 0
                        kb.wait("te", t_xat, *w_tok, *war_qko[buf],
                                *(war_qk[par] if which == 1 else ()))
                        for cc in range(NCH):
                            inst = kb.te.matmul(
                                ps_s0[:, par * 512:par * 512 + 512],
                                wt_sb[:, cc * D + ec * 128:cc * D + ec * 128 + 128],
                                xat[:, cc * S + nf * 512:cc * S + (nf + 1) * 512],
                                start=(cc == 0), stop=(cc == NCH - 1))
                        t_mm = kb.inc(inst, "te")
                        kb.wait("se", t_mm)
                        inst = kb.se.activation(
                            qko[:, buf * S + nf * 512:buf * S + (nf + 1) * 512],
                            ps_s0[:, par * 512:par * 512 + 512],
                            AF.Identity, bias=bias[:, ec:ec + 1])
                        t_ev = kb.inc(inst, "se")
                        war_qk[par] = (t_ev,)
                        if which == 0:
                            kb.wait("te", t_ev)
                        if which == 0:
                            kb.wait("te", scr["tok_dvec"])
                            inst = kb.te.matmul(
                                ps_ctx[0:16, nf * 512:(nf + 1) * 512],
                                scr["dvecblk"][:, ec * 16:(ec + 1) * 16],
                                qko[:, buf * S + nf * 512:buf * S + (nf + 1) * 512],
                                start=(ec == 0), stop=(ec == NCH - 1),
                                skip_group_check=True)
                            kb.inc(inst, "te")
                    kb.wait("sy", t_ev, kb.tok("te"))
                    inst = kb.sy.dma_start(
                        dst[ec * 128:(ec + 1) * 128, :],
                        qko[:, buf * S:(buf + 1) * S])
                    tk = kb.inc(inst, "dsp%d" % buf, 16)
                    war_qko[buf] = (tk,)
                if which == 0:
                    t_q_spill = (kb.tok("dsp0"), kb.tok("dsp1"))
                else:
                    t_k_spill = (kb.tok("dsp0"), kb.tok("dsp1"))
            kb.wait("ve", kb.tok("te"))
            inst = kb.ve.tensor_copy(rd_all[0:16, :], ps_ctx[0:16, :])
            t_rd = kb.inc(inst, "ve")

            pvb = (ps_s1, ps_ctx)
            pvstr = (1024, 2048)
            war_pv = [(), ()]
            pend_v = None

            def v_evac(ptc, ppar, ptmm):
                kb.wait("ve", ptmm)
                inst = kb.ve.tensor_copy(
                    ap(v_aug, ptc * 1040, [[16640, 128], [65, 16], [1, 64]]),
                    ap(pvb[ppar], 0, [[pvstr[ppar], 128], [64, 16], [1, 64]]))
                t = kb.inc(inst, "ve")
                war_pv[ppar] = (t,)
                return t

            for tc in range(NTC):
                par = tc % 2
                pb = pvb[par]
                t_mm = None
                for nh in range(2):
                    kb.wait("te", t_xat, *w_tok, t_vones, t_rd, *war_pv[par])
                    for cc in range(NCH):
                        inst = kb.te.matmul(
                            pb[:, nh * 512:(nh + 1) * 512],
                            xat[:, cc * S + tc * 128:cc * S + tc * 128 + 128],
                            wvs[:, cc * D + nh * 512:cc * D + (nh + 1) * 512],
                            start=(cc == 0), stop=False)
                    inst = kb.te.matmul(
                        pb[:, nh * 512:(nh + 1) * 512],
                        cm["ones_bf"][0:1, 0:128],
                        w["bv_row"][0:1, nh * 512:(nh + 1) * 512],
                        start=False, stop=True)
                    t_mm = kb.inc(inst, "te")
                if pend_v is not None:
                    v_evac(*pend_v)
                pend_v = (tc, par, t_mm)
            t_vev = v_evac(*pend_v)
        kb.phase_sync()

        with ExitStack() as pd:
            sbd = lambda n, shp, dt: pd.enter_context(nc.sbuf_tensor(n, shp, dt))
            stg = sbd("aa_stg", [128, 16 * TW], BF16)
            shn = sbd("aa_shn", [128, 32 * SHW], BF16)
            qh = [sbd("aa_qh%d" % i, [64, S], BF16) for i in range(2)]
            kh = [sbd("aa_kh%d" % i, [64, S], BF16) for i in range(2)]
            pkh = [sbd("aa_pkh%d" % i, [64, TW], BF16) for i in range(2)]
            pqh = [sbd("aa_pqh%d" % i, [64, TW], BF16) for i in range(2)]
            P = [sbd("aa_p%d" % i, [128, 1024], BF16) for i in range(2)]
            kdsat = sbd("aa_kdsat", [128, 32], F32)
            rd0 = sbd("aa_rd0", [1, 2 * S], BF16)
            zrow = sbd("aa_zrow", [1, S], F32)
            zrec = zrow
            c0 = sbd("aa_c0", [128, 2], F32)

            hld = {}

            def head_load(h):
                if h >= H:
                    return
                bb = h % 2
                kb.wait("sy", *hld.get(h - 2, {}).get("war", ()),
                        *t_q_spill, *t_k_spill)
                toks = []
                for dstt, srct in ((qh[bb], scr["q_s"]), (kh[bb], scr["k_s"])):
                    inst = kb.sy.dma_start(dstt[0:64, :],
                                           srct[h * 64:(h + 1) * 64, :])
                    toks.append(kb.inc(inst, "dt%d" % bb, 16))
                kb.wait("sy", scr["tok_tab"])
                inst = kb.sy.dma_start(pkh[bb][0:64, :],
                                       scr["pkrev_s"][h * 64:(h + 1) * 64, :])
                toks.append(kb.inc(inst, "dt%d" % bb, 16))
                inst = kb.sy.dma_start(pqh[bb][0:64, :],
                                       scr["pq_s"][h * 64:(h + 1) * 64, :])
                toks.append(kb.inc(inst, "dt%d" % bb, 16))
                kb.wait("sy", t_rd)
                inst = kb.sy.dma_start(rd0[0:1, bb * S:(bb + 1) * S],
                                       rd_all[h:h + 1, :])
                toks.append(kb.inc(inst, "dt%d" % bb, 16))
                hld[h] = {"tok": (toks[-1],), "war": ()}

            head_load(0)
            war_stg = ()
            war_shn = ()
            war_tab = [[(), ()], [(), ()]]
            war_P = [(), ()]
            war_ctx = ()
            war_ps = ()
            war_kds = ()
            war_ctxn = ()
            t_sh = None

            for h in range(H):
                b = h % 2
                head_load(h + 1)
                tq = kb.tok("dt%d" % b)

                pieces = ((0, 512, ps_s0, 0), (512, 512, ps_s0, 512),
                          (1024, 256, ps_s1, 0))
                war_pc = [tuple(war_ps), tuple(war_ps), tuple(war_ps)]
                rtoks = [[], []]
                for kind in range(2):
                    srct = qh[b] if kind == 0 else kh[b]
                    tab = pkh[b] if kind == 0 else pqh[b]
                    ev = "ve" if kind == 0 else "se"
                    eng = kb.ve if kind == 0 else kb.se
                    t_ev = None
                    ctoks = []
                    for i in range(NTC):
                        tc0 = None
                        for pidx, (p0, pn, pst, off) in enumerate(pieces):
                            kb.wait("te", tq, *war_stg, *war_pc[pidx])
                            inst = kb.te.matmul(
                                pst[:, off:off + pn],
                                srct[:, i * 128:(i + 1) * 128],
                                tab[:, p0:p0 + pn],
                                start=True, stop=True, skip_group_check=True)
                            t_mm = kb.inc(inst, "te")
                            kb.wait(ev, t_mm, *war_stg,
                                    *(war_kds if kind == 1 else ()))
                            dst_sl = stg[:, i * TW + p0:i * TW + p0 + pn]
                            if kind == 0:
                                if pidx == 0:
                                    inst = eng.tensor_copy(c0[:, 0:1],
                                                           ps_s0[:, 0:1])
                                    tc0 = kb.inc(inst, ev)
                                    inst = eng.tensor_scalar(
                                        dst_sl, pst[:, off:off + pn],
                                        ps_s0[:, 0:1], None, ALU.subtract)
                                else:
                                    kb.wait(ev, tc0)
                                    inst = eng.tensor_scalar(
                                        dst_sl, pst[:, off:off + pn],
                                        c0[:, 0:1], None, ALU.subtract)
                            else:
                                inst = eng.activation(dst_sl,
                                                      pst[:, off:off + pn],
                                                      AF.Identity)
                                if pidx == 0:
                                    inst = eng.activation(
                                        kdsat[:, 2 * i:2 * i + 1],
                                        ps_s0[:, 127:128], AF.Identity,
                                        bias=cm["negshift"][:, 0:1])
                                elif pidx == 2:
                                    inst = eng.activation(
                                        kdsat[:, 2 * i + 1:2 * i + 2],
                                        ps_s1[:, 129:130], AF.Identity,
                                        bias=cm["negshift"][:, 0:1])
                            t_ev = kb.inc(inst, ev)
                            war_pc[pidx] = (t_ev,)
                        if i % 4 == 3:
                            ctoks.append(t_ev)
                    tb = scr["tab_s"][kind][h % 2]
                    off0 = 127 if kind == 0 else 128
                    for c4 in range(4):
                        kb.wait("sy", ctoks[c4],
                                *(war_tab[kind][h % 2] if c4 == 0 else ()))
                        inst = kb.sy.dma_start(
                            ap(tb, c4 * 4 * TW, [[16 * TW, 128], [1, 4 * TW]]),
                            ap(stg, c4 * 4 * TW,
                               [[16 * TW, 128], [1, 4 * TW]]))
                        t_st = kb.inc(inst, "dtb", 16)
                        kb.wait("sy", t_st, *(war_shn if c4 == 0 else ()))
                        inst = kb.sy.dma_start(
                            ap(shn, kind * 16 * SHW + c4 * 4 * SHW,
                               [[32 * SHW, 128], [1, 4 * SHW]]),
                            ap(tb, off0 + c4 * 4 * TW,
                               [[16 * TW - 1, 128], [TW, 4], [1, SHW]]))
                        t_sh = kb.inc(inst, "dsh", 16)
                        rtoks[kind].append(t_sh)
                    war_stg = (t_st,)
                    war_tab[kind][h % 2] = (t_sh,)
                    if kind == 1:
                        t_kdse = t_ev

                no_c2p = bool(os.environ.get("AA_NO_C2P"))
                no_p2c = bool(os.environ.get("AA_NO_P2C"))
                war_half = [war_pc[0] + war_pc[1], tuple(war_pc[2])]
                prev = None
                it = 0
                for j in range(NTC):
                    lo_near = 128 * max(0, j - 4)
                    hi_near = 128 * min(NTC, j + 5)
                    for half, pss in ((0, ps_s0), (1, ps_s1)):
                        ip = it % 2
                        s0 = half * 1024
                        fr_end = min(max(lo_near, s0), s0 + 1024)
                        nr_end = min(max(hi_near, s0), s0 + 1024)
                        kb.wait("te", tq, t_rd, *war_half[half],
                                rtoks[1][min(3, j // 4)])
                        for nf in range(2):
                            inst = kb.te.matmul(
                                pss[:, nf * 512:(nf + 1) * 512],
                                kh[b][:, j * 128:(j + 1) * 128],
                                qh[b][:, s0 + nf * 512:s0 + (nf + 1) * 512],
                                start=True, stop=True, skip_group_check=True)
                        if fr_end < nr_end:
                            z0 = (fr_end - s0) + (s0 - 128 * j) + 512
                            pos = fr_end - s0
                            wdt = nr_end - fr_end
                            while (not no_p2c) and wdt > 0:
                                wn = min(512 - (pos % 512), wdt)
                                inst = kb.te.matmul(
                                    pss[:, pos:pos + wn],
                                    cm["id_bf"][:, :],
                                    ap(shn, (16 + j) * SHW + z0,
                                       [[32 * SHW, 128], [1, wn]]),
                                    start=False, stop=True,
                                    skip_group_check=True)
                                z0 += wn; pos += wn; wdt -= wn
                            for i in (() if no_c2p else
                                      range(max(0, j - 4), min(NTC, j + 5))):
                                si = 128 * i
                                if si < s0 or si >= s0 + 1024:
                                    continue
                                t = i - j
                                inst = kb.te.matmul(
                                    pss[:, si - s0:si - s0 + 128],
                                    ap(shn, i * SHW + 128 * (4 - t),
                                       [[32 * SHW, 128], [1, 128]]),
                                    cm["id_bf"][:, :],
                                    start=False, stop=True,
                                    skip_group_check=True)
                        if s0 < fr_end and not no_c2p:
                            pos = 0
                            wdt = fr_end - s0
                            while wdt > 0:
                                wn = min(512 - (pos % 512), wdt)
                                inst = kb.te.matmul(
                                    pss[:, pos:pos + wn],
                                    cm["ones_bf"][0:1, 0:128],
                                    rd0[0:1, b * S + s0 + pos:
                                        b * S + s0 + pos + wn],
                                    start=False, stop=True,
                                    skip_group_check=True)
                                pos += wn; wdt -= wn
                        t_bias = kb.inc(inst, "te")
                        kb.wait("se", t_bias, t_kdse, *war_P[ip])
                        if s0 < fr_end:
                            inst = kb.se.activation(
                                P[ip][:, 0:fr_end - s0],
                                pss[:, 0:fr_end - s0], AF.Exp,
                                bias=(cm["negshift"][:, 0:1] if no_p2c else
                                      kdsat[:, 2 * j:2 * j + 1]))
                        if fr_end < nr_end:
                            inst = kb.se.activation(
                                P[ip][:, fr_end - s0:nr_end - s0],
                                pss[:, fr_end - s0:nr_end - s0],
                                AF.Exp, bias=cm["negshift"][:, 0:1])
                        if nr_end < s0 + 1024:
                            inst = kb.se.activation(
                                P[ip][:, nr_end - s0:1024],
                                pss[:, nr_end - s0:1024],
                                AF.Exp,
                                bias=(cm["negshift"][:, 0:1] if no_p2c else
                                      kdsat[:, 2 * j + 1:2 * j + 2]))
                        t_exp = kb.inc(inst, "se")
                        war_half[half] = (t_exp,)
                        if prev is not None:
                            pj, pip, ps0, pt_exp = prev
                            kb.wait("te", pt_exp,
                                    *((t_vev,) + war_ctx if pj == 0 else ()))
                            for nf in range(2):
                                inst = kb.te.matmul(
                                    ps_ctx[0:65, ps0 + nf * 512:
                                           ps0 + (nf + 1) * 512],
                                    v_aug[:, pj * 1040 + h * 65:
                                          pj * 1040 + (h + 1) * 65],
                                    P[pip][:, nf * 512:(nf + 1) * 512],
                                    start=(pj == 0), stop=(pj == NTC - 1),
                                    skip_group_check=True)
                            t_pv = kb.inc(inst, "te")
                            war_P[pip] = (t_pv,)
                        prev = (j, ip, s0, t_exp)
                        it += 1
                pj, pip, ps0, pt_exp = prev
                kb.wait("te", pt_exp)
                for nf in range(2):
                    inst = kb.te.matmul(
                        ps_ctx[0:65, ps0 + nf * 512:ps0 + (nf + 1) * 512],
                        v_aug[:, pj * 1040 + h * 65:pj * 1040 + (h + 1) * 65],
                        P[pip][:, nf * 512:(nf + 1) * 512],
                        start=(pj == 0), stop=(pj == NTC - 1),
                        skip_group_check=True)
                t_pv = kb.inc(inst, "te")
                war_P[pip] = (t_pv,)

                kb.wait("se", t_pv)
                inst = kb.se.activation(zrow[0:1, :], ps_ctx[64:65, :], AF.Ln)
                t_z = kb.inc(inst, "se")
                kb.wait("se", t_z)
                inst = kb.se.activation(zrec[0:1, :], zrow[0:1, :], AF.Exp,
                                        scale=-1.0)
                t_z = kb.inc(inst, "se")
                war_kds = (t_z,)
                kb.wait("te", t_z)
                for nf in range(4):
                    pst = ps_s0 if nf < 2 else ps_s1
                    inst = kb.te.matmul(
                        pst[0:64, (nf % 2) * 512:(nf % 2 + 1) * 512],
                        cm["ones_f32"][0:1, 0:64],
                        zrec[0:1, nf * 512:(nf + 1) * 512],
                        start=True, stop=True, skip_group_check=True)
                t_bc = kb.inc(inst, "te")
                ctxn_b = stg[0:64, 16 * TW - 2048:16 * TW]
                bcast_b = stg[0:64, 16 * TW - 4096:16 * TW - 2048]
                kb.wait("ve", t_bc, *war_stg, *war_ctxn)
                inst = kb.ve.tensor_copy(bcast_b[:, 0:1024], ps_s0[0:64, :])
                inst = kb.ve.tensor_copy(bcast_b[:, 1024:2048], ps_s1[0:64, :])
                tcp = kb.inc(inst, "ve")
                kb.wait("ve", tcp)
                inst = kb.ve.tensor_tensor(ctxn_b[:, 0:1024],
                                           ps_ctx[0:64, 0:1024],
                                           bcast_b[:, 0:1024], ALU.mult)
                inst = kb.ve.tensor_tensor(ctxn_b[:, 1024:2048],
                                           ps_ctx[0:64, 1024:2048],
                                           bcast_b[:, 1024:2048], ALU.mult)
                t_cn = kb.inc(inst, "ve")
                war_ctx = (t_cn,)
                war_ps = war_half[0] + war_half[1] + (t_cn,)
                kb.wait("sy", t_cn)
                inst = kb.sy.dma_start(scr["ctxn_s"][h * 64:(h + 1) * 64, :],
                                       ctxn_b[:, :])
                tk = kb.inc(inst, "dcx", 16)
                war_ctxn = (tk,)
                war_stg = war_stg + (tk,)
                hld[h]["war"] = (t_cn,)
                war_shn = (kb.tok("te"),)
        kb.phase_sync()

        with ExitStack() as p6:
            ctf = p6.enter_context(nc.sbuf_tensor("aa_ctf", [128, NCH * S], BF16))
            hb2 = p6.enter_context(nc.sbuf_tensor("aa_hb2", [128, 4 * D], F32))
            kbsb, kb.sb = kb.sb, (lambda n, shp, dt:
                                  p6.enter_context(nc.sbuf_tensor(n, shp, dt)))
            wos, tw_o = load_w_bf16(kb, "aa_wos", w["wo"])
            kb.sb = kbsb
            t_cld = None
            for cc in range(NCH):
                kb.wait("sy", kb.tok("dcx"))
                inst = kb.sy.dma_start(
                    ap(ctf, cc * S, [[NCH * S, 128], [1, S]]),
                    scr["ctxn_s"][cc * 128:(cc + 1) * 128, :])
                t_cld = kb.inc(inst, "dct", 16)

            ld = {}

            def issue_load6(bk):
                if bk >= NTC // 2:
                    return
                kb.wait("sy", kb.tok("dho"))
                for i in range(2):
                    tc = bk * 2 + i
                    inst = kb.sy.dma_start(
                        hb2[:, ((bk % 2) * 2 + i) * D:((bk % 2) * 2 + i + 1) * D],
                        h_in[tc * 128:(tc + 1) * 128, :])
                    kb.inc(inst, "dhi%d" % (bk % 2), 16)

            issue_load6(0)
            for bk in range(NTC // 2):
                ldt = (kb.tok("dhi%d" % (bk % 2)),)
                issue_load6(bk + 1)
                for i in range(2):
                    tc = bk * 2 + i
                    for nh in range(2):
                        kb.wait("te", t_cld, tw_o, kb.tok("ve"))
                        for cc in range(NCH):
                            inst = kb.te.matmul(
                                ps_s0[:, nh * 512:(nh + 1) * 512],
                                ctf[:, cc * S + tc * 128:cc * S + tc * 128 + 128],
                                wos[:, cc * D + nh * 512:cc * D + (nh + 1) * 512],
                                start=(cc == 0), stop=False)
                        inst = kb.te.matmul(
                            ps_s0[:, nh * 512:(nh + 1) * 512],
                            cm["ones_bf"][0:1, 0:128],
                            w["bo_row"][0:1, nh * 512:(nh + 1) * 512],
                            start=False, stop=True)
                        t_mm = kb.inc(inst, "te")
                        kb.wait("ve", t_mm, *ldt)
                        hap = hb2[:, (((bk % 2) * 2 + i) * D + nh * 512):
                                  (((bk % 2) * 2 + i) * D + (nh + 1) * 512)]
                        inst = kb.ve.scalar_tensor_tensor(
                            hap, ps_s0[:, nh * 512:(nh + 1) * 512], 1.0, hap,
                            ALU.mult, ALU.add)
                        t_res = kb.inc(inst, "ve")
                        kb.wait("te", t_res)
                    kb.wait("sy", t_res)
                    inst = kb.sy.dma_start(h_out[tc * 128:(tc + 1) * 128, :],
                                           hb2[:, ((bk % 2) * 2 + i) * D:
                                               ((bk % 2) * 2 + i + 1) * D])
                    kb.inc(inst, "dho", 16)
        pass
    kb.phase_sync()
    return (kb.tok("dho"),)


def build_attn_a_setup(kb, cm, io, scr, st):
    nc = kb.nc
    lngp, _ = load_bias_part(kb, "lngp", io["lng"])
    lnbp, t_lnb = load_bias_part(kb, "lnbp", io["lnb"])
    lnbp_bf = kb.sb("lnbp_bf", [128, NCH], BF16)
    kb.wait("ve", t_lnb)
    inst = kb.ve.tensor_copy(lnbp_bf[:, :], lnbp[:, :])
    kb.inc(inst, "ve")
    bpkp, t_bpk = load_bias_part(kb, "bpkp", io["bpk"], scale=PK_SCALE)
    bpqp, t_bpq = load_bias_part(kb, "bpqp", io["bpq"], scale=PQ_SCALE)
    scr["dvecblk"] = kb.sb("dvecblk", [128, 128], BF16)
    inst = kb.gp.memset(scr["dvecblk"][:, :], 0.0)
    t_dv = kb.inc(inst, "gp")

    with ExitStack() as ss:
        sbs = lambda n, shp, dt: ss.enter_context(nc.sbuf_tensor(n, shp, dt))
        lnt = sbs("su_lnt", [128, NCH * TW], BF16)
        hbuf = sbs("su_hb", [128, D], F32)
        XH = sbs("su_xh", [128, D], F32)
        stage = sbs("su_stage", [128, D], F32)
        wsb = sbs("su_wsb", [128, NCH * D], BF16)
        wtmp = sbs("su_wtmp", [128, D], BF16)
        beff = sbs("su_beff", [128, NCH], F32)
        tb = sbs("su_tb", [128, TW], BF16)
        ps_a = ss.enter_context(nc.psum_tensor("su_psa", [128, 1024], F32))
        ps_b = ss.enter_context(nc.psum_tensor("su_psb", [128, 1024], F32))

        pieces = ((0, 512, ps_a, 0), (512, 512, ps_a, 512), (1024, 256, ps_b, 0))
        for kind, (rel, wp, bscaled, t_bs, scale, dst) in enumerate((
                (io["relrev"], io["wpk"], bpkp, t_bpk, PK_SCALE, scr["pkrev_s"]),
                (io["relfwd"], io["wpq"], bpqp, t_bpq, PQ_SCALE, scr["pq_s"]))):
            t_ev = None
            for rc in range(10):
                kb.wait("sy", kb.tok("te"), kb.tok("ve"), kb.tok("se"))
                inst = kb.sy.dma_start(hbuf[:, :], rel[rc * 128:(rc + 1) * 128, :])
                tk = kb.inc(inst, "dt0", 16)
                t_st = ln_stats_pair(kb, [hbuf[:, :]], st, dep=(tk,))
                t_x = ln_apply_one(kb, hbuf[:, :], XH[:, :], st, 0, 1, dep=t_st)
                for cq in range(2):
                    kb.wait("te", t_x, kb.tok("ve"))
                    for k4 in range(4):
                        c = cq * 4 + k4
                        inst = kb.te.matmul(
                            ps_a[:, k4 * 128:(k4 + 1) * 128],
                            XH[:, c * 128:(c + 1) * 128], cm["id_f32"][:, :],
                            start=(k4 == 0), stop=(k4 == 3), is_transpose=True)
                    t_te = kb.inc(inst, "te")
                    kb.wait("ve", t_te)
                    inst = kb.ve.tensor_copy(
                        ap(lnt, cq * 4 * TW + rc * 128,
                           [[NCH * TW, 128], [TW, 4], [1, 128]]),
                        ap(ps_a, 0, [[1024, 128], [128, 4], [1, 128]]))
                    t_ev = kb.inc(inst, "ve")
                    kb.wait("te", t_ev)
            if io.get("dbg") and kind == 0:
                kb.wait("sy", t_ev)
                for ci in range(NCH):
                    inst = kb.sy.dma_start(
                        io["dbg"]["lnt"][ci * 128:(ci + 1) * 128, :],
                        ap(lnt, ci * TW, [[NCH * TW, 128], [1, TW]]))
                    kb.inc(inst, "dct", 16)
            t_w = None
            for cc in range(NCH):
                kb.wait("sy", kb.tok("ve"), kb.tok("te"))
                inst = kb.sy.dma_start(stage[:, :], wp[cc * 128:(cc + 1) * 128, :])
                tk = kb.inc(inst, "dsl", 16)
                kb.wait("ve", tk)
                inst = kb.ve.tensor_scalar(wsb[:, cc * D:(cc + 1) * D], stage[:, :],
                                           lngp[:, cc:cc + 1], scale,
                                           ALU.mult, ALU.mult)
                kb.inc(inst, "ve")
                if cc > 0:
                    kb.wait("ve", t_bm)
                inst = kb.ve.tensor_scalar_mul(wtmp[:, :], stage[:, :], scale)
                t_w = kb.inc(inst, "ve")
                kb.wait("te", t_w)
                for ec in range(NCH):
                    inst = kb.te.matmul(
                        ps_b[:, 512 + ec:512 + ec + 1],
                        wtmp[:, ec * 128:(ec + 1) * 128],
                        lnbp_bf[:, cc:cc + 1],
                        start=(cc == 0 and ec == 0), stop=(cc == NCH - 1),
                        skip_group_check=True)
                t_bm = kb.inc(inst, "te")
            kb.wait("ve", t_bm, t_bs)
            inst = kb.ve.tensor_tensor(beff[:, :], ps_b[0:128, 512:512 + NCH],
                                       bscaled[:, :], ALU.add)
            t_be = kb.inc(inst, "ve")
            if io.get("dbg") and kind == 0:
                kb.wait("sy", t_be, t_w)
                inst = kb.sy.dma_start(io["dbg"]["beff"][:, :], beff[:, :])
                kb.inc(inst, "dct", 16)
                inst = kb.sy.dma_start(io["dbg"]["wsb"][:, :], wsb[:, :])
                kb.inc(inst, "dct", 16)
            t_tb = None
            for ec in range(NCH):
                kb.wait("te", t_ev, t_w, t_be, t_tb)
                for (p0, pn, pst, off) in pieces:
                    for cc in range(NCH):
                        inst = kb.te.matmul(
                            pst[:, off:off + pn],
                            wsb[:, cc * D + ec * 128:cc * D + ec * 128 + 128],
                            ap(lnt, cc * TW + p0, [[NCH * TW, 128], [1, pn]]),
                            start=(cc == 0), stop=(cc == NCH - 1),
                            skip_group_check=True)
                t_mm = kb.inc(inst, "te")
                kb.wait("se", t_mm, t_be, kb.tok("ds"))
                for (p0, pn, pst, off) in pieces:
                    inst = kb.se.activation(tb[:, p0:p0 + pn], pst[:, off:off + pn],
                                            AF.Identity, bias=beff[:, ec:ec + 1])
                t_tb = kb.inc(inst, "se")
                if kind == 0:
                    kb.wait("ve", t_tb, t_dv)
                    inst = kb.ve.tensor_tensor(stage[:, 0:1], tb[:, TW - 1:TW],
                                               tb[:, 0:1], ALU.subtract)
                    t_sub = kb.inc(inst, "ve")
                    kb.wait("ve", t_sub)
                    inst = kb.ve.tensor_copy(
                        scr["dvecblk"][0:64, ec * 16 + 2 * ec:ec * 16 + 2 * ec + 1],
                        stage[0:64, 0:1])
                    inst = kb.ve.tensor_copy(
                        scr["dvecblk"][64:128, ec * 16 + 2 * ec + 1:ec * 16 + 2 * ec + 2],
                        stage[64:128, 0:1])
                    t_dvw = kb.inc(inst, "ve")
                kb.wait("sy", t_tb, (t_dvw if kind == 0 else None))
                inst = kb.sy.dma_start(dst[ec * 128:(ec + 1) * 128, :], tb[:, :])
                tk = kb.inc(inst, "ds", 16)
                kb.wait("se", tk)
    kb.phase_sync()
    scr["tok_tab"] = kb.tok("ds")
    scr["tok_dvec"] = kb.tok("ve")
    return scr



def build_program():
    nc = bass.Bass("TRN2", target_bir_lowering=False,
                   detect_race_conditions=False)
    hb = {}
    hb["x"] = nc.dram_tensor("x", [SEQS_PER_CORE * S, D], F32, kind="ExternalInput")
    for nm in ("aa_wq", "aa_wk", "aa_wv", "aa_wo", "aa_wpk", "aa_wpq"):
        hb[nm] = nc.dram_tensor(nm, [D, D], F32, kind="ExternalInput")
    for nm in ("aa_bq", "aa_bk", "aa_bv", "aa_bo", "aa_bpk", "aa_bpq",
               "aa_lng", "aa_lnb", "f_mur", "f_muk",
               "b_bq1", "b_bk1", "b_bk2", "b_bproj"):
        hb[nm] = nc.dram_tensor(nm, [D], F32, kind="ExternalInput")
    for nm in ("aa_relfwd", "aa_relrev"):
        hb[nm] = nc.dram_tensor(nm, [TW, D], F32, kind="ExternalInput")
    for nm in ("f_wrt", "f_wkt", "f_wvt", "b_wq1t", "b_wk1t", "b_wk2t",
               "b_wprojt"):
        hb[nm] = nc.dram_tensor(nm, [D, D], F32, kind="ExternalInput")
    y = nc.dram_tensor("y", [SEQS_PER_CORE * S, D], F32, kind="ExternalOutput")
    h_a = nc.dram_tensor("h_a", [S, D], F32, kind="Internal")
    h_b = nc.dram_tensor("h_b", [S, D], F32, kind="Internal")
    scr = {
        "q_s": nc.dram_tensor("q_s", [D, S], BF16, kind="Internal"),
        "k_s": nc.dram_tensor("k_s", [D, S], BF16, kind="Internal"),
        "pkrev_s": nc.dram_tensor("pkrev_s", [D, TW], BF16, kind="Internal"),
        "pq_s": nc.dram_tensor("pq_s", [D, TW], BF16, kind="Internal"),
        "ctxn_s": nc.dram_tensor("ctxn_s", [D, S], BF16, kind="Internal"),
        "tab_s": [[nc.dram_tensor("tab_s%d_%d" % (k, p), [128, 16 * TW],
                                  BF16, kind="Internal") for p in range(2)]
                  for k in range(2)],
    }
    with ExitStack() as stack:
        kb = KB(nc, stack)
        cm = build_consts(kb)
        st = {"sq": kb.sb("ln_sq", [128, D], F32),
              "t": kb.sb("ln_t", [128, 24], F32)}
        io = {"relrev": hb["aa_relrev"], "relfwd": hb["aa_relfwd"],
              "wpk": hb["aa_wpk"], "wpq": hb["aa_wpq"], "bpk": hb["aa_bpk"],
              "bpq": hb["aa_bpq"], "lng": hb["aa_lng"], "lnb": hb["aa_lnb"]}
        build_attn_a_setup(kb, cm, io, scr, st)
        wa = {"wq": hb["aa_wq"], "wk": hb["aa_wk"], "wv": hb["aa_wv"],
              "wo": hb["aa_wo"], "bq": hb["aa_bq"], "bk": hb["aa_bk"],
              "bv": hb["aa_bv"], "bo": hb["aa_bo"]}
        wf = {"wrt": hb["f_wrt"], "wkt": hb["f_wkt"], "wvt": hb["f_wvt"],
              "mur": hb["f_mur"], "muk": hb["f_muk"]}
        wb = {"wq1t": hb["b_wq1t"], "wk1t": hb["b_wk1t"], "wk2t": hb["b_wk2t"],
              "wprojt": hb["b_wprojt"], "bq1": hb["b_bq1"], "bk1": hb["b_bk1"],
              "bk2": hb["b_bk2"], "bproj": hb["b_bproj"]}
        dep = None
        for sq in range(SEQS_PER_CORE):
            x_sl = hb["x"][sq * S:(sq + 1) * S, :]
            y_sl = y[sq * S:(sq + 1) * S, :]
            dep = build_attn_a(kb, cm, wa, scr, x_sl, h_a, st,
                               {"dep_h": dep})
            dep = build_ffn(kb, cm, wf, h_a, h_b, st, {"dep_h": dep})
            dep = build_attn_b(kb, cm, wb, h_b, h_a, st, {"dep_h": dep})
            dep = build_ffn(kb, cm, wf, h_a, y_sl, st, {"dep_h": dep})
        kb.wait("sy", kb.tok("dho"))
    return nc


_NC_CACHE = {}


def kernel(**inputs):
    g = lambda n: np.ascontiguousarray(inputs[n])
    t = lambda n: np.ascontiguousarray(np.transpose(inputs[n]))
    f = _bucket_f()
    rel = inputs["L0_attn_a_rel_emb"]
    relfwd = np.ascontiguousarray(rel[f])
    relrev = np.ascontiguousarray(rel[f[::-1]])
    common = {
        "aa_wq": g("L0_attn_a_Wq"), "aa_wk": g("L0_attn_a_Wk"),
        "aa_wv": g("L0_attn_a_Wv"), "aa_wo": g("L0_attn_a_Wo"),
        "aa_wpk": g("L0_attn_a_Wpk"), "aa_wpq": g("L0_attn_a_Wpq"),
        "aa_bq": g("L0_attn_a_bq"), "aa_bk": g("L0_attn_a_bk"),
        "aa_bv": g("L0_attn_a_bv"), "aa_bo": g("L0_attn_a_bo"),
        "aa_bpk": g("L0_attn_a_bpk"), "aa_bpq": g("L0_attn_a_bpq"),
        "aa_lng": g("L0_attn_a_ln_g"), "aa_lnb": g("L0_attn_a_ln_b"),
        "aa_relfwd": relfwd, "aa_relrev": relrev,
        "f_wrt": t("L0_ffn_W_r"), "f_wkt": t("L0_ffn_W_k"),
        "f_wvt": t("L0_ffn_W_v"),
        "f_mur": g("L0_ffn_mu_r"), "f_muk": g("L0_ffn_mu_k"),
        "b_wq1t": t("L0_attn_b_Wq1"), "b_wk1t": t("L0_attn_b_Wk1"),
        "b_wk2t": t("L0_attn_b_Wk2"), "b_wprojt": t("L0_attn_b_Wproj"),
        "b_bq1": g("L0_attn_b_bq1"), "b_bk1": g("L0_attn_b_bk1"),
        "b_bk2": g("L0_attn_b_bk2"), "b_bproj": g("L0_attn_b_bproj"),
    }
    xs = inputs["L0_attn_a_hidden_states"]
    B = xs.shape[0]
    per = B // NB_CORES
    if "nc" not in _NC_CACHE:
        _NC_CACHE["nc"] = build_program()
    nc = _NC_CACHE["nc"]
    in_maps = []
    for c in range(NB_CORES):
        m = dict(common)
        m["x"] = np.ascontiguousarray(
            xs[c * per:(c + 1) * per].reshape(per * S, D))
        in_maps.append(m)
    res = run_bass_kernel_spmd(nc, in_maps, core_ids=list(range(NB_CORES)))
    out = np.empty((B, S, D), dtype=np.float32)
    for c in range(NB_CORES):
        out[c * per:(c + 1) * per] = res.results[c]["y"].reshape(per, S, D)
    return out
